# version 27
# baseline (speedup 1.0000x reference)
import contextlib
import numpy as np

import concourse.bass as bass
import concourse.mybir as mybir
from concourse.bass_utils import run_bass_kernel_spmd

F32 = mybir.dt.float32
BF16 = mybir.dt.bfloat16
AF = mybir.ActivationFunctionType
ALU = mybir.AluOpType

D = 1024
NCH = 8
NPAIR = 8
C_SHIFT_A = 12.0
C_SHIFT_B = 26.0

AP = bass.AP
DBG = 0


class Sem:
    def __init__(self, h):
        self.h = h
        self.n = 0

    def nxt(self, k=1):
        self.n += k
        return self.n


class Eng:

    def __init__(self, name):
        self.name = name
        self.ops = []

    def do(self, fn, inc=None, k=1):
        if inc is None:
            self.ops.append(fn)
            return None
        tok = inc.nxt(k)
        self.ops.append(lambda e, fn=fn, sem=inc.h, k=k: fn(e).then_inc(sem, k))
        return tok

    def mark(self, sem):
        tok = sem.nxt(1)
        prev = self.ops[-1]

        def wrapped(e, p=prev, s=sem.h):
            r = p(e)
            assert r is not None, "mark() must follow an instruction-emitting op"
            return r.then_inc(s, 1)

        self.ops[-1] = wrapped
        return tok

    def wait(self, sem, tok):
        if tok is None:
            return
        self.ops.append(lambda e, s=sem.h, t=tok: e.wait_ge(s, t))

    def raw(self, fn):
        self.ops.append(fn)


class Psum:
    def __init__(self, t):
        self.t = t
        self.wtok = None
        self.rtoks = []


def build_nc(T, n_sub=4, span_ovr=None):
    NT = T // 128
    SPAN = span_ovr or min(1024, T)
    NSP = T // SPAN
    NKT = T // 128
    SP512 = min(512, T)
    NSP512 = T // SP512
    PTW = max(4 * SPAN, 4096)

    nc = bass.Bass("TRN2", target_bir_lowering=False, detect_race_conditions=False)

    dram = {}

    def dr(name, shape, kind="ExternalInput"):
        dram[name] = nc.dram_tensor(name, shape, F32, kind=kind)
        return dram[name]

    d_x = dr("x_in", [T, D])
    d_y = dr("y_out", [T, D], "ExternalOutput")
    d_wqkvT = dr("wqkvT", [D, 3 * D])
    d_woT = dr("woT", [D, D])
    d_qkvb = dr("qkvb_t", [128, 24])
    d_vbrow = dr("vb_row", [1, D])
    d_obrow = dr("ob_row", [1, D])
    d_bwqT = dr("bwqT", [D, D])
    d_bwkT = dr("bwkT", [D, D])
    d_bwvT = dr("bwvT", [D, D])
    d_bwoT = dr("bwoT", [D, D])
    d_fwqT = dr("fwqT", [D, D])
    d_routT = dr("routT", [D, 255])
    d_lv = dr("lv_w", [256, D])
    d_fwoutT = dr("fwoutT", [D, D])
    d_fbrow = dr("fb_row", [1, D])
    d_bc = nc.dram_tensor("bc_scr", [2, SPAN], F32, kind="ExternalOutput")
    d_ident = dr("ident_c", [128, 128])
    d_tri = dr("tri_c", [128, 128])
    d_cos = dr("cos_c", [128, 32 * NT])
    d_sin = dr("sin_c", [128, 32 * NT])
    d_ones = dr("ones_c", [128, 1])
    d_onesf = dr("onesf_c", [128, 128])
    d_onesb = dr("onesb_c", [128, 64])
    d_csh = dr("csh_c", [128, 2])

    es = contextlib.ExitStack()
    sb = lambda name, shape, dt: es.enter_context(nc.sbuf_tensor(name, shape, dt))
    sem = lambda name: Sem(es.enter_context(nc.semaphore(name)))

    X = sb("X", [128, NT * 1024], F32)
    SHRD = sb("SHRD", [128, 2048], F32)
    BIAS = sb("BIAS", [128, 1024], BF16)
    RB = sb("RB", [128, SPAN], F32)
    ROWS = sb("ROWS", [128, 1024], F32)
    SML = sb("SML", [128, 128], F32)
    STAT = sb("STAT", [128, 12 * 16], F32)
    QKVB = sb("QKVB", [128, 24], F32)
    XLN = sb("XLN", [128, NCH * T], BF16)
    QS = sb("QS", [128, NPAIR * T], BF16)
    KS = sb("KS", [128, 4 * T], BF16)
    VB = sb("VB", [128, NKT * 520], BF16)
    PT = sb("PT", [128, PTW], BF16)
    WA = sb("WA", [128, 8192], BF16)
    IDNB = sb("IDNB", [128, 128], BF16)
    TRI = sb("TRI", [128, 128], BF16)
    ONES = sb("ONES", [128, 1], BF16)
    ONESF = sb("ONESF", [128, 128], F32)
    ONESB = sb("ONESB", [128, 64], BF16)
    CSH = sb("CSH", [128, 2], F32)
    STG = sb("STG", [128, SPAN], BF16)

    class TV:

        def __init__(self, t, off):
            self._t = t
            self._o = off
            self.handle = t
            self.off = off

        def __getitem__(self, idx):
            r, c = idx
            return self._t[r, c.start + self._o:c.stop + self._o]

    PAB = es.enter_context(nc.psum_tensor("PAB", [128, 2048], F32))
    P = [Psum(TV(PAB, 0)), Psum(TV(PAB, 1024)),
         Psum(es.enter_context(nc.psum_tensor("P2", [128, 1024], F32))),
         Psum(es.enter_context(nc.psum_tensor("P3", [128, 1024], F32)))]

    tT, aA, vV, gG = sem("tT"), sem("aA"), sem("vV"), sem("gG")
    dX, dC, dO, dF = sem("dX"), sem("dC"), sem("dO"), sem("dF")
    dB = sem("dB")
    WSL = [sem(f"ws{i}") for i in range(8)]

    TE, VE, SE, GE, SY = Eng("t"), Eng("v"), Eng("s"), Eng("g"), Eng("y")
    MARK = {id(TE): tT, id(VE): vV, id(SE): aA, id(GE): gG}

    def p_write_ready(eng, p):
        for s, t in p.rtoks:
            eng.wait(s, t)
        p.rtoks = []

    def p_read_ready(eng, p):
        if p.wtok is not None:
            eng.wait(*p.wtok)

    def p_mark_w(p, tok):
        p.wtok = (tT, tok)

    def p_mark_r(p, eng, tok):
        p.rtoks.append((MARK[id(eng)], tok))

    def dma_w(slot_sem, dst_ap, src_ap, after_tok=None):
        if after_tok is not None:
            GE.wait(tT, after_tok)
        tok = slot_sem.nxt(16)
        GE.ops.append(lambda e, d=dst_ap, s=src_ap, sm=slot_sem.h:
                      e.dma_start(out=d, in_=s).then_inc(sm, 16))
        return tok


    def dma_c(dst, src):
        GE.ops.append(lambda e, d=dst, s=src: e.dma_start(out=d, in_=s).then_inc(dC.h, 16))
        dC.n += 16

    dma_c(IDNB[:, :], d_ident.ap()[:, :])
    dma_c(TRI[:, :], d_tri.ap()[:, :])
    dma_c(ONES[:, :], d_ones.ap()[:, :])
    dma_c(QKVB[:, :], d_qkvb.ap()[:, :])
    dma_c(CSH[:, :], d_csh.ap()[:, :])
    dma_c(ONESF[:, :], d_onesf.ap()[:, :])
    dma_c(ONESB[:, :], d_onesb.ap()[:, :])
    tok_consts = dC.n

    for _kt in range(NKT):
        VE.ops.append(lambda e, kt=_kt: e.memset(
            AP(VB, 520 * kt + 64, [[NKT * 520, 128], [130, 4], [65, 2]]), 1.0))
    NXC = min(4, NT)
    TPC = NT // NXC
    x_chunk_tok = []
    for c_ in range(NXC):
        SY.ops.append(lambda e, c_=c_: e.dma_start(
            out=X[:, 1024 * TPC * c_:1024 * TPC * (c_ + 1)],
            in_=AP(d_x, 128 * TPC * c_ * D, [[D, 128], [128 * D, TPC], [1, D]])).then_inc(dX.h, 16))
        dX.n += 16
        x_chunk_tok.append(dX.n)
    tok_x = dX.n

    def emit_ln(prev_x_tok=None, tile_toks=None):
        if DBG & 16:
            VE.wait(dX, tok_x)
            return VE.do(lambda e: e.memset(SML[:, 0:1], 0.0), inc=vV)
        first = prev_x_tok is None and tile_toks is None
        if not first:
            VE.wait(dX, tok_x)
        if tile_toks is None and prev_x_tok is not None:
            VE.wait(vV, prev_x_tok)
        VE.raw(lambda e: e.drain())
        for i in range(NT):
            xt = X[:, 1024 * i:1024 * (i + 1)]
            if tile_toks is not None:
                VE.wait(vV, tile_toks[i])
            if first:
                VE.wait(dX, x_chunk_tok[i // TPC])
            VE.do(lambda e, xt=xt, i=i: e.bn_stats(STAT[:, 12 * i:12 * i + 6], xt[:, 0:512]))
            VE.do(lambda e, xt=xt, i=i: e.bn_stats(STAT[:, 12 * i + 6:12 * i + 12], xt[:, 512:1024]))
        VE.raw(lambda e: e.drain())
        for i in range(NT):
            VE.do(lambda e, i=i: e.bn_aggr(SML[:, 80 + 2 * i:80 + 2 * i + 2], STAT[:, 12 * i:12 * i + 12]))
        VE.raw(lambda e: e.drain())
        VE.do(lambda e: e.tensor_scalar(SML[:, 0:NT], AP(SML, 81, [[128, 128], [2, NT]]),
                                        1e-5, None, ALU.add))
        vtok = VE.mark(vV)
        SE.wait(vV, vtok)
        stok = SE.do(lambda e: e.activation(out=SML[:, 16:16 + NT], in_=SML[:, 0:NT], func=AF.Sqrt), inc=aA)
        VE.wait(aA, stok)
        VE.raw(lambda e: e.drain())
        VE.do(lambda e: e.reciprocal(SML[:, 32:32 + NT], SML[:, 16:16 + NT]))
        VE.raw(lambda e: e.drain())
        for _ in range(2):
            VE.do(lambda e: e.tensor_tensor(SML[:, 48:48 + NT], SML[:, 32:32 + NT], SML[:, 32:32 + NT], ALU.mult))
            VE.raw(lambda e: e.drain())
            VE.do(lambda e: e.tensor_tensor(SML[:, 48:48 + NT], SML[:, 48:48 + NT], SML[:, 0:NT], ALU.mult))
            VE.raw(lambda e: e.drain())
            VE.do(lambda e: e.tensor_scalar(SML[:, 48:48 + NT], SML[:, 48:48 + NT], -0.5, 1.5, ALU.mult, ALU.add))
            VE.raw(lambda e: e.drain())
            VE.do(lambda e: e.tensor_tensor(SML[:, 32:32 + NT], SML[:, 32:32 + NT], SML[:, 48:48 + NT], ALU.mult))
            VE.raw(lambda e: e.drain())
        VE.do(lambda e: e.tensor_tensor(SML[:, 64:64 + NT], AP(SML, 80, [[128, 128], [2, NT]]),
                                        SML[:, 32:32 + NT], ALU.mult))
        VE.raw(lambda e: e.drain())
        VE.do(lambda e: e.tensor_scalar(SML[:, 64:64 + NT], SML[:, 64:64 + NT], -1.0, None, ALU.mult))
        ntok = VE.mark(vV)
        tr_tok = [None, None]
        last_ev = None
        ev_sems = []
        SE.wait(vV, ntok)
        for i in range(NT):
            xt = X[:, 1024 * i:1024 * (i + 1)]
            half = 1024 * (i % 2)
            xo = PT[:, half:half + 1024]
            if tr_tok[i % 2] is not None:
                SE.wait(tT, tr_tok[i % 2])
            atok = SE.do(lambda e, xt=xt, xo=xo, i=i: e.activation(
                out=xo, in_=xt, func=AF.Identity, bias=SML[:, 64 + i:65 + i],
                scale=SML[:, 32 + i:33 + i]), inc=aA)
            pt = P[i % 2]
            TE.wait(aA, atok)
            p_write_ready(TE, pt)
            for c in range(8):
                TE.do(lambda e, xo=xo, c=c, pt=pt: e.matmul(
                    pt.t[:, 128 * c:128 * (c + 1)], xo[:, 128 * c:128 * (c + 1)],
                    IDNB[:, :], start=True, stop=True))
            ttok = TE.mark(tT)
            p_mark_w(pt, ttok)
            tr_tok[i % 2] = ttok
            dst = AP(XLN, 128 * i, [[NCH * T, 128], [T, 8], [1, 128]])
            VE.wait(tT, ttok)
            etok = VE.do(lambda e, pt=pt, dst=dst: e.tensor_copy(dst, pt.t[:, 0:1024]), inc=vV)
            p_mark_r(pt, VE, etok)
            last_ev = etok
        return last_ev

    rows_free = [None]
    rb_free = [None]
    stg_free = [0]
    rowA_free = [None]
    rowB_free = [None]
    sh_free = [None]
    bc_free = [None, None]

    def emit_bias_bcast(src_row, reader_toks):
        if DBG & 32:
            return None
        s = WSL[6]
        if rows_free[0] is not None:
            GE.wait(tT, rows_free[0])
        tok = s.nxt(16)
        GE.ops.append(lambda e, sr=src_row: e.dma_start(
            out=ROWS[0:1, :], in_=sr).then_inc(s.h, 16))
        pp = P[0]
        TE.wait(WSL[6], tok)
        p_write_ready(TE, pp)
        for f in range(2):
            btok = TE.do(lambda e, f=f: e.matmul(pp.t[:, 512 * f:512 * (f + 1)],
                                                 ONESF[0:1, 0:128],
                                                 ROWS[0:1, 512 * f:512 * (f + 1)],
                                                 start=True, stop=True), inc=tT)
        p_mark_w(pp, btok)
        rows_free[0] = btok
        SE.wait(tT, btok)
        for st, tk in reader_toks:
            SE.wait(st, tk)
        atok = SE.do(lambda e: e.activation(out=BIAS[:, :], in_=pp.t[:, 0:1024], func=AF.Identity), inc=aA)
        p_mark_r(pp, SE, atok)
        return atok

    def emit_pair_proj(wsem, wcol, dst_sl, bias_col, wtok, pps, extra_wait=None):
        if DBG & 8:
            return None
        TE.wait(wsem, wtok)
        if extra_wait is not None:
            TE.wait(*extra_wait)
        last_atok = None
        for sp in range(NSP512):
            pp = pps[sp % len(pps)]
            half = SP512 * ((sp // len(pps)) % 2) if SP512 == 512 else 0
            p_write_ready(TE, pp)
            for ch in range(8):
                lhsT = WA[:, wcol + 128 * ch: wcol + 128 * (ch + 1)]
                rhs = XLN[:, ch * T + SP512 * sp: ch * T + SP512 * (sp + 1)]
                TE.do(lambda e, o=pp.t[:, half:half + SP512], l=lhsT, r=rhs,
                      s=(ch == 0), p2=(ch == 7): e.matmul(o, l, r, start=s, stop=p2))
            mtok = TE.mark(tT)
            p_mark_w(pp, mtok)
            SE.wait(tT, mtok)
            out = dst_sl[:, SP512 * sp:SP512 * (sp + 1)]
            if bias_col is not None:
                atok = SE.do(lambda e, o=out, i=pp.t[:, half:half + SP512],
                             b=QKVB[:, bias_col:bias_col + 1]:
                             e.activation(out=o, in_=i, func=AF.Identity, bias=b), inc=aA)
            else:
                atok = SE.do(lambda e, o=out, i=pp.t[:, half:half + SP512]:
                             e.activation(out=o, in_=i, func=AF.Identity), inc=aA)
            p_mark_r(pp, SE, atok)
            last_atok = atok
        return last_atok

    def emit_v_half(wcol, bias_half, wsem, wtok, bias_tok):
        if DBG & 8:
            return None
        TE.wait(wsem, wtok)
        last = None
        for kt in range(NKT):
            pp = P[kt % 2]
            p_write_ready(TE, pp)
            for ch in range(8):
                lhsT = XLN[:, ch * T + 128 * kt: ch * T + 128 * (kt + 1)]
                rhs = WA[:, wcol + 512 * ch: wcol + 512 * (ch + 1)]
                TE.do(lambda e, o=pp.t[:, 0:512], l=lhsT, r=rhs, s=(ch == 0), p2=(ch == 7):
                      e.matmul(o, l, r, start=s, stop=p2))
            mtok = TE.mark(tT)
            p_mark_w(pp, mtok)
            VE.wait(tT, mtok)
            if bias_tok is not None and kt == 0:
                VE.wait(aA, bias_tok)
            dst = AP(VB, 520 * kt, [[NKT * 520, 128], [130, 4], [65, 2], [1, 64]])
            srcp = AP(pp.t.handle, pp.t.off, [[2048, 128], [128, 4], [64, 2], [1, 64]])
            if bias_half is not None:
                bsrc = AP(BIAS, 512 * bias_half, [[1024, 128], [128, 4], [64, 2], [1, 64]])
                vtok = VE.do(lambda e, d=dst, i=srcp, b=bsrc:
                             e.tensor_tensor(d, i, b, ALU.add), inc=vV)
            else:
                vtok = VE.do(lambda e, d=dst, i=srcp: e.tensor_copy(d, i), inc=vV)
            p_mark_r(pp, VE, vtok)
            last = vtok
        return last

    def emit_attention_pair(c, causal, cshift, qk_tok, v_tok):
        if DBG & 4:
            return v_tok
        qsl = QS[:, c * T:(c + 1) * T]
        koff = ((c % 4) if causal else (c % 2)) * T
        ksl = KS[:, koff:koff + T]
        vbase = 130 * (c % 4)
        S_A, S_B, PV_A, PV_B = P[0], P[1], P[2], P[3]
        TE.wait(vV, v_tok)
        if qk_tok is not None:
            TE.wait(aA, qk_tok)
        last_ev = None
        for sp in range(NSP):
            kts = [kt for kt in range(NKT)
                   if not (causal and 128 * kt > SPAN * sp + SPAN - 1)]
            def _off(kt):
                return max(0, 128 * kt - SPAN * sp) if causal else 0
            last_kt_bank = {}
            for bi in range((SPAN + 511) // 512):
                contrib = [kt for kt in kts if _off(kt) < 512 * bi + 512]
                last_kt_bank[bi] = contrib[-1]
            def emit_pv(hh, kt, ki, segs, ptok, slot):
                pv = PV_A if hh == 0 else PV_B
                TE.wait(*ptok)
                if ki == 0:
                    p_write_ready(TE, pv)
                vsl = VB[:, 520 * kt + vbase + 65 * hh:520 * kt + vbase + 65 * hh + 65]
                st = (ki == 0)
                for (a, b) in segs:
                    prt = PT[:, slot + SPAN * hh + a:slot + SPAN * hh + b]
                    fin = (kt == last_kt_bank[a // 512])
                    TE.do(lambda e, o_=pv.t[0:65, a:b], l=vsl, r=prt, s=st, f=fin:
                          e.matmul(o_, l, r, start=s, stop=f))
                pvtok = TE.mark(tT)
                p_mark_w(pv, pvtok)

            pending = None
            for ki, kt in enumerate(kts):
                o = max(0, 128 * kt - SPAN * sp) if causal else 0
                segs = []
                n0 = o
                while n0 < SPAN:
                    n1 = min((n0 // 512 + 1) * 512, SPAN)
                    segs.append((n0, n1))
                    n0 = n1
                slot = 2 * SPAN * (kt % 2)
                mtoks = [None, None]
                for hh in range(2):
                    pp = S_A if hh == 0 else S_B
                    rowlo = 64 * hh
                    p_write_ready(TE, pp)
                    lhsT = ksl[rowlo:rowlo + 64, 128 * kt:128 * (kt + 1)]
                    for (a, b) in segs:
                        rhs = qsl[rowlo:rowlo + 64, SPAN * sp + a:SPAN * sp + b]
                        TE.do(lambda e, o_=pp.t[0:128, a:b], l=lhsT, r=rhs, tp=(rowlo, 0):
                              e.matmul(o_, l, r, start=True, stop=True, tile_position=tp))
                    mtoks[hh] = TE.mark(tT)
                    p_mark_w(pp, mtoks[hh])
                ptoks = [None, None]
                if False:
                    SE.wait(tT, mtoks[1])
                    etok = SE.do(lambda e, o_=PT[:, slot:slot + 2 * SPAN],
                                 i=PAB[:, 0:2 * SPAN], cs=cshift:
                                 e.activation(out=o_, in_=i, func=AF.Exp,
                                              bias=CSH[:, cs:cs + 1], scale=0.125),
                                 inc=aA)
                    p_mark_r(S_A, SE, etok)
                    p_mark_r(S_B, SE, etok)
                    etoks = [etok, etok]
                else:
                    etoks = [None, None]
                    for hh in range(2):
                        pp = S_A if hh == 0 else S_B
                        SE.wait(tT, mtoks[hh])
                        etoks[hh] = SE.do(lambda e, o_=PT[:, slot + SPAN * hh + o:slot + SPAN * (hh + 1)],
                                          i=pp.t[:, o:SPAN], cs=cshift:
                                          e.activation(out=o_, in_=i, func=AF.Exp,
                                                       bias=CSH[:, cs:cs + 1], scale=0.125),
                                          inc=aA)
                        p_mark_r(pp, SE, etoks[hh])
                for hh in range(2):
                    if causal and 128 * kt >= SPAN * sp:
                        VE.wait(aA, etoks[hh])
                        mtok2 = VE.do(lambda e, o_=PT[:, slot + SPAN * hh + o:slot + SPAN * hh + o + 128]:
                                      e.tensor_tensor(o_, o_, TRI[:, :], ALU.mult), inc=vV)
                        ptoks[hh] = (vV, mtok2)
                    else:
                        ptoks[hh] = (aA, etoks[hh])
                if pending is not None:
                    emit_pv(0, pending[0], pending[1], pending[2], pending[3], pending[5])
                    emit_pv(1, pending[0], pending[1], pending[2], pending[4], pending[5])
                pending = (kt, ki, segs, ptoks[0], ptoks[1], slot)
            emit_pv(0, pending[0], pending[1], pending[2], pending[3], pending[5])
            emit_pv(1, pending[0], pending[1], pending[2], pending[4], pending[5])
            if DBG & 1:
                last_ev = vV.n if vV.n > 0 else None
                continue
            spc0, spc1 = SPAN * sp, SPAN * (sp + 1)
            SE.wait(*PV_A.wtok)
            if rowA_free[0] is not None:
                SE.wait(dB, rowA_free[0])
            SE.do(lambda e: e.activation(out=ROWS[64:65, 0:SPAN], in_=PV_A.t[64:65, 0:SPAN], func=AF.Ln))
            SE.wait(*PV_B.wtok)
            if rowB_free[0] is not None:
                SE.wait(dB, rowB_free[0])
            SE.do(lambda e: e.activation(out=RB[64:65, 0:SPAN], in_=PV_B.t[64:65, 0:SPAN], func=AF.Ln))
            SE.raw(lambda e: e.drain())
            r1A = SE.do(lambda e: e.activation(out=ROWS[64:65, 0:SPAN], in_=ROWS[64:65, 0:SPAN], func=AF.Exp, scale=-1.0), inc=aA)
            p_mark_r(PV_A, SE, r1A)
            r1B = SE.do(lambda e: e.activation(out=RB[64:65, 0:SPAN], in_=RB[64:65, 0:SPAN], func=AF.Exp, scale=-1.0), inc=aA)
            p_mark_r(PV_B, SE, r1B)
            for hh, (pv, r1, srow) in enumerate(((PV_A, r1A, ROWS), (PV_B, r1B, RB))):
                dst = RB if hh == 0 else SHRD
                dco = 0 if hh == 0 else 1024
                GE.wait(aA, r1)
                if bc_free[hh] is not None:
                    GE.wait(dB, bc_free[hh])
                otok = dB.nxt(16)
                GE.ops.append(lambda e, hh=hh, srow=srow: e.dma_start(
                    out=d_bc.ap()[hh:hh + 1, :], in_=srow[64:65, 0:SPAN]).then_inc(dB.h, 16))
                if hh == 0:
                    rowA_free[0] = otok
                else:
                    rowB_free[0] = otok
                GE.wait(dB, otok)
                GE.wait(vV, rb_free[0] if hh == 0 else sh_free[0])
                for (sa, sb) in ((x, min(x + 512, SPAN)) for x in range(0, SPAN, 512)):
                    dB.n += 16
                    GE.ops.append(lambda e, hh=hh, dst=dst, a=dco + sa, s=sa, n_=sb - sa:
                                  e.dma_start(out=dst[0:64, a:a + n_],
                                              in_=AP(d_bc, SPAN * hh + s, [[0, 64], [1, n_]])).then_inc(dB.h, 16))
                itok = dB.n
                bc_free[hh] = itok
                VE.wait(dB, itok)
                VE.wait(*pv.wtok)
                if hh == 0:
                    e1 = VE.do(lambda e, d=qsl[0:64, spc0:spc1], pv=pv:
                               e.tensor_tensor(d, pv.t[0:64, 0:SPAN], RB[0:64, 0:SPAN], ALU.mult), inc=vV)
                    p_mark_r(pv, VE, e1)
                    rb_free[0] = e1
                else:
                    VE.wait(dF, stg_free[0])
                    e2 = VE.do(lambda e, pv=pv:
                               e.tensor_tensor(STG[0:64, 0:SPAN], pv.t[0:64, 0:SPAN], SHRD[0:64, 1024:1024 + SPAN], ALU.mult), inc=vV)
                    p_mark_r(pv, VE, e2)
                    sh_free[0] = e2
                    GE.wait(vV, e2)
                    dF.n += 16
                    stg_free[0] = dF.n
                    GE.ops.append(lambda e, d=qsl[64:128, spc0:spc1]:
                                  e.dma_start(out=d, in_=STG[0:64, 0:SPAN]).then_inc(dF.h, 16))
            last_ev = e2
        return last_ev

    def emit_out_proj(src_buf, w_dram, wslots, bias_tok, ready_toks, store=False):
        last_vtok = None
        pass_mm_tok = [None, None]
        p1_toks = [None] * NT
        if DBG & 2:
            return (vV.n if vV.n else None), [None] * NT, pass_mm_tok
        for p_ in range(2):
            wsem, wcol, afree = wslots[p_]
            src_ap = AP(w_dram, 512 * p_ * D, [[D, 128], [128 * D, 4], [1, D]])
            wtok = dma_w(wsem, WA[:, wcol:wcol + 4096], src_ap, after_tok=afree)
            TE.wait(wsem, wtok)
            if p_ == 0:
                TE.wait(dF, dF.n)
                for s, t in ready_toks:
                    TE.wait(s, t)
            for i in range(NT):
                pp = P[i % 2] if p_ == 0 else P[2 + i % 2]
                p_write_ready(TE, pp)
                for ch4 in range(4):
                    ch = 4 * p_ + ch4
                    lhsT = src_buf[:, ch * T + 128 * i: ch * T + 128 * (i + 1)]
                    for f in range(2):
                        rhs = WA[:, wcol + 1024 * ch4 + 512 * f: wcol + 1024 * ch4 + 512 * (f + 1)]
                        TE.do(lambda e, o=pp.t[:, 512 * f:512 * (f + 1)], l=lhsT, r=rhs,
                              s=(ch4 == 0), p2=(ch4 == 3): e.matmul(o, l, r, start=s, stop=p2))
                mtok = TE.mark(tT)
                p_mark_w(pp, mtok)
                pass_mm_tok[p_] = mtok
                VE.wait(tT, mtok)
                xsl = X[:, 1024 * i:1024 * (i + 1)]
                vt = VE.do(lambda e, x=xsl, i_=pp.t[:, 0:1024]:
                           e.tensor_tensor(x, x, i_, ALU.add), inc=vV)
                p_mark_r(pp, VE, vt)
                last_vtok = vt
                if p_ == 1:
                    p1_toks[i] = vt
            if p_ == 0 and bias_tok is not None:
                VE.wait(aA, bias_tok)
                VE.raw(lambda e: e.drain())
                for i in range(NT):
                    xsl = X[:, 1024 * i:1024 * (i + 1)]
                    VE.do(lambda e, x=xsl: e.tensor_tensor(x, x, BIAS[:, :], ALU.add))
        tile_toks = p1_toks
        if store:
            for i in range(NT):
                SY.wait(vV, tile_toks[i])
                dO.n += 16
                SY.ops.append(lambda e, i=i: e.dma_start(
                    out=AP(d_y, 128 * i * D, [[D, 128], [1, D]]),
                    in_=X[:, 1024 * i:1024 * (i + 1)]).then_inc(dO.h, 16))
        return last_vtok, tile_toks, pass_mm_tok

    GE.wait(dC, tok_consts)
    TE.wait(dC, tok_consts)
    VE.wait(dC, tok_consts)
    SE.wait(dC, tok_consts)
    ln_tok = emit_ln()
    vb_tok = emit_bias_bcast(d_vbrow.ap()[:, :], [])

    pair_free = [None, None]
    v_free = None
    v_tok = None
    last_attn = None
    for c in range(NPAIR):
        wsem = WSL[c % 2]
        wcol = 2048 * (c % 2)
        qtok = dma_w(wsem, WA[:, wcol:wcol + 1024],
                     AP(d_wqkvT, 128 * c, [[3 * D, 128], [3 * D * 128, 8], [1, 128]]),
                     after_tok=pair_free[c % 2])
        ktok = dma_w(wsem, WA[:, wcol + 1024:wcol + 2048],
                     AP(d_wqkvT, D + 128 * c, [[3 * D, 128], [3 * D * 128, 8], [1, 128]]))
        emit_pair_proj(wsem, wcol, QS[:, c * T:(c + 1) * T], c, ktok, [P[2], P[3]],
                       extra_wait=(vV, ln_tok) if c == 0 else None)
        ak = emit_pair_proj(wsem, wcol + 1024, KS[:, (c % 2) * T:(c % 2) * T + T],
                            8 + c, ktok, [P[2], P[3]])
        pair_free[c % 2] = tT.n
        if c in (0, 4):
            vwtok = dma_w(WSL[2], WA[:, 4096:8192],
                          AP(d_wqkvT, 2 * D + 512 * (c // 4),
                             [[3 * D, 128], [3 * D * 128, 8], [1, 512]]),
                          after_tok=v_free)
            v_tok = emit_v_half(4096, c // 4, WSL[2], vwtok, vb_tok if c == 0 else None)
            v_free = tT.n
        last_attn = emit_attention_pair(c, False, 0, ak, v_tok)

    ob_tok = emit_bias_bcast(d_obrow.ap()[:, :], [(vV, v_tok)])
    wo_slots = [(WSL[4], 0, tT.n), (WSL[5], 4096, tT.n)]
    wo_tok, wo_tts, wo_pmt = emit_out_proj(QS, d_woT, wo_slots, ob_tok, [(vV, last_attn)])

    def emit_ffn(prev_tok, bias_reader_tok, store, tile_toks=None, prev_pmt=None):
        ln_t = emit_ln(prev_x_tok=prev_tok, tile_toks=tile_toks)
        wa_free = tT.n
        wa_p0 = prev_pmt[0] if prev_pmt and prev_pmt[0] else wa_free
        wa_p1 = prev_pmt[1] if prev_pmt and prev_pmt[1] else wa_free
        rtok = dma_w(WSL[2], WA[:, 2048:2048 + 2040],
                     AP(d_routT, 0, [[255, 128], [255 * 128, 8], [1, 255]]),
                     after_tok=wa_p0)
        lvtok = dma_w(WSL[3], WA[:, 4096:6144],
                      AP(d_lv, 0, [[D, 128], [D * 128, 2], [1, D]]),
                      after_tok=wa_p1)
        fb_tok = emit_bias_bcast(d_fbrow.ap()[:, :], [(vV, bias_reader_tok)])
        qf_tok = None
        slot_free = [wa_p0, wa_p0]
        for et in range(8):
            wsem = WSL[et % 2]
            wcol = 1024 * (et % 2)
            wtok = dma_w(wsem, WA[:, wcol:wcol + 1024],
                         AP(d_fwqT, 128 * et, [[D, 128], [D * 128, 8], [1, 128]]),
                         after_tok=slot_free[et % 2])
            qf_tok = emit_pair_proj(wsem, wcol, QS[:, et * T:(et + 1) * T], None,
                                    wtok, [P[0], P[1]],
                                    extra_wait=(vV, ln_t) if et == 0 else None)
            slot_free[et % 2] = tT.n
        TE.wait(WSL[2], rtok)
        TE.wait(WSL[3], lvtok)

        NSPW = T // SP512
        TPS = SP512 // 128
        lp_read = {}
        tr_toks = {}
        sig_tok = {}
        ret_done = {}

        def stage_a(i):
            pz = P[2 + i % 2]
            p_write_ready(TE, pz)
            if i == 0:
                TE.wait(aA, qf_tok)
            for ch in range(8):
                lhsT = QS[:, ch * T + 128 * i: ch * T + 128 * (i + 1)]
                rhs = WA[:, 2048 + 255 * ch: 2048 + 255 * (ch + 1)]
                TE.do(lambda e, o=pz.t[:, 0:255], l=lhsT, r=rhs, s_=(ch == 0), p2=(ch == 7):
                      e.matmul(o, l, r, start=s_, stop=p2))
            ztok = TE.mark(tT)
            p_mark_w(pz, ztok)
            g2o = 512 * (i % 2)
            SE.wait(tT, ztok)
            if i >= 2 and (i - 2) in lp_read:
                SE.wait(vV, lp_read[i - 2])
            SE.do(lambda e, o=SHRD[:, g2o:g2o + 255], i_=pz.t[:, 0:255]:
                  e.activation(out=o, in_=i_, func=AF.Sigmoid, scale=-1.0))
            gtok = SE.do(lambda e, o=SHRD[:, g2o + 255:g2o + 510], i_=pz.t[:, 0:255]:
                         e.activation(out=o, in_=i_, func=AF.Sigmoid), inc=aA)
            p_mark_r(pz, SE, gtok)
            sig_tok[i] = gtok

        def stage_b(i):
            g2o = 512 * (i % 2)
            EN = GE if i % 2 == 0 else VE
            EN.wait(aA, sig_tok[i])
            if i >= 2 and (i - 2) in tr_toks:
                EN.wait(tT, tr_toks[i - 2])
            lpA = 1024 + 512 * (i % 2)
            lpB = lpA + 256
            ptf = 256 * (i % 2)
            EN.do(lambda e, g2o=g2o, lpA=lpA: e.tensor_copy(
                AP(SHRD, lpA, [[2048, 128], [1, 2]]),
                AP(SHRD, g2o, [[2048, 128], [255, 2]])))
            if EN is VE:
                EN.raw(lambda e: e.drain())
            cur, oth = lpA, lpB
            for lv_ in range(1, 8):
                n = 1 << lv_
                last = lv_ == 7
                o_ev = AP(PT, ptf, [[PTW, 128], [2, n]]) if last else AP(SHRD, oth, [[2048, 128], [2, n]])
                o_od = AP(PT, ptf + 1, [[PTW, 128], [2, n]]) if last else AP(SHRD, oth + 1, [[2048, 128], [2, n]])
                EN.do(lambda e, cur=cur, o_=o_ev, n=n, g2o=g2o: e.tensor_tensor(
                    o_, AP(SHRD, cur, [[2048, 128], [1, n]]),
                    AP(SHRD, g2o + n - 1, [[2048, 128], [1, n]]), ALU.mult))
                EN.do(lambda e, cur=cur, o_=o_od, n=n, g2o=g2o: e.tensor_tensor(
                    o_, AP(SHRD, cur, [[2048, 128], [1, n]]),
                    AP(SHRD, g2o + 255 + n - 1, [[2048, 128], [1, n]]), ALU.mult))
                if EN is VE:
                    EN.raw(lambda e: e.drain())
                cur, oth = oth, cur
            lptok = EN.mark(gG if EN is GE else vV)
            lpsem = gG if EN is GE else vV
            pt_ = P[(i + 1) % 2]
            TE.wait(lpsem, lptok)
            p_write_ready(TE, pt_)
            for b in range(2):
                TE.do(lambda e, b=b, pt_=pt_, ptf=ptf: e.matmul(
                    pt_.t[:, 256 + 128 * b:256 + 128 * (b + 1)],
                    PT[:, ptf + 128 * b:ptf + 128 * (b + 1)], IDNB[:, :], start=True, stop=True))
            ttk = TE.mark(tT)
            p_mark_w(pt_, ttk)
            tr_toks[i] = ttk
            VE.wait(tT, ttk)
            dst = AP(KS, 128 * i, [[4 * T, 128], [T, 2], [1, 128]])
            lp_read[i] = VE.do(lambda e, dst=dst, pt_=pt_: e.tensor_copy(dst, pt_.t[:, 256:512]), inc=vV)
            p_mark_r(pt_, VE, lp_read[i])

        def emit_ret(s):
            TE.wait(vV, lp_read[TPS * (s + 1) - 1])
            for dt_ in range(8):
                pp = P[dt_ % 2]
                p_write_ready(TE, pp)
                for m in range(2):
                    lhsT = WA[:, 4096 + 1024 * m + 128 * dt_: 4096 + 1024 * m + 128 * (dt_ + 1)]
                    rhs = KS[:, m * T + SP512 * s: m * T + SP512 * (s + 1)]
                    TE.do(lambda e, o=pp.t[:, 0:SP512], l=lhsT, r=rhs, s_=(m == 0), p2=(m == 1):
                          e.matmul(o, l, r, start=s_, stop=p2))
                mtok = TE.mark(tT)
                p_mark_w(pp, mtok)
                SE.wait(tT, mtok)
                atok = SE.do(lambda e, o=XLN[:, dt_ * T + SP512 * s: dt_ * T + SP512 * (s + 1)],
                             i_=pp.t[:, 0:SP512]: e.activation(out=o, in_=i_, func=AF.Identity),
                             inc=aA)
                p_mark_r(pp, SE, atok)
                ret_done[s] = atok

        for s in range(NSPW):
            for i in range(TPS * s, TPS * (s + 1)):
                stage_a(i)
                if i >= 1:
                    stage_b(i - 1)
            if s >= 1:
                emit_ret(s - 1)
        stage_b(TPS * NSPW - 1)
        emit_ret(NSPW - 1)
        wslots = [(WSL[4], 0, tT.n), (WSL[5], 4096, tT.n)]
        lv_, tts_, pmt_ = emit_out_proj(XLN, d_fwoutT, wslots, fb_tok, [(aA, ret_done[NSPW - 1])], store=store)
        return lv_, tts_, pmt_

    ffn1_res = emit_ffn(wo_tok, wo_tok, store=False, tile_toks=wo_tts, prev_pmt=wo_pmt) if n_sub >= 2 else None
    ffn1_tok, ffn1_tts, ffn1_pmt = ffn1_res if ffn1_res else (None, None, [None, None])

    def emit_attn_b(prev_tok, tile_toks=None, prev_pmt=None):
        ln_t = emit_ln(prev_x_tok=prev_tok, tile_toks=tile_toks)
        wa_free = tT.n
        wa_p0 = prev_pmt[0] if prev_pmt and prev_pmt[0] else wa_free
        wa_p1 = prev_pmt[1] if prev_pmt and prev_pmt[1] else wa_free
        cs_sem = WSL[3]
        GE.wait(vV, ln_t)
        ctok = cs_sem.nxt(16)
        GE.ops.append(lambda e: e.dma_start(out=SHRD[:, 0:32 * NT],
                                            in_=d_cos.ap()[:, :]).then_inc(cs_sem.h, 16))
        stok = cs_sem.nxt(16)
        GE.ops.append(lambda e: e.dma_start(out=SHRD[:, 512:512 + 32 * NT],
                                            in_=d_sin.ap()[:, :]).then_inc(cs_sem.h, 16))
        last_attn = None
        quad_free = wa_free
        v_free2 = wa_p0
        rt_by = {}
        tt_by = {}
        VBP = NKT * 520
        ro_buf = [(PT, 0, PTW), (PT, 1024, PTW), (VB, 0, VBP)]
        for quad in range(2):
            wsem = WSL[quad % 2]
            qwtok = dma_w(wsem, WA[:, 0:4096],
                          AP(d_bwqT, 512 * quad, [[D, 128], [D * 128, 8], [1, 512]]),
                          after_tok=v_free2 if quad else wa_p0)
            kwtok = dma_w(wsem, WA[:, 4096:8192],
                          AP(d_bwkT, 512 * quad, [[D, 128], [D * 128, 8], [1, 512]]),
                          after_tok=None if quad else wa_p1)
            TE.wait(wsem, kwtok)
            if quad == 0:
                TE.wait(vV, ln_t)
            VE.wait(cs_sem, stok)
            pair_tok = None

            def flush_pend(i, g, quad=None):
                pt_ = P[i % 2]
                buf, base, _ = ro_buf[g % 3]
                TE.wait(vV, rt_by[g])
                p_write_ready(TE, pt_)
                for b in range(8):
                    TE.do(lambda e, b=b, pt_=pt_, buf=buf, base=base: e.matmul(
                        pt_.t[:, 128 * b:128 * (b + 1)], buf[:, base + 128 * b:base + 128 * (b + 1)],
                        IDNB[:, :], start=True, stop=True))
                ttok = TE.mark(tT)
                p_mark_w(pt_, ttok)
                tt_by[g] = ttok
                qdst = AP(QS, (4 * quad) * T + 128 * i, [[NPAIR * T, 128], [T, 4], [1, 128]])
                kdst = AP(KS, 128 * i, [[4 * T, 128], [T, 4], [1, 128]])
                SE.wait(tT, ttok)
                SE.do(lambda e, d=qdst, pt_=pt_: e.activation(out=d, in_=pt_.t[:, 0:512], func=AF.Identity))
                ptok = SE.do(lambda e, d=kdst, pt_=pt_: e.activation(out=d, in_=pt_.t[:, 512:1024], func=AF.Identity), inc=aA)
                p_mark_r(pt_, SE, ptok)
                return ptok

            for i in range(NT):
                g = quad * NT + i
                pq = P[2 + i % 2]
                p_write_ready(TE, pq)
                for ch in range(8):
                    lhsT = XLN[:, ch * T + 128 * i: ch * T + 128 * (i + 1)]
                    TE.do(lambda e, o=pq.t[:, 0:512], l=lhsT, r=WA[:, 512 * ch:512 * (ch + 1)],
                          s=(ch == 0), p2=(ch == 7): e.matmul(o, l, r, start=s, stop=p2))
                for ch in range(8):
                    lhsT = XLN[:, ch * T + 128 * i: ch * T + 128 * (i + 1)]
                    TE.do(lambda e, o=pq.t[:, 512:1024], l=lhsT, r=WA[:, 4096 + 512 * ch:4096 + 512 * (ch + 1)],
                          s=(ch == 0), p2=(ch == 7): e.matmul(o, l, r, start=s, stop=p2))
                mtok = TE.mark(tT)
                p_mark_w(pq, mtok)
                EV = 2048 + 1024 * (i % 2)
                SE.wait(tT, mtok)
                if g - 2 in rt_by:
                    SE.wait(vV, rt_by[g - 2])
                pqh = getattr(pq.t, "handle", pq.t)
                pqo = getattr(pq.t, "off", 0)
                ppitch = 2048 if pqh is PAB else 1024
                inev = AP(pqh, pqo, [[ppitch, 128], [64, 16], [2, 32]])
                inod = AP(pqh, pqo + 1, [[ppitch, 128], [64, 16], [2, 32]])
                evev = AP(PT, EV, [[PTW, 128], [64, 16], [1, 32]])
                evod = AP(PT, EV + 32, [[PTW, 128], [64, 16], [1, 32]])
                SE.do(lambda e, o_=evev, i_=inev: e.activation(out=o_, in_=i_, func=AF.Identity))
                a2 = SE.do(lambda e, o_=evod, i_=inod: e.activation(out=o_, in_=i_, func=AF.Identity), inc=aA)
                p_mark_r(pq, SE, a2)
                co = 32 * i
                rbuf, rbase, rpitch = ro_buf[g % 3]
                sin_r = AP(SHRD, 512 + co, [[2048, 128], [0, 16], [1, 32]])
                cos_r = AP(SHRD, co, [[2048, 128], [0, 16], [1, 32]])
                t1 = ROWS[:, 0:512]
                t2 = ROWS[:, 512:1024]
                bev = AP(rbuf, rbase, [[rpitch, 128], [64, 16], [1, 32]])
                bod = AP(rbuf, rbase + 32, [[rpitch, 128], [64, 16], [1, 32]])
                VE.wait(aA, a2)
                if g - 3 in tt_by:
                    VE.wait(tT, tt_by[g - 3])
                VE.do(lambda e, a=t1, b=evev, s=sin_r: e.tensor_tensor(a, b, s, ALU.mult))
                VE.do(lambda e, a=t2, b=evod, s=sin_r: e.tensor_tensor(a, b, s, ALU.mult))
                VE.raw(lambda e: e.drain())
                VE.do(lambda e, a=evev, c_=cos_r: e.tensor_tensor(a, a, c_, ALU.mult))
                VE.do(lambda e, a=evod, c_=cos_r: e.tensor_tensor(a, a, c_, ALU.mult))
                VE.raw(lambda e: e.drain())
                VE.do(lambda e, o_=bev, a=evev, b=t2: e.tensor_tensor(o_, a, b, ALU.subtract))
                VE.do(lambda e, o_=bod, a=evod, b=t1: e.tensor_tensor(o_, a, b, ALU.add))
                rtok = VE.mark(vV)
                rt_by[g] = rtok
                if i >= 2:
                    pair_tok = flush_pend(i - 2, g - 2, quad=quad)
            pair_tok = flush_pend(NT - 2, quad * NT + NT - 2, quad=quad)
            pair_tok = flush_pend(NT - 1, quad * NT + NT - 1, quad=quad)
            quad_free = tT.n
            vwtok = dma_w(WSL[2], WA[:, 0:4096],
                          AP(d_bwvT, 512 * quad, [[D, 128], [D * 128, 8], [1, 512]]),
                          after_tok=quad_free)
            VE.wait(tT, tt_by[quad * NT + NT - 1])
            for kt_ in (0, 1):
                VE.do(lambda e, kt=kt_: e.memset(
                    AP(VB, 520 * kt + 64, [[VBP, 128], [130, 4], [65, 2]]), 1.0))
            v_tok2 = emit_v_half(0, None, WSL[2], vwtok, None)
            v_free2 = tT.n
            for cl in range(4):
                c = 4 * quad + cl
                last_attn = emit_attention_pair(c, True, 1, pair_tok if cl == 0 else None, v_tok2)
        wslots = [(WSL[4], 0, tT.n), (WSL[5], 4096, None)]
        lvb_, ttsb_, pmtb_ = emit_out_proj(QS, d_bwoT, wslots, None, [(vV, last_attn)])
        return lvb_, ttsb_, pmtb_

    def emit_store(last_tok):
        SY.wait(vV, last_tok)
        if DBG & 512:
            GE.wait(aA, aA.n)
            GE.wait(vV, vV.n)
            dO.n += 16
            GE.ops.append(lambda e: e.dma_start(
                out=AP(d_y, 0, [[128, 128], [1, 128]]),
                in_=SML[:, :]).then_inc(dO.h, 16))
            return
        if DBG & 256:
            GE.wait(aA, aA.n)
            GE.wait(vV, vV.n)
            GE.wait(tT, tT.n)
            dO.n += 16
            GE.ops.append(lambda e, PTW=PTW: e.dma_start(
                out=AP(d_y, 0, [[D, 128], [128 * D, PTW // 1024], [1, D]]),
                in_=AP(PT, 0, [[PTW, 128], [1024, PTW // 1024], [1, 1024]])).then_inc(dO.h, 16))
            return
        if DBG & 128:
            GE.wait(aA, aA.n)
            GE.wait(vV, vV.n)
            dO.n += 16
            GE.ops.append(lambda e: e.dma_start(
                out=AP(d_y, 0, [[D, 128], [128 * D, (NCH * T) // 1024], [1, D]]),
                in_=AP(XLN, 0, [[NCH * T, 128], [1024, (NCH * T) // 1024], [1, 1024]])).then_inc(dO.h, 16))
            return
        if DBG & 64:
            GE.wait(aA, aA.n)
            GE.wait(vV, vV.n)
            GE.wait(dF, dF.n)
            dO.n += 16
            GE.ops.append(lambda e: e.dma_start(
                out=AP(d_y, 0, [[D, 128], [128 * D, NPAIR * T // 1024, ], [1, D]]) if False else AP(d_y, 0, [[D, 128], [128 * D, (NPAIR * T) // 1024], [1, D]]),
                in_=AP(QS, 0, [[NPAIR * T, 128], [1024, (NPAIR * T) // 1024], [1, 1024]])).then_inc(dO.h, 16))
            return
        for i in range(NT):
            dO.n += 16
            SY.ops.append(lambda e, i=i: e.dma_start(
                out=AP(d_y, 128 * i * D, [[D, 128], [1, D]]),
                in_=X[:, 1024 * i:1024 * (i + 1)]).then_inc(dO.h, 16))

    if n_sub >= 3:
        attb_tok, attb_tts, attb_pmt = emit_attn_b(ffn1_tok, tile_toks=ffn1_tts, prev_pmt=ffn1_pmt)
    if n_sub >= 4:
        ffn2_tok = emit_ffn(attb_tok, attb_tok, store=True, tile_toks=attb_tts, prev_pmt=attb_pmt)[0]
    elif n_sub == 3:
        emit_store(attb_tok)
    elif n_sub == 2:
        emit_store(ffn1_tok)
    else:
        emit_store(wo_tok)

    SY.raw(lambda e, n=dO.n: e.wait_ge(dO.h, n))

    import os as _os
    if _os.environ.get("KCOUNT"):
        print("opcounts:", {nm: len(EG.ops) for nm, EG in (("t", TE), ("v", VE), ("s", SE), ("g", GE), ("y", SY))})
    _tr = _os.environ.get("KBISECT", "")
    if _tr:
        eng_sel, frac = _tr.split(":")
        frac = float(frac)
        for nm, EG in (("t", TE), ("v", VE), ("s", SE), ("g", GE), ("y", SY)):
            if eng_sel != "all" and nm not in eng_sel:
                EG.ops = []
            else:
                EG.ops = EG.ops[:int(len(EG.ops) * frac)]
    with es:
        block = es.enter_context(nc.Block())

        @block.gpsimd
        def _(g):
            for f in GE.ops:
                f(g)

        @block.vector
        def _(v):
            for f in VE.ops:
                f(v)

        @block.scalar
        def _(s):
            for f in SE.ops:
                f(s)

        @block.tensor
        def _(t):
            for f in TE.ops:
                f(t)

        @block.sync
        def _(s):
            for f in SY.ops:
                f(s)

    return nc


def _prep_consts(T):
    NT = T // 128
    ident = np.eye(128, dtype=np.float32)
    tri = (np.arange(128)[None, :] >= np.arange(128)[:, None]).astype(np.float32)
    pos = np.arange(T, dtype=np.float32)
    pair = np.arange(32, dtype=np.float32)
    rates = (np.float32(10000.0) ** (-(2.0 * pair) / np.float32(64.0))).astype(np.float32)
    ang = (pos[:, None] * rates[None, :]).astype(np.float32)
    cos = np.cos(ang).astype(np.float32)
    sin = np.sin(ang).astype(np.float32)
    cos_t = np.zeros((128, 32 * NT), dtype=np.float32)
    sin_t = np.zeros((128, 32 * NT), dtype=np.float32)
    for i in range(NT):
        cos_t[:, 32 * i:32 * (i + 1)] = cos[128 * i:128 * (i + 1)]
        sin_t[:, 32 * i:32 * (i + 1)] = sin[128 * i:128 * (i + 1)]
    ones = np.ones((128, 1), dtype=np.float32)
    onesf = np.ones((128, 128), dtype=np.float32)
    csh = np.zeros((128, 2), dtype=np.float32)
    csh[:, 0] = -C_SHIFT_A
    csh[:, 1] = -C_SHIFT_B
    return ident, tri, cos_t, sin_t, ones, csh, onesf


def make_in_map(xc, w, T):
    ident, tri, cos_t, sin_t, ones, csh, onesf = _prep_consts(T)
    asc = lambda a: np.ascontiguousarray(a, dtype=np.float32)
    return {
        "x_in": asc(xc),
        "wqkvT": asc(w["L0_attn_a_qkv_w"].T),
        "woT": asc(w["L0_attn_a_out_w"].T),
        "qkvb_t": asc(w["L0_attn_a_qkv_b"].reshape(24, 128).T),
        "vb_row": asc(w["L0_attn_a_qkv_b"][2048:3072].reshape(1, 1024)),
        "ob_row": asc(w["L0_attn_a_out_b"].reshape(1, 1024)),
        "bwqT": asc(w["L0_attn_b_Wq"].T),
        "bwkT": asc(w["L0_attn_b_Wk"].T),
        "bwvT": asc(w["L0_attn_b_Wv"].T),
        "bwoT": asc(w["L0_attn_b_Wo"].T),
        "fwqT": asc(w["L0_ffn_Wq"].T),
        "routT": asc(w["L0_ffn_routers"].T),
        "lv_w": asc(w["L0_ffn_leaf_values"]),
        "fwoutT": asc(w["L0_ffn_Wout_w"].T),
        "fb_row": asc(w["L0_ffn_Wout_b"].reshape(1, 1024)),
        "ident_c": ident, "tri_c": tri, "cos_c": cos_t, "sin_c": sin_t,
        "ones_c": ones, "csh_c": csh, "onesf_c": onesf, "onesb_c": onesf[:, 0:64].copy(),
    }


def kernel(**inputs):
    x = np.asarray(inputs["L0_attn_a_x"], dtype=np.float32)
    B, T, _ = x.shape
    w = {k: np.asarray(v, dtype=np.float32) for k, v in inputs.items()
         if k != "L0_attn_a_x"}
    nc = build_nc(T)
    in_maps = [make_in_map(x[i], w, T) for i in range(B)]
    res = run_bass_kernel_spmd(nc, in_maps, core_ids=list(range(B)))
    out = np.stack([np.asarray(res.results[i]["y_out"]) for i in range(B)], axis=0)
    return out.astype(np.float32)


# version 29
# speedup vs baseline: 1.0188x; 1.0188x over previous
import contextlib
import numpy as np

import concourse.bass as bass
import concourse.mybir as mybir
from concourse.bass_utils import run_bass_kernel_spmd

F32 = mybir.dt.float32
BF16 = mybir.dt.bfloat16
AF = mybir.ActivationFunctionType
ALU = mybir.AluOpType

D = 1024
NCH = 8
NPAIR = 8
C_SHIFT_A = 12.0
C_SHIFT_B = 26.0

AP = bass.AP
DBG = 0


class Sem:
    def __init__(self, h):
        self.h = h
        self.n = 0

    def nxt(self, k=1):
        self.n += k
        return self.n


class Eng:

    def __init__(self, name):
        self.name = name
        self.ops = []

    def do(self, fn, inc=None, k=1):
        if inc is None:
            self.ops.append(fn)
            return None
        tok = inc.nxt(k)
        self.ops.append(lambda e, fn=fn, sem=inc.h, k=k: fn(e).then_inc(sem, k))
        return tok

    def mark(self, sem):
        tok = sem.nxt(1)
        prev = self.ops[-1]

        def wrapped(e, p=prev, s=sem.h):
            r = p(e)
            assert r is not None, "mark() must follow an instruction-emitting op"
            return r.then_inc(s, 1)

        self.ops[-1] = wrapped
        return tok

    def wait(self, sem, tok):
        if tok is None:
            return
        self.ops.append(lambda e, s=sem.h, t=tok: e.wait_ge(s, t))

    def raw(self, fn):
        self.ops.append(fn)


class Psum:
    def __init__(self, t):
        self.t = t
        self.wtok = None
        self.rtoks = []


def build_nc(T, n_sub=4, span_ovr=None):
    NT = T // 128
    SPAN = span_ovr or min(1024, T)
    NSP = T // SPAN
    NKT = T // 128
    SP512 = min(512, T)
    NSP512 = T // SP512
    PTW = max(4 * SPAN, 4096)

    nc = bass.Bass("TRN2", target_bir_lowering=False, detect_race_conditions=False)

    dram = {}

    def dr(name, shape, kind="ExternalInput"):
        dram[name] = nc.dram_tensor(name, shape, F32, kind=kind)
        return dram[name]

    d_x = dr("x_in", [T, D])
    d_y = dr("y_out", [T, D], "ExternalOutput")
    d_wqkvT = dr("wqkvT", [D, 3 * D])
    d_woT = dr("woT", [D, D])
    d_qkvb = dr("qkvb_t", [128, 24])
    d_vbrow = dr("vb_row", [1, D])
    d_obrow = dr("ob_row", [1, D])
    d_bwqT = dr("bwqT", [D, D])
    d_bwkT = dr("bwkT", [D, D])
    d_bwvT = dr("bwvT", [D, D])
    d_bwoT = dr("bwoT", [D, D])
    d_fwqT = dr("fwqT", [D, D])
    d_routT = dr("routT", [D, 255])
    d_lv = dr("lv_w", [256, D])
    d_fwoutT = dr("fwoutT", [D, D])
    d_fbrow = dr("fb_row", [1, D])
    d_ident = dr("ident_c", [128, 128])
    d_tri = dr("tri_c", [128, 128])
    d_cos = dr("cos_c", [128, 32 * NT])
    d_sin = dr("sin_c", [128, 32 * NT])
    d_ones = dr("ones_c", [128, 1])
    d_onesf = dr("onesf_c", [128, 128])
    d_onesb = dr("onesb_c", [128, 64])
    d_csh = dr("csh_c", [128, 2])

    es = contextlib.ExitStack()
    sb = lambda name, shape, dt: es.enter_context(nc.sbuf_tensor(name, shape, dt))
    sem = lambda name: Sem(es.enter_context(nc.semaphore(name)))

    X = sb("X", [128, NT * 1024], F32)
    SHRD = sb("SHRD", [128, 2048], F32)
    BIAS = sb("BIAS", [128, 1024], BF16)
    RB = sb("RB", [128, SPAN], F32)
    ROWS = sb("ROWS", [128, 1024], F32)
    SML = sb("SML", [128, 128], F32)
    STAT = sb("STAT", [128, 12 * 16], F32)
    QKVB = sb("QKVB", [128, 24], F32)
    XLN = sb("XLN", [128, NCH * T], BF16)
    QS = sb("QS", [128, NPAIR * T], BF16)
    KS = sb("KS", [128, 4 * T], BF16)
    VB = sb("VB", [128, NKT * 520], BF16)
    PT = sb("PT", [128, PTW], BF16)
    WA = sb("WA", [128, 8192], BF16)
    IDNB = sb("IDNB", [128, 128], BF16)
    TRI = sb("TRI", [128, 128], BF16)
    ONES = sb("ONES", [128, 1], BF16)
    ONESF = sb("ONESF", [128, 128], F32)
    ONESB = sb("ONESB", [128, 64], BF16)
    CSH = sb("CSH", [128, 2], F32)
    STG = sb("STG", [128, SPAN], BF16)

    class TV:

        def __init__(self, t, off):
            self._t = t
            self._o = off
            self.handle = t
            self.off = off

        def __getitem__(self, idx):
            r, c = idx
            return self._t[r, c.start + self._o:c.stop + self._o]

    PAB = es.enter_context(nc.psum_tensor("PAB", [128, 2048], F32))
    P = [Psum(TV(PAB, 0)), Psum(TV(PAB, 1024)),
         Psum(es.enter_context(nc.psum_tensor("P2", [128, 1024], F32))),
         Psum(es.enter_context(nc.psum_tensor("P3", [128, 1024], F32)))]

    tT, aA, vV, gG = sem("tT"), sem("aA"), sem("vV"), sem("gG")
    dX, dC, dO, dF = sem("dX"), sem("dC"), sem("dO"), sem("dF")
    WSL = [sem(f"ws{i}") for i in range(8)]

    TE, VE, SE, GE, SY = Eng("t"), Eng("v"), Eng("s"), Eng("g"), Eng("y")
    MARK = {id(TE): tT, id(VE): vV, id(SE): aA, id(GE): gG}

    def p_write_ready(eng, p):
        for s, t in p.rtoks:
            eng.wait(s, t)
        p.rtoks = []

    def p_read_ready(eng, p):
        if p.wtok is not None:
            eng.wait(*p.wtok)

    def p_mark_w(p, tok):
        p.wtok = (tT, tok)

    def p_mark_r(p, eng, tok):
        p.rtoks.append((MARK[id(eng)], tok))

    def dma_w(slot_sem, dst_ap, src_ap, after_tok=None):
        if after_tok is not None:
            GE.wait(tT, after_tok)
        tok = slot_sem.nxt(16)
        GE.ops.append(lambda e, d=dst_ap, s=src_ap, sm=slot_sem.h:
                      e.dma_start(out=d, in_=s).then_inc(sm, 16))
        return tok


    def dma_c(dst, src):
        GE.ops.append(lambda e, d=dst, s=src: e.dma_start(out=d, in_=s).then_inc(dC.h, 16))
        dC.n += 16

    dma_c(IDNB[:, :], d_ident.ap()[:, :])
    dma_c(TRI[:, :], d_tri.ap()[:, :])
    dma_c(ONES[:, :], d_ones.ap()[:, :])
    dma_c(QKVB[:, :], d_qkvb.ap()[:, :])
    dma_c(CSH[:, :], d_csh.ap()[:, :])
    dma_c(ONESF[:, :], d_onesf.ap()[:, :])
    dma_c(ONESB[:, :], d_onesb.ap()[:, :])
    tok_consts = dC.n

    for _kt in range(NKT):
        VE.ops.append(lambda e, kt=_kt: e.memset(
            AP(VB, 520 * kt + 64, [[NKT * 520, 128], [130, 4], [65, 2]]), 1.0))
    NXC = min(4, NT)
    TPC = NT // NXC
    x_chunk_tok = []
    for c_ in range(NXC):
        SY.ops.append(lambda e, c_=c_: e.dma_start(
            out=X[:, 1024 * TPC * c_:1024 * TPC * (c_ + 1)],
            in_=AP(d_x, 128 * TPC * c_ * D, [[D, 128], [128 * D, TPC], [1, D]])).then_inc(dX.h, 16))
        dX.n += 16
        x_chunk_tok.append(dX.n)
    tok_x = dX.n

    ln_evs = [None]

    def emit_ln(prev_x_tok=None, tile_toks=None):
        if DBG & 16:
            VE.wait(dX, tok_x)
            mtk = VE.do(lambda e: e.memset(SML[:, 0:1], 0.0), inc=vV)
            ln_evs[0] = [mtk] * NT
            return mtk
        first = prev_x_tok is None and tile_toks is None
        if not first:
            VE.wait(dX, tok_x)
        if tile_toks is None and prev_x_tok is not None:
            VE.wait(vV, prev_x_tok)
        VE.raw(lambda e: e.drain())
        for i in range(NT):
            xt = X[:, 1024 * i:1024 * (i + 1)]
            if tile_toks is not None:
                VE.wait(vV, tile_toks[i])
            if first:
                VE.wait(dX, x_chunk_tok[i // TPC])
            VE.do(lambda e, xt=xt, i=i: e.bn_stats(STAT[:, 12 * i:12 * i + 6], xt[:, 0:512]))
            VE.do(lambda e, xt=xt, i=i: e.bn_stats(STAT[:, 12 * i + 6:12 * i + 12], xt[:, 512:1024]))
        VE.raw(lambda e: e.drain())
        for i in range(NT):
            VE.do(lambda e, i=i: e.bn_aggr(SML[:, 80 + 2 * i:80 + 2 * i + 2], STAT[:, 12 * i:12 * i + 12]))
        VE.raw(lambda e: e.drain())
        VE.do(lambda e: e.tensor_scalar(SML[:, 0:NT], AP(SML, 81, [[128, 128], [2, NT]]),
                                        1e-5, None, ALU.add))
        vtok = VE.mark(vV)
        SE.wait(vV, vtok)
        stok = SE.do(lambda e: e.activation(out=SML[:, 16:16 + NT], in_=SML[:, 0:NT], func=AF.Sqrt), inc=aA)
        VE.wait(aA, stok)
        VE.raw(lambda e: e.drain())
        VE.do(lambda e: e.reciprocal(SML[:, 32:32 + NT], SML[:, 16:16 + NT]))
        VE.raw(lambda e: e.drain())
        for _ in range(2):
            VE.do(lambda e: e.tensor_tensor(SML[:, 48:48 + NT], SML[:, 32:32 + NT], SML[:, 32:32 + NT], ALU.mult))
            VE.raw(lambda e: e.drain())
            VE.do(lambda e: e.tensor_tensor(SML[:, 48:48 + NT], SML[:, 48:48 + NT], SML[:, 0:NT], ALU.mult))
            VE.raw(lambda e: e.drain())
            VE.do(lambda e: e.tensor_scalar(SML[:, 48:48 + NT], SML[:, 48:48 + NT], -0.5, 1.5, ALU.mult, ALU.add))
            VE.raw(lambda e: e.drain())
            VE.do(lambda e: e.tensor_tensor(SML[:, 32:32 + NT], SML[:, 32:32 + NT], SML[:, 48:48 + NT], ALU.mult))
            VE.raw(lambda e: e.drain())
        VE.do(lambda e: e.tensor_tensor(SML[:, 64:64 + NT], AP(SML, 80, [[128, 128], [2, NT]]),
                                        SML[:, 32:32 + NT], ALU.mult))
        VE.raw(lambda e: e.drain())
        VE.do(lambda e: e.tensor_scalar(SML[:, 64:64 + NT], SML[:, 64:64 + NT], -1.0, None, ALU.mult))
        ntok = VE.mark(vV)
        tr_tok = [None, None]
        last_ev = None
        ev_sems = []
        SE.wait(vV, ntok)
        for i in range(NT):
            xt = X[:, 1024 * i:1024 * (i + 1)]
            half = 1024 * (i % 2)
            xo = PT[:, half:half + 1024]
            if tr_tok[i % 2] is not None:
                SE.wait(tT, tr_tok[i % 2])
            atok = SE.do(lambda e, xt=xt, xo=xo, i=i: e.activation(
                out=xo, in_=xt, func=AF.Identity, bias=SML[:, 64 + i:65 + i],
                scale=SML[:, 32 + i:33 + i]), inc=aA)
            pt = P[i % 2]
            TE.wait(aA, atok)
            p_write_ready(TE, pt)
            for c in range(8):
                TE.do(lambda e, xo=xo, c=c, pt=pt: e.matmul(
                    pt.t[:, 128 * c:128 * (c + 1)], xo[:, 128 * c:128 * (c + 1)],
                    IDNB[:, :], start=True, stop=True))
            ttok = TE.mark(tT)
            p_mark_w(pt, ttok)
            tr_tok[i % 2] = ttok
            dst = AP(XLN, 128 * i, [[NCH * T, 128], [T, 8], [1, 128]])
            VE.wait(tT, ttok)
            etok = VE.do(lambda e, pt=pt, dst=dst: e.tensor_copy(dst, pt.t[:, 0:1024]), inc=vV)
            p_mark_r(pt, VE, etok)
            ev_sems.append(etok)
            last_ev = etok
        ln_evs[0] = ev_sems
        return last_ev

    rows_free = [None]
    rb_free = [None]
    stg_free = [0]
    rowA_free = [None]
    rowB_free = [None]

    def emit_bias_bcast(src_row, reader_toks):
        if DBG & 32:
            return None
        s = WSL[6]
        if rows_free[0] is not None:
            GE.wait(tT, rows_free[0])
        tok = s.nxt(16)
        GE.ops.append(lambda e, sr=src_row: e.dma_start(
            out=ROWS[0:1, :], in_=sr).then_inc(s.h, 16))
        pp = P[0]
        TE.wait(WSL[6], tok)
        p_write_ready(TE, pp)
        for f in range(2):
            btok = TE.do(lambda e, f=f: e.matmul(pp.t[:, 512 * f:512 * (f + 1)],
                                                 ONESF[0:1, 0:128],
                                                 ROWS[0:1, 512 * f:512 * (f + 1)],
                                                 start=True, stop=True), inc=tT)
        p_mark_w(pp, btok)
        rows_free[0] = btok
        SE.wait(tT, btok)
        for st, tk in reader_toks:
            SE.wait(st, tk)
        atok = SE.do(lambda e: e.activation(out=BIAS[:, :], in_=pp.t[:, 0:1024], func=AF.Identity), inc=aA)
        p_mark_r(pp, SE, atok)
        return atok

    def emit_pair_proj(wsem, wcol, dst_sl, bias_col, wtok, pps, extra_wait=None,
                       span_evs=None):
        if DBG & 8:
            return None
        TE.wait(wsem, wtok)
        if extra_wait is not None:
            TE.wait(*extra_wait)
        last_atok = None
        tps = SP512 // 128
        for sp in range(NSP512):
            pp = pps[sp % len(pps)]
            half = SP512 * ((sp // len(pps)) % 2) if SP512 == 512 else 0
            if span_evs is not None:
                TE.wait(vV, span_evs[min(tps * (sp + 1) - 1, NT - 1)])
            p_write_ready(TE, pp)
            for ch in range(8):
                lhsT = WA[:, wcol + 128 * ch: wcol + 128 * (ch + 1)]
                rhs = XLN[:, ch * T + SP512 * sp: ch * T + SP512 * (sp + 1)]
                TE.do(lambda e, o=pp.t[:, half:half + SP512], l=lhsT, r=rhs,
                      s=(ch == 0), p2=(ch == 7): e.matmul(o, l, r, start=s, stop=p2))
            mtok = TE.mark(tT)
            p_mark_w(pp, mtok)
            SE.wait(tT, mtok)
            out = dst_sl[:, SP512 * sp:SP512 * (sp + 1)]
            if bias_col is not None:
                atok = SE.do(lambda e, o=out, i=pp.t[:, half:half + SP512],
                             b=QKVB[:, bias_col:bias_col + 1]:
                             e.activation(out=o, in_=i, func=AF.Identity, bias=b), inc=aA)
            else:
                atok = SE.do(lambda e, o=out, i=pp.t[:, half:half + SP512]:
                             e.activation(out=o, in_=i, func=AF.Identity), inc=aA)
            p_mark_r(pp, SE, atok)
            last_atok = atok
        return last_atok

    def emit_v_half(wcol, bias_half, wsem, wtok, bias_tok):
        if DBG & 8:
            return None
        TE.wait(wsem, wtok)
        last = None
        for kt in range(NKT):
            pp = P[kt % 2]
            p_write_ready(TE, pp)
            for ch in range(8):
                lhsT = XLN[:, ch * T + 128 * kt: ch * T + 128 * (kt + 1)]
                rhs = WA[:, wcol + 512 * ch: wcol + 512 * (ch + 1)]
                TE.do(lambda e, o=pp.t[:, 0:512], l=lhsT, r=rhs, s=(ch == 0), p2=(ch == 7):
                      e.matmul(o, l, r, start=s, stop=p2))
            mtok = TE.mark(tT)
            p_mark_w(pp, mtok)
            VE.wait(tT, mtok)
            if bias_tok is not None and kt == 0:
                VE.wait(aA, bias_tok)
            dst = AP(VB, 520 * kt, [[NKT * 520, 128], [130, 4], [65, 2], [1, 64]])
            srcp = AP(pp.t.handle, pp.t.off, [[2048, 128], [128, 4], [64, 2], [1, 64]])
            if bias_half is not None:
                bsrc = AP(BIAS, 512 * bias_half, [[1024, 128], [128, 4], [64, 2], [1, 64]])
                vtok = VE.do(lambda e, d=dst, i=srcp, b=bsrc:
                             e.tensor_tensor(d, i, b, ALU.add), inc=vV)
            else:
                vtok = VE.do(lambda e, d=dst, i=srcp: e.tensor_copy(d, i), inc=vV)
            p_mark_r(pp, VE, vtok)
            last = vtok
        return last

    def emit_attention_pair(c, causal, cshift, qk_tok, v_tok):
        if DBG & 4:
            return v_tok
        qsl = QS[:, c * T:(c + 1) * T]
        koff = ((c % 4) if causal else (c % 2)) * T
        ksl = KS[:, koff:koff + T]
        vbase = 130 * (c % 4)
        S_A, S_B, PV_A, PV_B = P[0], P[1], P[2], P[3]
        TE.wait(vV, v_tok)
        if qk_tok is not None:
            TE.wait(aA, qk_tok)
        last_ev = None
        for sp in range(NSP):
            kts = [kt for kt in range(NKT)
                   if not (causal and 128 * kt > SPAN * sp + SPAN - 1)]
            def _off(kt):
                return max(0, 128 * kt - SPAN * sp) if causal else 0
            last_kt_bank = {}
            for bi in range((SPAN + 511) // 512):
                contrib = [kt for kt in kts if _off(kt) < 512 * bi + 512]
                last_kt_bank[bi] = contrib[-1]
            def emit_pv(hh, kt, ki, segs, ptok, slot):
                pv = PV_A if hh == 0 else PV_B
                TE.wait(*ptok)
                if ki == 0 and hh == 0:
                    p_write_ready(TE, PV_A)
                    p_write_ready(TE, PV_B)
                vsl = VB[:, 520 * kt + vbase + 65 * hh:520 * kt + vbase + 65 * hh + 65]
                st = (ki == 0)
                for (a, b) in segs:
                    prt = PT[:, slot + SPAN * hh + a:slot + SPAN * hh + b]
                    fin = (kt == last_kt_bank[a // 512])
                    TE.do(lambda e, o_=pv.t[0:65, a:b], l=vsl, r=prt, s=st, f=fin:
                          e.matmul(o_, l, r, start=s, stop=f))
                pvtok = TE.mark(tT)
                p_mark_w(pv, pvtok)

            pending = None
            for ki, kt in enumerate(kts):
                o = max(0, 128 * kt - SPAN * sp) if causal else 0
                segs = []
                n0 = o
                while n0 < SPAN:
                    n1 = min((n0 // 512 + 1) * 512, SPAN)
                    segs.append((n0, n1))
                    n0 = n1
                slot = 2 * SPAN * (kt % 2)
                mtoks = [None, None]
                for hh in range(2):
                    pp = S_A if hh == 0 else S_B
                    rowlo = 64 * hh
                    p_write_ready(TE, pp)
                    lhsT = ksl[rowlo:rowlo + 64, 128 * kt:128 * (kt + 1)]
                    for (a, b) in segs:
                        rhs = qsl[rowlo:rowlo + 64, SPAN * sp + a:SPAN * sp + b]
                        TE.do(lambda e, o_=pp.t[0:128, a:b], l=lhsT, r=rhs, tp=(rowlo, 0):
                              e.matmul(o_, l, r, start=True, stop=True, tile_position=tp))
                    mtoks[hh] = TE.mark(tT)
                    p_mark_w(pp, mtoks[hh])
                ptoks = [None, None]
                if False:
                    SE.wait(tT, mtoks[1])
                    etok = SE.do(lambda e, o_=PT[:, slot:slot + 2 * SPAN],
                                 i=PAB[:, 0:2 * SPAN], cs=cshift:
                                 e.activation(out=o_, in_=i, func=AF.Exp,
                                              bias=CSH[:, cs:cs + 1], scale=0.125),
                                 inc=aA)
                    p_mark_r(S_A, SE, etok)
                    p_mark_r(S_B, SE, etok)
                    etoks = [etok, etok]
                else:
                    etoks = [None, None]
                    for hh in range(2):
                        pp = S_A if hh == 0 else S_B
                        SE.wait(tT, mtoks[hh])
                        etoks[hh] = SE.do(lambda e, o_=PT[:, slot + SPAN * hh + o:slot + SPAN * (hh + 1)],
                                          i=pp.t[:, o:SPAN], cs=cshift:
                                          e.activation(out=o_, in_=i, func=AF.Exp,
                                                       bias=CSH[:, cs:cs + 1], scale=0.125),
                                          inc=aA)
                        p_mark_r(pp, SE, etoks[hh])
                for hh in range(2):
                    if causal and 128 * kt >= SPAN * sp:
                        VE.wait(aA, etoks[hh])
                        mtok2 = VE.do(lambda e, o_=PT[:, slot + SPAN * hh + o:slot + SPAN * hh + o + 128]:
                                      e.tensor_tensor(o_, o_, TRI[:, :], ALU.mult), inc=vV)
                        ptoks[hh] = (vV, mtok2)
                    else:
                        ptoks[hh] = (aA, etoks[hh])
                if pending is not None:
                    emit_pv(0, pending[0], pending[1], pending[2], pending[3], pending[5])
                    emit_pv(1, pending[0], pending[1], pending[2], pending[4], pending[5])
                pending = (kt, ki, segs, ptoks[0], ptoks[1], slot)
            emit_pv(0, pending[0], pending[1], pending[2], pending[3], pending[5])
            emit_pv(1, pending[0], pending[1], pending[2], pending[4], pending[5])
            if DBG & 1:
                last_ev = vV.n if vV.n > 0 else None
                continue
            spc0, spc1 = SPAN * sp, SPAN * (sp + 1)
            SE.wait(*PV_A.wtok)
            SE.wait(tT, rowA_free[0])
            SE.do(lambda e: e.activation(out=ROWS[64:65, 0:SPAN], in_=PV_A.t[64:65, 0:SPAN], func=AF.Ln))
            SE.wait(*PV_B.wtok)
            SE.wait(tT, rowB_free[0])
            SE.do(lambda e: e.activation(out=RB[64:65, 0:SPAN], in_=PV_B.t[64:65, 0:SPAN], func=AF.Ln))
            SE.raw(lambda e: e.drain())
            r1A = SE.do(lambda e: e.activation(out=ROWS[64:65, 0:SPAN], in_=ROWS[64:65, 0:SPAN], func=AF.Exp, scale=-1.0), inc=aA)
            p_mark_r(PV_A, SE, r1A)
            r1B = SE.do(lambda e: e.activation(out=RB[64:65, 0:SPAN], in_=RB[64:65, 0:SPAN], func=AF.Exp, scale=-1.0), inc=aA)
            p_mark_r(PV_B, SE, r1B)
            for hh, (pv, bp, r1, srow) in enumerate(((PV_A, S_A, r1A, ROWS), (PV_B, S_B, r1B, RB))):
                TE.wait(aA, r1)
                p_write_ready(TE, bp)
                for (a, b) in ((x, min(x + 512, SPAN)) for x in range(0, SPAN, 512)):
                    btok = TE.do(lambda e, bp=bp, a=a, b=b, srow=srow: e.matmul(
                        bp.t[0:64, a:b], ONESF[64:65, 0:64], srow[64:65, a:b],
                        start=True, stop=True, tile_position=(64, 0)), inc=tT)
                p_mark_w(bp, btok)
                if hh == 0:
                    rowA_free[0] = btok
                else:
                    rowB_free[0] = btok
                SE.wait(tT, btok)
                if hh == 1:
                    SE.wait(vV, e1)
                else:
                    SE.wait(vV, rb_free[0])
                ctok = SE.do(lambda e, bp=bp: e.activation(out=RB[0:64, 0:SPAN], in_=bp.t[0:64, 0:SPAN], func=AF.Identity), inc=aA)
                p_mark_r(bp, SE, ctok)
                VE.wait(aA, ctok)
                VE.wait(*pv.wtok)
                if hh == 0:
                    e1 = VE.do(lambda e, d=qsl[0:64, spc0:spc1], pv=pv:
                               e.tensor_tensor(d, pv.t[0:64, 0:SPAN], RB[0:64, 0:SPAN], ALU.mult), inc=vV)
                    p_mark_r(pv, VE, e1)
                else:
                    VE.wait(dF, stg_free[0])
                    e2 = VE.do(lambda e, pv=pv:
                               e.tensor_tensor(STG[0:64, 0:SPAN], pv.t[0:64, 0:SPAN], RB[0:64, 0:SPAN], ALU.mult), inc=vV)
                    p_mark_r(pv, VE, e2)
                    rb_free[0] = e2
                    GE.wait(vV, e2)
                    dF.n += 16
                    stg_free[0] = dF.n
                    GE.ops.append(lambda e, d=qsl[64:128, spc0:spc1]:
                                  e.dma_start(out=d, in_=STG[0:64, 0:SPAN]).then_inc(dF.h, 16))
            last_ev = e2
        return last_ev

    def emit_out_proj(src_buf, w_dram, wslots, bias_tok, ready_toks, store=False):
        last_vtok = None
        pass_mm_tok = [None, None]
        p1_toks = [None] * NT
        if DBG & 2:
            return (vV.n if vV.n else None), [None] * NT, pass_mm_tok
        for p_ in range(2):
            wsem, wcol, afree = wslots[p_]
            src_ap = AP(w_dram, 512 * p_ * D, [[D, 128], [128 * D, 4], [1, D]])
            wtok = dma_w(wsem, WA[:, wcol:wcol + 4096], src_ap, after_tok=afree)
            TE.wait(wsem, wtok)
            if p_ == 0:
                TE.wait(dF, dF.n)
                for s, t in ready_toks:
                    TE.wait(s, t)
            for i in range(NT):
                pp = P[i % 2] if p_ == 0 else P[2 + i % 2]
                p_write_ready(TE, pp)
                for ch4 in range(4):
                    ch = 4 * p_ + ch4
                    lhsT = src_buf[:, ch * T + 128 * i: ch * T + 128 * (i + 1)]
                    for f in range(2):
                        rhs = WA[:, wcol + 1024 * ch4 + 512 * f: wcol + 1024 * ch4 + 512 * (f + 1)]
                        TE.do(lambda e, o=pp.t[:, 512 * f:512 * (f + 1)], l=lhsT, r=rhs,
                              s=(ch4 == 0), p2=(ch4 == 3): e.matmul(o, l, r, start=s, stop=p2))
                mtok = TE.mark(tT)
                p_mark_w(pp, mtok)
                pass_mm_tok[p_] = mtok
                VE.wait(tT, mtok)
                xsl = X[:, 1024 * i:1024 * (i + 1)]
                vt = VE.do(lambda e, x=xsl, i_=pp.t[:, 0:1024]:
                           e.tensor_tensor(x, x, i_, ALU.add), inc=vV)
                p_mark_r(pp, VE, vt)
                last_vtok = vt
                if p_ == 1:
                    p1_toks[i] = vt
            if p_ == 0 and bias_tok is not None:
                VE.wait(aA, bias_tok)
                VE.raw(lambda e: e.drain())
                for i in range(NT):
                    xsl = X[:, 1024 * i:1024 * (i + 1)]
                    VE.do(lambda e, x=xsl: e.tensor_tensor(x, x, BIAS[:, :], ALU.add))
        tile_toks = p1_toks
        if store:
            for i in range(NT):
                SY.wait(vV, tile_toks[i])
                dO.n += 16
                SY.ops.append(lambda e, i=i: e.dma_start(
                    out=AP(d_y, 128 * i * D, [[D, 128], [1, D]]),
                    in_=X[:, 1024 * i:1024 * (i + 1)]).then_inc(dO.h, 16))
        return last_vtok, tile_toks, pass_mm_tok

    GE.wait(dC, tok_consts)
    TE.wait(dC, tok_consts)
    VE.wait(dC, tok_consts)
    SE.wait(dC, tok_consts)
    ln_tok = emit_ln()
    vb_tok = emit_bias_bcast(d_vbrow.ap()[:, :], [])

    pair_free = [None, None]
    v_free = None
    v_tok = None
    last_attn = None
    for c in range(NPAIR):
        wsem = WSL[c % 2]
        wcol = 2048 * (c % 2)
        qtok = dma_w(wsem, WA[:, wcol:wcol + 1024],
                     AP(d_wqkvT, 128 * c, [[3 * D, 128], [3 * D * 128, 8], [1, 128]]),
                     after_tok=pair_free[c % 2])
        ktok = dma_w(wsem, WA[:, wcol + 1024:wcol + 2048],
                     AP(d_wqkvT, D + 128 * c, [[3 * D, 128], [3 * D * 128, 8], [1, 128]]))
        emit_pair_proj(wsem, wcol, QS[:, c * T:(c + 1) * T], c, ktok, [P[2], P[3]],
                       span_evs=ln_evs[0] if c == 0 else None)
        ak = emit_pair_proj(wsem, wcol + 1024, KS[:, (c % 2) * T:(c % 2) * T + T],
                            8 + c, ktok, [P[2], P[3]])
        pair_free[c % 2] = tT.n
        if c in (0, 4):
            vwtok = dma_w(WSL[2], WA[:, 4096:8192],
                          AP(d_wqkvT, 2 * D + 512 * (c // 4),
                             [[3 * D, 128], [3 * D * 128, 8], [1, 512]]),
                          after_tok=v_free)
            v_tok = emit_v_half(4096, c // 4, WSL[2], vwtok, vb_tok if c == 0 else None)
            v_free = tT.n
        last_attn = emit_attention_pair(c, False, 0, ak, v_tok)

    ob_tok = emit_bias_bcast(d_obrow.ap()[:, :], [(vV, v_tok)])
    wo_slots = [(WSL[4], 0, tT.n), (WSL[5], 4096, tT.n)]
    wo_tok, wo_tts, wo_pmt = emit_out_proj(QS, d_woT, wo_slots, ob_tok, [(vV, last_attn)])

    def emit_ffn(prev_tok, bias_reader_tok, store, tile_toks=None, prev_pmt=None):
        ln_t = emit_ln(prev_x_tok=prev_tok, tile_toks=tile_toks)
        wa_free = tT.n
        wa_p0 = prev_pmt[0] if prev_pmt and prev_pmt[0] else wa_free
        wa_p1 = prev_pmt[1] if prev_pmt and prev_pmt[1] else wa_free
        rtok = dma_w(WSL[2], WA[:, 2048:2048 + 2040],
                     AP(d_routT, 0, [[255, 128], [255 * 128, 8], [1, 255]]),
                     after_tok=wa_p0)
        lvtok = dma_w(WSL[3], WA[:, 4096:6144],
                      AP(d_lv, 0, [[D, 128], [D * 128, 2], [1, D]]),
                      after_tok=wa_p1)
        fb_tok = emit_bias_bcast(d_fbrow.ap()[:, :], [(vV, bias_reader_tok)])
        qf_tok = None
        slot_free = [wa_p0, wa_p0]
        for et in range(8):
            wsem = WSL[et % 2]
            wcol = 1024 * (et % 2)
            wtok = dma_w(wsem, WA[:, wcol:wcol + 1024],
                         AP(d_fwqT, 128 * et, [[D, 128], [D * 128, 8], [1, 128]]),
                         after_tok=slot_free[et % 2])
            qf_tok = emit_pair_proj(wsem, wcol, QS[:, et * T:(et + 1) * T], None,
                                    wtok, [P[0], P[1]],
                                    span_evs=ln_evs[0] if et == 0 else None)
            slot_free[et % 2] = tT.n
        TE.wait(WSL[2], rtok)
        TE.wait(WSL[3], lvtok)

        NSPW = T // SP512
        TPS = SP512 // 128
        lp_read = {}
        tr_toks = {}
        sig_tok = {}
        ret_done = {}

        def stage_a(i):
            pz = P[2 + i % 2]
            p_write_ready(TE, pz)
            if i == 0:
                TE.wait(aA, qf_tok)
            for ch in range(8):
                lhsT = QS[:, ch * T + 128 * i: ch * T + 128 * (i + 1)]
                rhs = WA[:, 2048 + 255 * ch: 2048 + 255 * (ch + 1)]
                TE.do(lambda e, o=pz.t[:, 0:255], l=lhsT, r=rhs, s_=(ch == 0), p2=(ch == 7):
                      e.matmul(o, l, r, start=s_, stop=p2))
            ztok = TE.mark(tT)
            p_mark_w(pz, ztok)
            g2o = 512 * (i % 2)
            SE.wait(tT, ztok)
            if i >= 2 and (i - 2) in lp_read:
                SE.wait(vV, lp_read[i - 2])
            SE.do(lambda e, o=SHRD[:, g2o:g2o + 255], i_=pz.t[:, 0:255]:
                  e.activation(out=o, in_=i_, func=AF.Sigmoid, scale=-1.0))
            gtok = SE.do(lambda e, o=SHRD[:, g2o + 255:g2o + 510], i_=pz.t[:, 0:255]:
                         e.activation(out=o, in_=i_, func=AF.Sigmoid), inc=aA)
            p_mark_r(pz, SE, gtok)
            sig_tok[i] = gtok

        def stage_b(i):
            g2o = 512 * (i % 2)
            EN = GE if i % 2 == 0 else VE
            EN.wait(aA, sig_tok[i])
            if i >= 2 and (i - 2) in tr_toks:
                EN.wait(tT, tr_toks[i - 2])
            lpA = 1024 + 512 * (i % 2)
            lpB = lpA + 256
            ptf = 256 * (i % 2)
            EN.do(lambda e, g2o=g2o, lpA=lpA: e.tensor_copy(
                AP(SHRD, lpA, [[2048, 128], [1, 2]]),
                AP(SHRD, g2o, [[2048, 128], [255, 2]])))
            if EN is VE:
                EN.raw(lambda e: e.drain())
            cur, oth = lpA, lpB
            for lv_ in range(1, 8):
                n = 1 << lv_
                last = lv_ == 7
                o_ev = AP(PT, ptf, [[PTW, 128], [2, n]]) if last else AP(SHRD, oth, [[2048, 128], [2, n]])
                o_od = AP(PT, ptf + 1, [[PTW, 128], [2, n]]) if last else AP(SHRD, oth + 1, [[2048, 128], [2, n]])
                EN.do(lambda e, cur=cur, o_=o_ev, n=n, g2o=g2o: e.tensor_tensor(
                    o_, AP(SHRD, cur, [[2048, 128], [1, n]]),
                    AP(SHRD, g2o + n - 1, [[2048, 128], [1, n]]), ALU.mult))
                EN.do(lambda e, cur=cur, o_=o_od, n=n, g2o=g2o: e.tensor_tensor(
                    o_, AP(SHRD, cur, [[2048, 128], [1, n]]),
                    AP(SHRD, g2o + 255 + n - 1, [[2048, 128], [1, n]]), ALU.mult))
                if EN is VE:
                    EN.raw(lambda e: e.drain())
                cur, oth = oth, cur
            lptok = EN.mark(gG if EN is GE else vV)
            lpsem = gG if EN is GE else vV
            pt_ = P[(i + 1) % 2]
            TE.wait(lpsem, lptok)
            p_write_ready(TE, pt_)
            for b in range(2):
                TE.do(lambda e, b=b, pt_=pt_, ptf=ptf: e.matmul(
                    pt_.t[:, 256 + 128 * b:256 + 128 * (b + 1)],
                    PT[:, ptf + 128 * b:ptf + 128 * (b + 1)], IDNB[:, :], start=True, stop=True))
            ttk = TE.mark(tT)
            p_mark_w(pt_, ttk)
            tr_toks[i] = ttk
            VE.wait(tT, ttk)
            dst = AP(KS, 128 * i, [[4 * T, 128], [T, 2], [1, 128]])
            lp_read[i] = VE.do(lambda e, dst=dst, pt_=pt_: e.tensor_copy(dst, pt_.t[:, 256:512]), inc=vV)
            p_mark_r(pt_, VE, lp_read[i])

        def emit_ret(s):
            TE.wait(vV, lp_read[TPS * (s + 1) - 1])
            for dt_ in range(8):
                pp = P[dt_ % 2]
                p_write_ready(TE, pp)
                for m in range(2):
                    lhsT = WA[:, 4096 + 1024 * m + 128 * dt_: 4096 + 1024 * m + 128 * (dt_ + 1)]
                    rhs = KS[:, m * T + SP512 * s: m * T + SP512 * (s + 1)]
                    TE.do(lambda e, o=pp.t[:, 0:SP512], l=lhsT, r=rhs, s_=(m == 0), p2=(m == 1):
                          e.matmul(o, l, r, start=s_, stop=p2))
                mtok = TE.mark(tT)
                p_mark_w(pp, mtok)
                SE.wait(tT, mtok)
                atok = SE.do(lambda e, o=XLN[:, dt_ * T + SP512 * s: dt_ * T + SP512 * (s + 1)],
                             i_=pp.t[:, 0:SP512]: e.activation(out=o, in_=i_, func=AF.Identity),
                             inc=aA)
                p_mark_r(pp, SE, atok)
                ret_done[s] = atok

        for s in range(NSPW):
            for i in range(TPS * s, TPS * (s + 1)):
                stage_a(i)
                if i >= 1:
                    stage_b(i - 1)
            if s >= 1:
                emit_ret(s - 1)
        stage_b(TPS * NSPW - 1)
        emit_ret(NSPW - 1)
        wslots = [(WSL[4], 0, tT.n), (WSL[5], 4096, tT.n)]
        lv_, tts_, pmt_ = emit_out_proj(XLN, d_fwoutT, wslots, fb_tok, [(aA, ret_done[NSPW - 1])], store=store)
        return lv_, tts_, pmt_

    ffn1_res = emit_ffn(wo_tok, wo_tok, store=False, tile_toks=wo_tts, prev_pmt=wo_pmt) if n_sub >= 2 else None
    ffn1_tok, ffn1_tts, ffn1_pmt = ffn1_res if ffn1_res else (None, None, [None, None])

    def emit_attn_b(prev_tok, tile_toks=None, prev_pmt=None):
        ln_t = emit_ln(prev_x_tok=prev_tok, tile_toks=tile_toks)
        wa_free = tT.n
        wa_p0 = prev_pmt[0] if prev_pmt and prev_pmt[0] else wa_free
        wa_p1 = prev_pmt[1] if prev_pmt and prev_pmt[1] else wa_free
        cs_sem = WSL[3]
        GE.wait(vV, ln_t)
        ctok = cs_sem.nxt(16)
        GE.ops.append(lambda e: e.dma_start(out=SHRD[:, 0:32 * NT],
                                            in_=d_cos.ap()[:, :]).then_inc(cs_sem.h, 16))
        stok = cs_sem.nxt(16)
        GE.ops.append(lambda e: e.dma_start(out=SHRD[:, 512:512 + 32 * NT],
                                            in_=d_sin.ap()[:, :]).then_inc(cs_sem.h, 16))
        last_attn = None
        quad_free = wa_free
        v_free2 = wa_p0
        rt_by = {}
        tt_by = {}
        VBP = NKT * 520
        ro_buf = [(PT, 0, PTW), (PT, 1024, PTW), (VB, 0, VBP)]
        for quad in range(2):
            wsem = WSL[quad % 2]
            qwtok = dma_w(wsem, WA[:, 0:4096],
                          AP(d_bwqT, 512 * quad, [[D, 128], [D * 128, 8], [1, 512]]),
                          after_tok=v_free2 if quad else wa_p0)
            kwtok = dma_w(wsem, WA[:, 4096:8192],
                          AP(d_bwkT, 512 * quad, [[D, 128], [D * 128, 8], [1, 512]]),
                          after_tok=None if quad else wa_p1)
            TE.wait(wsem, kwtok)
            VE.wait(cs_sem, stok)
            pair_tok = None

            def flush_pend(i, g, quad=None):
                pt_ = P[i % 2]
                buf, base, _ = ro_buf[g % 3]
                TE.wait(vV, rt_by[g])
                p_write_ready(TE, pt_)
                for b in range(8):
                    TE.do(lambda e, b=b, pt_=pt_, buf=buf, base=base: e.matmul(
                        pt_.t[:, 128 * b:128 * (b + 1)], buf[:, base + 128 * b:base + 128 * (b + 1)],
                        IDNB[:, :], start=True, stop=True))
                ttok = TE.mark(tT)
                p_mark_w(pt_, ttok)
                tt_by[g] = ttok
                qdst = AP(QS, (4 * quad) * T + 128 * i, [[NPAIR * T, 128], [T, 4], [1, 128]])
                kdst = AP(KS, 128 * i, [[4 * T, 128], [T, 4], [1, 128]])
                SE.wait(tT, ttok)
                SE.do(lambda e, d=qdst, pt_=pt_: e.activation(out=d, in_=pt_.t[:, 0:512], func=AF.Identity))
                ptok = SE.do(lambda e, d=kdst, pt_=pt_: e.activation(out=d, in_=pt_.t[:, 512:1024], func=AF.Identity), inc=aA)
                p_mark_r(pt_, SE, ptok)
                return ptok

            for i in range(NT):
                g = quad * NT + i
                pq = P[2 + i % 2]
                if quad == 0:
                    TE.wait(vV, ln_evs[0][i])
                p_write_ready(TE, pq)
                for ch in range(8):
                    lhsT = XLN[:, ch * T + 128 * i: ch * T + 128 * (i + 1)]
                    TE.do(lambda e, o=pq.t[:, 0:512], l=lhsT, r=WA[:, 512 * ch:512 * (ch + 1)],
                          s=(ch == 0), p2=(ch == 7): e.matmul(o, l, r, start=s, stop=p2))
                for ch in range(8):
                    lhsT = XLN[:, ch * T + 128 * i: ch * T + 128 * (i + 1)]
                    TE.do(lambda e, o=pq.t[:, 512:1024], l=lhsT, r=WA[:, 4096 + 512 * ch:4096 + 512 * (ch + 1)],
                          s=(ch == 0), p2=(ch == 7): e.matmul(o, l, r, start=s, stop=p2))
                mtok = TE.mark(tT)
                p_mark_w(pq, mtok)
                EV = 2048 + 1024 * (i % 2)
                SE.wait(tT, mtok)
                if g - 2 in rt_by:
                    SE.wait(vV, rt_by[g - 2])
                pqh = getattr(pq.t, "handle", pq.t)
                pqo = getattr(pq.t, "off", 0)
                ppitch = 2048 if pqh is PAB else 1024
                inev = AP(pqh, pqo, [[ppitch, 128], [64, 16], [2, 32]])
                inod = AP(pqh, pqo + 1, [[ppitch, 128], [64, 16], [2, 32]])
                evev = AP(PT, EV, [[PTW, 128], [64, 16], [1, 32]])
                evod = AP(PT, EV + 32, [[PTW, 128], [64, 16], [1, 32]])
                SE.do(lambda e, o_=evev, i_=inev: e.activation(out=o_, in_=i_, func=AF.Identity))
                a2 = SE.do(lambda e, o_=evod, i_=inod: e.activation(out=o_, in_=i_, func=AF.Identity), inc=aA)
                p_mark_r(pq, SE, a2)
                co = 32 * i
                rbuf, rbase, rpitch = ro_buf[g % 3]
                sin_r = AP(SHRD, 512 + co, [[2048, 128], [0, 16], [1, 32]])
                cos_r = AP(SHRD, co, [[2048, 128], [0, 16], [1, 32]])
                t1 = ROWS[:, 0:512]
                t2 = ROWS[:, 512:1024]
                bev = AP(rbuf, rbase, [[rpitch, 128], [64, 16], [1, 32]])
                bod = AP(rbuf, rbase + 32, [[rpitch, 128], [64, 16], [1, 32]])
                VE.wait(aA, a2)
                if g - 3 in tt_by:
                    VE.wait(tT, tt_by[g - 3])
                VE.do(lambda e, a=t1, b=evev, s=sin_r: e.tensor_tensor(a, b, s, ALU.mult))
                VE.do(lambda e, a=t2, b=evod, s=sin_r: e.tensor_tensor(a, b, s, ALU.mult))
                VE.raw(lambda e: e.drain())
                VE.do(lambda e, a=evev, c_=cos_r: e.tensor_tensor(a, a, c_, ALU.mult))
                VE.do(lambda e, a=evod, c_=cos_r: e.tensor_tensor(a, a, c_, ALU.mult))
                VE.raw(lambda e: e.drain())
                VE.do(lambda e, o_=bev, a=evev, b=t2: e.tensor_tensor(o_, a, b, ALU.subtract))
                VE.do(lambda e, o_=bod, a=evod, b=t1: e.tensor_tensor(o_, a, b, ALU.add))
                rtok = VE.mark(vV)
                rt_by[g] = rtok
                if i >= 2:
                    pair_tok = flush_pend(i - 2, g - 2, quad=quad)
            pair_tok = flush_pend(NT - 2, quad * NT + NT - 2, quad=quad)
            pair_tok = flush_pend(NT - 1, quad * NT + NT - 1, quad=quad)
            quad_free = tT.n
            vwtok = dma_w(WSL[2], WA[:, 0:4096],
                          AP(d_bwvT, 512 * quad, [[D, 128], [D * 128, 8], [1, 512]]),
                          after_tok=quad_free)
            VE.wait(tT, tt_by[quad * NT + NT - 1])
            for kt_ in (0, 1):
                VE.do(lambda e, kt=kt_: e.memset(
                    AP(VB, 520 * kt + 64, [[VBP, 128], [130, 4], [65, 2]]), 1.0))
            v_tok2 = emit_v_half(0, None, WSL[2], vwtok, None)
            v_free2 = tT.n
            for cl in range(4):
                c = 4 * quad + cl
                last_attn = emit_attention_pair(c, True, 1, pair_tok if cl == 0 else None, v_tok2)
        wslots = [(WSL[4], 0, tT.n), (WSL[5], 4096, None)]
        lvb_, ttsb_, pmtb_ = emit_out_proj(QS, d_bwoT, wslots, None, [(vV, last_attn)])
        return lvb_, ttsb_, pmtb_

    def emit_store(last_tok):
        SY.wait(vV, last_tok)
        if DBG & 512:
            GE.wait(aA, aA.n)
            GE.wait(vV, vV.n)
            dO.n += 16
            GE.ops.append(lambda e: e.dma_start(
                out=AP(d_y, 0, [[128, 128], [1, 128]]),
                in_=SML[:, :]).then_inc(dO.h, 16))
            return
        if DBG & 256:
            GE.wait(aA, aA.n)
            GE.wait(vV, vV.n)
            GE.wait(tT, tT.n)
            dO.n += 16
            GE.ops.append(lambda e, PTW=PTW: e.dma_start(
                out=AP(d_y, 0, [[D, 128], [128 * D, PTW // 1024], [1, D]]),
                in_=AP(PT, 0, [[PTW, 128], [1024, PTW // 1024], [1, 1024]])).then_inc(dO.h, 16))
            return
        if DBG & 128:
            GE.wait(aA, aA.n)
            GE.wait(vV, vV.n)
            dO.n += 16
            GE.ops.append(lambda e: e.dma_start(
                out=AP(d_y, 0, [[D, 128], [128 * D, (NCH * T) // 1024], [1, D]]),
                in_=AP(XLN, 0, [[NCH * T, 128], [1024, (NCH * T) // 1024], [1, 1024]])).then_inc(dO.h, 16))
            return
        if DBG & 64:
            GE.wait(aA, aA.n)
            GE.wait(vV, vV.n)
            GE.wait(dF, dF.n)
            dO.n += 16
            GE.ops.append(lambda e: e.dma_start(
                out=AP(d_y, 0, [[D, 128], [128 * D, NPAIR * T // 1024, ], [1, D]]) if False else AP(d_y, 0, [[D, 128], [128 * D, (NPAIR * T) // 1024], [1, D]]),
                in_=AP(QS, 0, [[NPAIR * T, 128], [1024, (NPAIR * T) // 1024], [1, 1024]])).then_inc(dO.h, 16))
            return
        for i in range(NT):
            dO.n += 16
            SY.ops.append(lambda e, i=i: e.dma_start(
                out=AP(d_y, 128 * i * D, [[D, 128], [1, D]]),
                in_=X[:, 1024 * i:1024 * (i + 1)]).then_inc(dO.h, 16))

    if n_sub >= 3:
        attb_tok, attb_tts, attb_pmt = emit_attn_b(ffn1_tok, tile_toks=ffn1_tts, prev_pmt=ffn1_pmt)
    if n_sub >= 4:
        ffn2_tok = emit_ffn(attb_tok, attb_tok, store=True, tile_toks=attb_tts, prev_pmt=attb_pmt)[0]
    elif n_sub == 3:
        emit_store(attb_tok)
    elif n_sub == 2:
        emit_store(ffn1_tok)
    else:
        emit_store(wo_tok)

    SY.raw(lambda e, n=dO.n: e.wait_ge(dO.h, n))

    import os as _os
    if _os.environ.get("KCOUNT"):
        print("opcounts:", {nm: len(EG.ops) for nm, EG in (("t", TE), ("v", VE), ("s", SE), ("g", GE), ("y", SY))})
    _tr = _os.environ.get("KBISECT", "")
    if _tr:
        eng_sel, frac = _tr.split(":")
        frac = float(frac)
        for nm, EG in (("t", TE), ("v", VE), ("s", SE), ("g", GE), ("y", SY)):
            if eng_sel != "all" and nm not in eng_sel:
                EG.ops = []
            else:
                EG.ops = EG.ops[:int(len(EG.ops) * frac)]
    with es:
        block = es.enter_context(nc.Block())

        @block.gpsimd
        def _(g):
            for f in GE.ops:
                f(g)

        @block.vector
        def _(v):
            for f in VE.ops:
                f(v)

        @block.scalar
        def _(s):
            for f in SE.ops:
                f(s)

        @block.tensor
        def _(t):
            for f in TE.ops:
                f(t)

        @block.sync
        def _(s):
            for f in SY.ops:
                f(s)

    return nc


def _prep_consts(T):
    NT = T // 128
    ident = np.eye(128, dtype=np.float32)
    tri = (np.arange(128)[None, :] >= np.arange(128)[:, None]).astype(np.float32)
    pos = np.arange(T, dtype=np.float32)
    pair = np.arange(32, dtype=np.float32)
    rates = (np.float32(10000.0) ** (-(2.0 * pair) / np.float32(64.0))).astype(np.float32)
    ang = (pos[:, None] * rates[None, :]).astype(np.float32)
    cos = np.cos(ang).astype(np.float32)
    sin = np.sin(ang).astype(np.float32)
    cos_t = np.zeros((128, 32 * NT), dtype=np.float32)
    sin_t = np.zeros((128, 32 * NT), dtype=np.float32)
    for i in range(NT):
        cos_t[:, 32 * i:32 * (i + 1)] = cos[128 * i:128 * (i + 1)]
        sin_t[:, 32 * i:32 * (i + 1)] = sin[128 * i:128 * (i + 1)]
    ones = np.ones((128, 1), dtype=np.float32)
    onesf = np.ones((128, 128), dtype=np.float32)
    csh = np.zeros((128, 2), dtype=np.float32)
    csh[:, 0] = -C_SHIFT_A
    csh[:, 1] = -C_SHIFT_B
    return ident, tri, cos_t, sin_t, ones, csh, onesf


def make_in_map(xc, w, T):
    ident, tri, cos_t, sin_t, ones, csh, onesf = _prep_consts(T)
    asc = lambda a: np.ascontiguousarray(a, dtype=np.float32)
    return {
        "x_in": asc(xc),
        "wqkvT": asc(w["L0_attn_a_qkv_w"].T),
        "woT": asc(w["L0_attn_a_out_w"].T),
        "qkvb_t": asc(w["L0_attn_a_qkv_b"].reshape(24, 128).T),
        "vb_row": asc(w["L0_attn_a_qkv_b"][2048:3072].reshape(1, 1024)),
        "ob_row": asc(w["L0_attn_a_out_b"].reshape(1, 1024)),
        "bwqT": asc(w["L0_attn_b_Wq"].T),
        "bwkT": asc(w["L0_attn_b_Wk"].T),
        "bwvT": asc(w["L0_attn_b_Wv"].T),
        "bwoT": asc(w["L0_attn_b_Wo"].T),
        "fwqT": asc(w["L0_ffn_Wq"].T),
        "routT": asc(w["L0_ffn_routers"].T),
        "lv_w": asc(w["L0_ffn_leaf_values"]),
        "fwoutT": asc(w["L0_ffn_Wout_w"].T),
        "fb_row": asc(w["L0_ffn_Wout_b"].reshape(1, 1024)),
        "ident_c": ident, "tri_c": tri, "cos_c": cos_t, "sin_c": sin_t,
        "ones_c": ones, "csh_c": csh, "onesf_c": onesf, "onesb_c": onesf[:, 0:64].copy(),
    }


def kernel(**inputs):
    x = np.asarray(inputs["L0_attn_a_x"], dtype=np.float32)
    B, T, _ = x.shape
    w = {k: np.asarray(v, dtype=np.float32) for k, v in inputs.items()
         if k != "L0_attn_a_x"}
    nc = build_nc(T)
    in_maps = [make_in_map(x[i], w, T) for i in range(B)]
    res = run_bass_kernel_spmd(nc, in_maps, core_ids=list(range(B)))
    out = np.stack([np.asarray(res.results[i]["y_out"]) for i in range(B)], axis=0)
    return out.astype(np.float32)


# version 30
# speedup vs baseline: 1.0204x; 1.0015x over previous
import contextlib
import numpy as np

import concourse.bass as bass
import concourse.mybir as mybir
from concourse.bass_utils import run_bass_kernel_spmd

F32 = mybir.dt.float32
BF16 = mybir.dt.bfloat16
AF = mybir.ActivationFunctionType
ALU = mybir.AluOpType

D = 1024
NCH = 8
NPAIR = 8
C_SHIFT_A = 12.0
C_SHIFT_B = 26.0

AP = bass.AP
DBG = 0


class Sem:
    def __init__(self, h):
        self.h = h
        self.n = 0

    def nxt(self, k=1):
        self.n += k
        return self.n


class Eng:

    def __init__(self, name):
        self.name = name
        self.ops = []

    def do(self, fn, inc=None, k=1):
        if inc is None:
            self.ops.append(fn)
            return None
        tok = inc.nxt(k)
        self.ops.append(lambda e, fn=fn, sem=inc.h, k=k: fn(e).then_inc(sem, k))
        return tok

    def mark(self, sem):
        tok = sem.nxt(1)
        prev = self.ops[-1]

        def wrapped(e, p=prev, s=sem.h):
            r = p(e)
            assert r is not None, "mark() must follow an instruction-emitting op"
            return r.then_inc(s, 1)

        self.ops[-1] = wrapped
        return tok

    def wait(self, sem, tok):
        if tok is None:
            return
        self.ops.append(lambda e, s=sem.h, t=tok: e.wait_ge(s, t))

    def raw(self, fn):
        self.ops.append(fn)


class Psum:
    def __init__(self, t):
        self.t = t
        self.wtok = None
        self.rtoks = []


def build_nc(T, n_sub=4, span_ovr=None):
    NT = T // 128
    SPAN = span_ovr or min(1024, T)
    NSP = T // SPAN
    NKT = T // 128
    SP512 = min(512, T)
    NSP512 = T // SP512
    PTW = max(4 * SPAN, 4096)

    nc = bass.Bass("TRN2", target_bir_lowering=False, detect_race_conditions=False)

    dram = {}

    def dr(name, shape, kind="ExternalInput"):
        dram[name] = nc.dram_tensor(name, shape, F32, kind=kind)
        return dram[name]

    d_x = dr("x_in", [T, D])
    d_y = dr("y_out", [T, D], "ExternalOutput")
    d_wqkvT = dr("wqkvT", [D, 3 * D])
    d_woT = dr("woT", [D, D])
    d_qkvb = dr("qkvb_t", [128, 24])
    d_vbrow = dr("vb_row", [1, D])
    d_obrow = dr("ob_row", [1, D])
    d_bwqT = dr("bwqT", [D, D])
    d_bwkT = dr("bwkT", [D, D])
    d_bwvT = dr("bwvT", [D, D])
    d_bwoT = dr("bwoT", [D, D])
    d_fwqT = dr("fwqT", [D, D])
    d_routT = dr("routT", [D, 255])
    d_lv = dr("lv_w", [256, D])
    d_fwoutT = dr("fwoutT", [D, D])
    d_fbrow = dr("fb_row", [1, D])
    d_ident = dr("ident_c", [128, 128])
    d_tri = dr("tri_c", [128, 128])
    d_cos = dr("cos_c", [128, 32 * NT])
    d_sin = dr("sin_c", [128, 32 * NT])
    d_ones = dr("ones_c", [128, 1])
    d_onesf = dr("onesf_c", [128, 128])
    d_onesb = dr("onesb_c", [128, 64])
    d_csh = dr("csh_c", [128, 2])

    es = contextlib.ExitStack()
    sb = lambda name, shape, dt: es.enter_context(nc.sbuf_tensor(name, shape, dt))
    sem = lambda name: Sem(es.enter_context(nc.semaphore(name)))

    X = sb("X", [128, NT * 1024], F32)
    SHRD = sb("SHRD", [128, 2048], F32)
    BIAS = sb("BIAS", [128, 1024], BF16)
    RB = sb("RB", [128, SPAN], F32)
    ROWS = sb("ROWS", [128, 1024], F32)
    SML = sb("SML", [128, 128], F32)
    STAT = sb("STAT", [128, 12 * 16], F32)
    QKVB = sb("QKVB", [128, 24], F32)
    XLN = sb("XLN", [128, NCH * T], BF16)
    QS = sb("QS", [128, NPAIR * T], BF16)
    KS = sb("KS", [128, 4 * T], BF16)
    VB = sb("VB", [128, NKT * 520], BF16)
    PT = sb("PT", [128, PTW], BF16)
    WA = sb("WA", [128, 8192], BF16)
    IDNB = sb("IDNB", [128, 128], BF16)
    TRI = sb("TRI", [128, 128], BF16)
    ONES = sb("ONES", [128, 1], BF16)
    ONESF = sb("ONESF", [128, 128], F32)
    ONESB = sb("ONESB", [128, 64], BF16)
    CSH = sb("CSH", [128, 2], F32)
    STG = sb("STG", [128, SPAN], BF16)

    class TV:

        def __init__(self, t, off):
            self._t = t
            self._o = off
            self.handle = t
            self.off = off

        def __getitem__(self, idx):
            r, c = idx
            return self._t[r, c.start + self._o:c.stop + self._o]

    PAB = es.enter_context(nc.psum_tensor("PAB", [128, 2048], F32))
    P = [Psum(TV(PAB, 0)), Psum(TV(PAB, 1024)),
         Psum(es.enter_context(nc.psum_tensor("P2", [128, 1024], F32))),
         Psum(es.enter_context(nc.psum_tensor("P3", [128, 1024], F32)))]

    tT, aA, vV, gG = sem("tT"), sem("aA"), sem("vV"), sem("gG")
    dX, dC, dO, dF = sem("dX"), sem("dC"), sem("dO"), sem("dF")
    WSL = [sem(f"ws{i}") for i in range(8)]

    TE, VE, SE, GE, SY = Eng("t"), Eng("v"), Eng("s"), Eng("g"), Eng("y")
    MARK = {id(TE): tT, id(VE): vV, id(SE): aA, id(GE): gG}

    def p_write_ready(eng, p):
        for s, t in p.rtoks:
            eng.wait(s, t)
        p.rtoks = []

    def p_read_ready(eng, p):
        if p.wtok is not None:
            eng.wait(*p.wtok)

    def p_mark_w(p, tok):
        p.wtok = (tT, tok)

    def p_mark_r(p, eng, tok):
        p.rtoks.append((MARK[id(eng)], tok))

    def dma_w(slot_sem, dst_ap, src_ap, after_tok=None):
        if after_tok is not None:
            GE.wait(tT, after_tok)
        tok = slot_sem.nxt(16)
        GE.ops.append(lambda e, d=dst_ap, s=src_ap, sm=slot_sem.h:
                      e.dma_start(out=d, in_=s).then_inc(sm, 16))
        return tok


    def dma_c(dst, src):
        GE.ops.append(lambda e, d=dst, s=src: e.dma_start(out=d, in_=s).then_inc(dC.h, 16))
        dC.n += 16

    dma_c(IDNB[:, :], d_ident.ap()[:, :])
    dma_c(TRI[:, :], d_tri.ap()[:, :])
    dma_c(ONES[:, :], d_ones.ap()[:, :])
    dma_c(QKVB[:, :], d_qkvb.ap()[:, :])
    dma_c(CSH[:, :], d_csh.ap()[:, :])
    dma_c(ONESF[:, :], d_onesf.ap()[:, :])
    dma_c(ONESB[:, :], d_onesb.ap()[:, :])
    tok_consts = dC.n

    for _kt in range(NKT):
        VE.ops.append(lambda e, kt=_kt: e.memset(
            AP(VB, 520 * kt + 64, [[NKT * 520, 128], [130, 4], [65, 2]]), 1.0))
    NXC = min(4, NT)
    TPC = NT // NXC
    x_chunk_tok = []
    for c_ in range(NXC):
        SY.ops.append(lambda e, c_=c_: e.dma_start(
            out=X[:, 1024 * TPC * c_:1024 * TPC * (c_ + 1)],
            in_=AP(d_x, 128 * TPC * c_ * D, [[D, 128], [128 * D, TPC], [1, D]])).then_inc(dX.h, 16))
        dX.n += 16
        x_chunk_tok.append(dX.n)
    tok_x = dX.n

    ln_evs = [None]

    def emit_ln(prev_x_tok=None, tile_toks=None):
        if DBG & 16:
            VE.wait(dX, tok_x)
            mtk = VE.do(lambda e: e.memset(SML[:, 0:1], 0.0), inc=vV)
            ln_evs[0] = [mtk] * NT
            return mtk
        first = prev_x_tok is None and tile_toks is None
        if not first:
            VE.wait(dX, tok_x)
        if tile_toks is None and prev_x_tok is not None:
            VE.wait(vV, prev_x_tok)
        VE.raw(lambda e: e.drain())
        for i in range(NT):
            xt = X[:, 1024 * i:1024 * (i + 1)]
            if tile_toks is not None:
                VE.wait(vV, tile_toks[i])
            if first:
                VE.wait(dX, x_chunk_tok[i // TPC])
            VE.do(lambda e, xt=xt, i=i: e.bn_stats(STAT[:, 12 * i:12 * i + 6], xt[:, 0:512]))
            VE.do(lambda e, xt=xt, i=i: e.bn_stats(STAT[:, 12 * i + 6:12 * i + 12], xt[:, 512:1024]))
        VE.raw(lambda e: e.drain())
        for i in range(NT):
            VE.do(lambda e, i=i: e.bn_aggr(SML[:, 80 + 2 * i:80 + 2 * i + 2], STAT[:, 12 * i:12 * i + 12]))
        VE.raw(lambda e: e.drain())
        VE.do(lambda e: e.tensor_scalar(SML[:, 0:NT], AP(SML, 81, [[128, 128], [2, NT]]),
                                        1e-5, None, ALU.add))
        vtok = VE.mark(vV)
        SE.wait(vV, vtok)
        stok = SE.do(lambda e: e.activation(out=SML[:, 16:16 + NT], in_=SML[:, 0:NT], func=AF.Sqrt), inc=aA)
        VE.wait(aA, stok)
        VE.raw(lambda e: e.drain())
        VE.do(lambda e: e.reciprocal(SML[:, 32:32 + NT], SML[:, 16:16 + NT]))
        VE.raw(lambda e: e.drain())
        for _ in range(2):
            VE.do(lambda e: e.tensor_tensor(SML[:, 48:48 + NT], SML[:, 32:32 + NT], SML[:, 32:32 + NT], ALU.mult))
            VE.raw(lambda e: e.drain())
            VE.do(lambda e: e.tensor_tensor(SML[:, 48:48 + NT], SML[:, 48:48 + NT], SML[:, 0:NT], ALU.mult))
            VE.raw(lambda e: e.drain())
            VE.do(lambda e: e.tensor_scalar(SML[:, 48:48 + NT], SML[:, 48:48 + NT], -0.5, 1.5, ALU.mult, ALU.add))
            VE.raw(lambda e: e.drain())
            VE.do(lambda e: e.tensor_tensor(SML[:, 32:32 + NT], SML[:, 32:32 + NT], SML[:, 48:48 + NT], ALU.mult))
            VE.raw(lambda e: e.drain())
        VE.do(lambda e: e.tensor_tensor(SML[:, 64:64 + NT], AP(SML, 80, [[128, 128], [2, NT]]),
                                        SML[:, 32:32 + NT], ALU.mult))
        VE.raw(lambda e: e.drain())
        VE.do(lambda e: e.tensor_scalar(SML[:, 64:64 + NT], SML[:, 64:64 + NT], -1.0, None, ALU.mult))
        ntok = VE.mark(vV)
        tr_tok = [None, None]
        last_ev = None
        ev_sems = []
        SE.wait(vV, ntok)
        for i in range(NT):
            xt = X[:, 1024 * i:1024 * (i + 1)]
            half = 1024 * (i % 2)
            xo = PT[:, half:half + 1024]
            if tr_tok[i % 2] is not None:
                SE.wait(tT, tr_tok[i % 2])
            atok = SE.do(lambda e, xt=xt, xo=xo, i=i: e.activation(
                out=xo, in_=xt, func=AF.Identity, bias=SML[:, 64 + i:65 + i],
                scale=SML[:, 32 + i:33 + i]), inc=aA)
            pt = P[i % 2]
            TE.wait(aA, atok)
            p_write_ready(TE, pt)
            for c in range(8):
                TE.do(lambda e, xo=xo, c=c, pt=pt: e.matmul(
                    pt.t[:, 128 * c:128 * (c + 1)], xo[:, 128 * c:128 * (c + 1)],
                    IDNB[:, :], start=True, stop=True))
            ttok = TE.mark(tT)
            p_mark_w(pt, ttok)
            tr_tok[i % 2] = ttok
            dst = AP(XLN, 128 * i, [[NCH * T, 128], [T, 8], [1, 128]])
            VE.wait(tT, ttok)
            etok = VE.do(lambda e, pt=pt, dst=dst: e.tensor_copy(dst, pt.t[:, 0:1024]), inc=vV)
            p_mark_r(pt, VE, etok)
            ev_sems.append(etok)
            last_ev = etok
        ln_evs[0] = ev_sems
        return last_ev

    rows_free = [None]
    rb_free = [None]
    stg_free = [0]
    rowA_free = [None]
    rowB_free = [None]

    def emit_bias_bcast(src_row, reader_toks):
        if DBG & 32:
            return None
        s = WSL[6]
        if rows_free[0] is not None:
            GE.wait(tT, rows_free[0])
        tok = s.nxt(16)
        GE.ops.append(lambda e, sr=src_row: e.dma_start(
            out=ROWS[0:1, :], in_=sr).then_inc(s.h, 16))
        pp = P[0]
        TE.wait(WSL[6], tok)
        p_write_ready(TE, pp)
        for f in range(2):
            btok = TE.do(lambda e, f=f: e.matmul(pp.t[:, 512 * f:512 * (f + 1)],
                                                 ONESF[0:1, 0:128],
                                                 ROWS[0:1, 512 * f:512 * (f + 1)],
                                                 start=True, stop=True), inc=tT)
        p_mark_w(pp, btok)
        rows_free[0] = btok
        SE.wait(tT, btok)
        for st, tk in reader_toks:
            SE.wait(st, tk)
        atok = SE.do(lambda e: e.activation(out=BIAS[:, :], in_=pp.t[:, 0:1024], func=AF.Identity), inc=aA)
        p_mark_r(pp, SE, atok)
        return atok

    def emit_pair_proj(wsem, wcol, dst_sl, bias_col, wtok, pps, extra_wait=None,
                       span_evs=None):
        if DBG & 8:
            return None
        TE.wait(wsem, wtok)
        if extra_wait is not None:
            TE.wait(*extra_wait)
        last_atok = None
        tps = SP512 // 128
        for sp in range(NSP512):
            pp = pps[sp % len(pps)]
            half = SP512 * ((sp // len(pps)) % 2) if SP512 == 512 else 0
            if span_evs is not None:
                TE.wait(vV, span_evs[min(tps * (sp + 1) - 1, NT - 1)])
            p_write_ready(TE, pp)
            for ch in range(8):
                lhsT = WA[:, wcol + 128 * ch: wcol + 128 * (ch + 1)]
                rhs = XLN[:, ch * T + SP512 * sp: ch * T + SP512 * (sp + 1)]
                TE.do(lambda e, o=pp.t[:, half:half + SP512], l=lhsT, r=rhs,
                      s=(ch == 0), p2=(ch == 7): e.matmul(o, l, r, start=s, stop=p2))
            mtok = TE.mark(tT)
            p_mark_w(pp, mtok)
            VE.wait(tT, mtok)
            out = dst_sl[:, SP512 * sp:SP512 * (sp + 1)]
            if bias_col is not None:
                brow = AP(QKVB, bias_col, [[24, 128], [0, SP512]])
                atok = VE.do(lambda e, o=out, i=pp.t[:, half:half + SP512], b=brow:
                             e.tensor_tensor(o, i, b, ALU.add), inc=vV)
            else:
                atok = VE.do(lambda e, o=out, i=pp.t[:, half:half + SP512]:
                             e.tensor_copy(o, i), inc=vV)
            p_mark_r(pp, VE, atok)
            last_atok = atok
        return last_atok

    def emit_v_half(wcol, bias_half, wsem, wtok, bias_tok):
        if DBG & 8:
            return None
        TE.wait(wsem, wtok)
        last = None
        for kt in range(NKT):
            pp = P[kt % 2]
            p_write_ready(TE, pp)
            for ch in range(8):
                lhsT = XLN[:, ch * T + 128 * kt: ch * T + 128 * (kt + 1)]
                rhs = WA[:, wcol + 512 * ch: wcol + 512 * (ch + 1)]
                TE.do(lambda e, o=pp.t[:, 0:512], l=lhsT, r=rhs, s=(ch == 0), p2=(ch == 7):
                      e.matmul(o, l, r, start=s, stop=p2))
            mtok = TE.mark(tT)
            p_mark_w(pp, mtok)
            VE.wait(tT, mtok)
            if bias_tok is not None and kt == 0:
                VE.wait(aA, bias_tok)
            dst = AP(VB, 520 * kt, [[NKT * 520, 128], [130, 4], [65, 2], [1, 64]])
            srcp = AP(pp.t.handle, pp.t.off, [[2048, 128], [128, 4], [64, 2], [1, 64]])
            if bias_half is not None:
                bsrc = AP(BIAS, 512 * bias_half, [[1024, 128], [128, 4], [64, 2], [1, 64]])
                vtok = VE.do(lambda e, d=dst, i=srcp, b=bsrc:
                             e.tensor_tensor(d, i, b, ALU.add), inc=vV)
            else:
                vtok = VE.do(lambda e, d=dst, i=srcp: e.tensor_copy(d, i), inc=vV)
            p_mark_r(pp, VE, vtok)
            last = vtok
        return last

    def emit_attention_pair(c, causal, cshift, qk_tok, v_tok):
        if DBG & 4:
            return v_tok
        qsl = QS[:, c * T:(c + 1) * T]
        koff = ((c % 4) if causal else (c % 2)) * T
        ksl = KS[:, koff:koff + T]
        vbase = 130 * (c % 4)
        S_A, S_B, PV_A, PV_B = P[0], P[1], P[2], P[3]
        TE.wait(vV, v_tok)
        if qk_tok is not None:
            TE.wait(*qk_tok)
        last_ev = None
        for sp in range(NSP):
            kts = [kt for kt in range(NKT)
                   if not (causal and 128 * kt > SPAN * sp + SPAN - 1)]
            def _off(kt):
                return max(0, 128 * kt - SPAN * sp) if causal else 0
            last_kt_bank = {}
            for bi in range((SPAN + 511) // 512):
                contrib = [kt for kt in kts if _off(kt) < 512 * bi + 512]
                last_kt_bank[bi] = contrib[-1]
            def emit_pv(hh, kt, ki, segs, ptok, slot):
                pv = PV_A if hh == 0 else PV_B
                TE.wait(*ptok)
                if ki == 0 and hh == 0:
                    p_write_ready(TE, PV_A)
                    p_write_ready(TE, PV_B)
                vsl = VB[:, 520 * kt + vbase + 65 * hh:520 * kt + vbase + 65 * hh + 65]
                st = (ki == 0)
                for (a, b) in segs:
                    prt = PT[:, slot + SPAN * hh + a:slot + SPAN * hh + b]
                    fin = (kt == last_kt_bank[a // 512])
                    TE.do(lambda e, o_=pv.t[0:65, a:b], l=vsl, r=prt, s=st, f=fin:
                          e.matmul(o_, l, r, start=s, stop=f))
                pvtok = TE.mark(tT)
                p_mark_w(pv, pvtok)

            pending = None
            for ki, kt in enumerate(kts):
                o = max(0, 128 * kt - SPAN * sp) if causal else 0
                segs = []
                n0 = o
                while n0 < SPAN:
                    n1 = min((n0 // 512 + 1) * 512, SPAN)
                    segs.append((n0, n1))
                    n0 = n1
                slot = 2 * SPAN * (kt % 2)
                mtoks = [None, None]
                for hh in range(2):
                    pp = S_A if hh == 0 else S_B
                    rowlo = 64 * hh
                    p_write_ready(TE, pp)
                    lhsT = ksl[rowlo:rowlo + 64, 128 * kt:128 * (kt + 1)]
                    for (a, b) in segs:
                        rhs = qsl[rowlo:rowlo + 64, SPAN * sp + a:SPAN * sp + b]
                        TE.do(lambda e, o_=pp.t[0:128, a:b], l=lhsT, r=rhs, tp=(rowlo, 0):
                              e.matmul(o_, l, r, start=True, stop=True, tile_position=tp))
                    mtoks[hh] = TE.mark(tT)
                    p_mark_w(pp, mtoks[hh])
                ptoks = [None, None]
                if False:
                    SE.wait(tT, mtoks[1])
                    etok = SE.do(lambda e, o_=PT[:, slot:slot + 2 * SPAN],
                                 i=PAB[:, 0:2 * SPAN], cs=cshift:
                                 e.activation(out=o_, in_=i, func=AF.Exp,
                                              bias=CSH[:, cs:cs + 1], scale=0.125),
                                 inc=aA)
                    p_mark_r(S_A, SE, etok)
                    p_mark_r(S_B, SE, etok)
                    etoks = [etok, etok]
                else:
                    etoks = [None, None]
                    for hh in range(2):
                        pp = S_A if hh == 0 else S_B
                        SE.wait(tT, mtoks[hh])
                        etoks[hh] = SE.do(lambda e, o_=PT[:, slot + SPAN * hh + o:slot + SPAN * (hh + 1)],
                                          i=pp.t[:, o:SPAN], cs=cshift:
                                          e.activation(out=o_, in_=i, func=AF.Exp,
                                                       bias=CSH[:, cs:cs + 1], scale=0.125),
                                          inc=aA)
                        p_mark_r(pp, SE, etoks[hh])
                for hh in range(2):
                    if causal and 128 * kt >= SPAN * sp:
                        VE.wait(aA, etoks[hh])
                        mtok2 = VE.do(lambda e, o_=PT[:, slot + SPAN * hh + o:slot + SPAN * hh + o + 128]:
                                      e.tensor_tensor(o_, o_, TRI[:, :], ALU.mult), inc=vV)
                        ptoks[hh] = (vV, mtok2)
                    else:
                        ptoks[hh] = (aA, etoks[hh])
                if pending is not None:
                    emit_pv(0, pending[0], pending[1], pending[2], pending[3], pending[5])
                    emit_pv(1, pending[0], pending[1], pending[2], pending[4], pending[5])
                pending = (kt, ki, segs, ptoks[0], ptoks[1], slot)
            emit_pv(0, pending[0], pending[1], pending[2], pending[3], pending[5])
            emit_pv(1, pending[0], pending[1], pending[2], pending[4], pending[5])
            if DBG & 1:
                last_ev = vV.n if vV.n > 0 else None
                continue
            spc0, spc1 = SPAN * sp, SPAN * (sp + 1)
            SE.wait(*PV_A.wtok)
            SE.wait(tT, rowA_free[0])
            SE.do(lambda e: e.activation(out=ROWS[64:65, 0:SPAN], in_=PV_A.t[64:65, 0:SPAN], func=AF.Ln))
            SE.wait(*PV_B.wtok)
            SE.wait(tT, rowB_free[0])
            SE.do(lambda e: e.activation(out=RB[64:65, 0:SPAN], in_=PV_B.t[64:65, 0:SPAN], func=AF.Ln))
            SE.raw(lambda e: e.drain())
            r1A = SE.do(lambda e: e.activation(out=ROWS[64:65, 0:SPAN], in_=ROWS[64:65, 0:SPAN], func=AF.Exp, scale=-1.0), inc=aA)
            p_mark_r(PV_A, SE, r1A)
            r1B = SE.do(lambda e: e.activation(out=RB[64:65, 0:SPAN], in_=RB[64:65, 0:SPAN], func=AF.Exp, scale=-1.0), inc=aA)
            p_mark_r(PV_B, SE, r1B)
            for hh, (pv, bp, r1, srow) in enumerate(((PV_A, S_A, r1A, ROWS), (PV_B, S_B, r1B, RB))):
                TE.wait(aA, r1)
                p_write_ready(TE, bp)
                for (a, b) in ((x, min(x + 512, SPAN)) for x in range(0, SPAN, 512)):
                    btok = TE.do(lambda e, bp=bp, a=a, b=b, srow=srow: e.matmul(
                        bp.t[0:64, a:b], ONESF[64:65, 0:64], srow[64:65, a:b],
                        start=True, stop=True, tile_position=(64, 0)), inc=tT)
                p_mark_w(bp, btok)
                if hh == 0:
                    rowA_free[0] = btok
                else:
                    rowB_free[0] = btok
                SE.wait(tT, btok)
                if hh == 1:
                    SE.wait(vV, e1)
                else:
                    SE.wait(vV, rb_free[0])
                ctok = SE.do(lambda e, bp=bp: e.activation(out=RB[0:64, 0:SPAN], in_=bp.t[0:64, 0:SPAN], func=AF.Identity), inc=aA)
                p_mark_r(bp, SE, ctok)
                VE.wait(aA, ctok)
                VE.wait(*pv.wtok)
                if hh == 0:
                    e1 = VE.do(lambda e, d=qsl[0:64, spc0:spc1], pv=pv:
                               e.tensor_tensor(d, pv.t[0:64, 0:SPAN], RB[0:64, 0:SPAN], ALU.mult), inc=vV)
                    p_mark_r(pv, VE, e1)
                else:
                    VE.wait(dF, stg_free[0])
                    e2 = VE.do(lambda e, pv=pv:
                               e.tensor_tensor(STG[0:64, 0:SPAN], pv.t[0:64, 0:SPAN], RB[0:64, 0:SPAN], ALU.mult), inc=vV)
                    p_mark_r(pv, VE, e2)
                    rb_free[0] = e2
                    GE.wait(vV, e2)
                    dF.n += 16
                    stg_free[0] = dF.n
                    GE.ops.append(lambda e, d=qsl[64:128, spc0:spc1]:
                                  e.dma_start(out=d, in_=STG[0:64, 0:SPAN]).then_inc(dF.h, 16))
            last_ev = e2
        return last_ev

    def emit_out_proj(src_buf, w_dram, wslots, bias_tok, ready_toks, store=False):
        last_vtok = None
        pass_mm_tok = [None, None]
        p1_toks = [None] * NT
        if DBG & 2:
            return (vV.n if vV.n else None), [None] * NT, pass_mm_tok
        for p_ in range(2):
            wsem, wcol, afree = wslots[p_]
            src_ap = AP(w_dram, 512 * p_ * D, [[D, 128], [128 * D, 4], [1, D]])
            wtok = dma_w(wsem, WA[:, wcol:wcol + 4096], src_ap, after_tok=afree)
            TE.wait(wsem, wtok)
            if p_ == 0:
                TE.wait(dF, dF.n)
                for s, t in ready_toks:
                    TE.wait(s, t)
            for i in range(NT):
                pp = P[i % 2] if p_ == 0 else P[2 + i % 2]
                p_write_ready(TE, pp)
                for ch4 in range(4):
                    ch = 4 * p_ + ch4
                    lhsT = src_buf[:, ch * T + 128 * i: ch * T + 128 * (i + 1)]
                    for f in range(2):
                        rhs = WA[:, wcol + 1024 * ch4 + 512 * f: wcol + 1024 * ch4 + 512 * (f + 1)]
                        TE.do(lambda e, o=pp.t[:, 512 * f:512 * (f + 1)], l=lhsT, r=rhs,
                              s=(ch4 == 0), p2=(ch4 == 3): e.matmul(o, l, r, start=s, stop=p2))
                mtok = TE.mark(tT)
                p_mark_w(pp, mtok)
                pass_mm_tok[p_] = mtok
                VE.wait(tT, mtok)
                xsl = X[:, 1024 * i:1024 * (i + 1)]
                vt = VE.do(lambda e, x=xsl, i_=pp.t[:, 0:1024]:
                           e.tensor_tensor(x, x, i_, ALU.add), inc=vV)
                p_mark_r(pp, VE, vt)
                last_vtok = vt
                if p_ == 1:
                    p1_toks[i] = vt
            if p_ == 0 and bias_tok is not None:
                VE.wait(aA, bias_tok)
                VE.raw(lambda e: e.drain())
                for i in range(NT):
                    xsl = X[:, 1024 * i:1024 * (i + 1)]
                    VE.do(lambda e, x=xsl: e.tensor_tensor(x, x, BIAS[:, :], ALU.add))
        tile_toks = p1_toks
        if store:
            for i in range(NT):
                SY.wait(vV, tile_toks[i])
                dO.n += 16
                SY.ops.append(lambda e, i=i: e.dma_start(
                    out=AP(d_y, 128 * i * D, [[D, 128], [1, D]]),
                    in_=X[:, 1024 * i:1024 * (i + 1)]).then_inc(dO.h, 16))
        return last_vtok, tile_toks, pass_mm_tok

    GE.wait(dC, tok_consts)
    TE.wait(dC, tok_consts)
    VE.wait(dC, tok_consts)
    SE.wait(dC, tok_consts)
    ln_tok = emit_ln()
    vb_tok = emit_bias_bcast(d_vbrow.ap()[:, :], [])

    pair_free = [None, None]
    v_free = None
    v_tok = None
    last_attn = None
    for c in range(NPAIR):
        wsem = WSL[c % 2]
        wcol = 2048 * (c % 2)
        qtok = dma_w(wsem, WA[:, wcol:wcol + 1024],
                     AP(d_wqkvT, 128 * c, [[3 * D, 128], [3 * D * 128, 8], [1, 128]]),
                     after_tok=pair_free[c % 2])
        ktok = dma_w(wsem, WA[:, wcol + 1024:wcol + 2048],
                     AP(d_wqkvT, D + 128 * c, [[3 * D, 128], [3 * D * 128, 8], [1, 128]]))
        emit_pair_proj(wsem, wcol, QS[:, c * T:(c + 1) * T], c, ktok, [P[2], P[3]],
                       span_evs=ln_evs[0] if c == 0 else None)
        ak = emit_pair_proj(wsem, wcol + 1024, KS[:, (c % 2) * T:(c % 2) * T + T],
                            8 + c, ktok, [P[2], P[3]])
        pair_free[c % 2] = tT.n
        if c in (0, 4):
            vwtok = dma_w(WSL[2], WA[:, 4096:8192],
                          AP(d_wqkvT, 2 * D + 512 * (c // 4),
                             [[3 * D, 128], [3 * D * 128, 8], [1, 512]]),
                          after_tok=v_free)
            v_tok = emit_v_half(4096, c // 4, WSL[2], vwtok, vb_tok if c == 0 else None)
            v_free = tT.n
        last_attn = emit_attention_pair(c, False, 0, (vV, ak), v_tok)

    ob_tok = emit_bias_bcast(d_obrow.ap()[:, :], [(vV, v_tok)])
    wo_slots = [(WSL[4], 0, tT.n), (WSL[5], 4096, tT.n)]
    wo_tok, wo_tts, wo_pmt = emit_out_proj(QS, d_woT, wo_slots, ob_tok, [(vV, last_attn)])

    def emit_ffn(prev_tok, bias_reader_tok, store, tile_toks=None, prev_pmt=None):
        ln_t = emit_ln(prev_x_tok=prev_tok, tile_toks=tile_toks)
        wa_free = tT.n
        wa_p0 = prev_pmt[0] if prev_pmt and prev_pmt[0] else wa_free
        wa_p1 = prev_pmt[1] if prev_pmt and prev_pmt[1] else wa_free
        rtok = dma_w(WSL[2], WA[:, 2048:2048 + 2040],
                     AP(d_routT, 0, [[255, 128], [255 * 128, 8], [1, 255]]),
                     after_tok=wa_p0)
        lvtok = dma_w(WSL[3], WA[:, 4096:6144],
                      AP(d_lv, 0, [[D, 128], [D * 128, 2], [1, D]]),
                      after_tok=wa_p1)
        fb_tok = emit_bias_bcast(d_fbrow.ap()[:, :], [(vV, bias_reader_tok)])
        qf_tok = None
        slot_free = [wa_p0, wa_p0]
        for et in range(8):
            wsem = WSL[et % 2]
            wcol = 1024 * (et % 2)
            wtok = dma_w(wsem, WA[:, wcol:wcol + 1024],
                         AP(d_fwqT, 128 * et, [[D, 128], [D * 128, 8], [1, 128]]),
                         after_tok=slot_free[et % 2])
            qf_tok = emit_pair_proj(wsem, wcol, QS[:, et * T:(et + 1) * T], None,
                                    wtok, [P[0], P[1]],
                                    span_evs=ln_evs[0] if et == 0 else None)
            slot_free[et % 2] = tT.n
        TE.wait(WSL[2], rtok)
        TE.wait(WSL[3], lvtok)

        NSPW = T // SP512
        TPS = SP512 // 128
        lp_read = {}
        tr_toks = {}
        sig_tok = {}
        ret_done = {}

        def stage_a(i):
            pz = P[2 + i % 2]
            p_write_ready(TE, pz)
            if i == 0:
                TE.wait(vV, qf_tok)
            for ch in range(8):
                lhsT = QS[:, ch * T + 128 * i: ch * T + 128 * (i + 1)]
                rhs = WA[:, 2048 + 255 * ch: 2048 + 255 * (ch + 1)]
                TE.do(lambda e, o=pz.t[:, 0:255], l=lhsT, r=rhs, s_=(ch == 0), p2=(ch == 7):
                      e.matmul(o, l, r, start=s_, stop=p2))
            ztok = TE.mark(tT)
            p_mark_w(pz, ztok)
            g2o = 512 * (i % 2)
            SE.wait(tT, ztok)
            if i >= 2 and (i - 2) in lp_read:
                SE.wait(vV, lp_read[i - 2])
            SE.do(lambda e, o=SHRD[:, g2o:g2o + 255], i_=pz.t[:, 0:255]:
                  e.activation(out=o, in_=i_, func=AF.Sigmoid, scale=-1.0))
            gtok = SE.do(lambda e, o=SHRD[:, g2o + 255:g2o + 510], i_=pz.t[:, 0:255]:
                         e.activation(out=o, in_=i_, func=AF.Sigmoid), inc=aA)
            p_mark_r(pz, SE, gtok)
            sig_tok[i] = gtok

        def stage_b(i):
            g2o = 512 * (i % 2)
            EN = GE if i % 2 == 0 else VE
            EN.wait(aA, sig_tok[i])
            if i >= 2 and (i - 2) in tr_toks:
                EN.wait(tT, tr_toks[i - 2])
            lpA = 1024 + 512 * (i % 2)
            lpB = lpA + 256
            ptf = 256 * (i % 2)
            EN.do(lambda e, g2o=g2o, lpA=lpA: e.tensor_copy(
                AP(SHRD, lpA, [[2048, 128], [1, 2]]),
                AP(SHRD, g2o, [[2048, 128], [255, 2]])))
            if EN is VE:
                EN.raw(lambda e: e.drain())
            cur, oth = lpA, lpB
            for lv_ in range(1, 8):
                n = 1 << lv_
                last = lv_ == 7
                o_ev = AP(PT, ptf, [[PTW, 128], [2, n]]) if last else AP(SHRD, oth, [[2048, 128], [2, n]])
                o_od = AP(PT, ptf + 1, [[PTW, 128], [2, n]]) if last else AP(SHRD, oth + 1, [[2048, 128], [2, n]])
                EN.do(lambda e, cur=cur, o_=o_ev, n=n, g2o=g2o: e.tensor_tensor(
                    o_, AP(SHRD, cur, [[2048, 128], [1, n]]),
                    AP(SHRD, g2o + n - 1, [[2048, 128], [1, n]]), ALU.mult))
                EN.do(lambda e, cur=cur, o_=o_od, n=n, g2o=g2o: e.tensor_tensor(
                    o_, AP(SHRD, cur, [[2048, 128], [1, n]]),
                    AP(SHRD, g2o + 255 + n - 1, [[2048, 128], [1, n]]), ALU.mult))
                if EN is VE:
                    EN.raw(lambda e: e.drain())
                cur, oth = oth, cur
            lptok = EN.mark(gG if EN is GE else vV)
            lpsem = gG if EN is GE else vV
            pt_ = P[(i + 1) % 2]
            TE.wait(lpsem, lptok)
            p_write_ready(TE, pt_)
            for b in range(2):
                TE.do(lambda e, b=b, pt_=pt_, ptf=ptf: e.matmul(
                    pt_.t[:, 256 + 128 * b:256 + 128 * (b + 1)],
                    PT[:, ptf + 128 * b:ptf + 128 * (b + 1)], IDNB[:, :], start=True, stop=True))
            ttk = TE.mark(tT)
            p_mark_w(pt_, ttk)
            tr_toks[i] = ttk
            VE.wait(tT, ttk)
            dst = AP(KS, 128 * i, [[4 * T, 128], [T, 2], [1, 128]])
            lp_read[i] = VE.do(lambda e, dst=dst, pt_=pt_: e.tensor_copy(dst, pt_.t[:, 256:512]), inc=vV)
            p_mark_r(pt_, VE, lp_read[i])

        def emit_ret(s):
            TE.wait(vV, lp_read[TPS * (s + 1) - 1])
            for dt_ in range(8):
                pp = P[dt_ % 2]
                p_write_ready(TE, pp)
                for m in range(2):
                    lhsT = WA[:, 4096 + 1024 * m + 128 * dt_: 4096 + 1024 * m + 128 * (dt_ + 1)]
                    rhs = KS[:, m * T + SP512 * s: m * T + SP512 * (s + 1)]
                    TE.do(lambda e, o=pp.t[:, 0:SP512], l=lhsT, r=rhs, s_=(m == 0), p2=(m == 1):
                          e.matmul(o, l, r, start=s_, stop=p2))
                mtok = TE.mark(tT)
                p_mark_w(pp, mtok)
                SE.wait(tT, mtok)
                atok = SE.do(lambda e, o=XLN[:, dt_ * T + SP512 * s: dt_ * T + SP512 * (s + 1)],
                             i_=pp.t[:, 0:SP512]: e.activation(out=o, in_=i_, func=AF.Identity),
                             inc=aA)
                p_mark_r(pp, SE, atok)
                ret_done[s] = atok

        for s in range(NSPW):
            for i in range(TPS * s, TPS * (s + 1)):
                stage_a(i)
                if i >= 1:
                    stage_b(i - 1)
            if s >= 1:
                emit_ret(s - 1)
        stage_b(TPS * NSPW - 1)
        emit_ret(NSPW - 1)
        wslots = [(WSL[4], 0, tT.n), (WSL[5], 4096, tT.n)]
        lv_, tts_, pmt_ = emit_out_proj(XLN, d_fwoutT, wslots, fb_tok, [(aA, ret_done[NSPW - 1])], store=store)
        return lv_, tts_, pmt_

    ffn1_res = emit_ffn(wo_tok, wo_tok, store=False, tile_toks=wo_tts, prev_pmt=wo_pmt) if n_sub >= 2 else None
    ffn1_tok, ffn1_tts, ffn1_pmt = ffn1_res if ffn1_res else (None, None, [None, None])

    def emit_attn_b(prev_tok, tile_toks=None, prev_pmt=None):
        ln_t = emit_ln(prev_x_tok=prev_tok, tile_toks=tile_toks)
        wa_free = tT.n
        wa_p0 = prev_pmt[0] if prev_pmt and prev_pmt[0] else wa_free
        wa_p1 = prev_pmt[1] if prev_pmt and prev_pmt[1] else wa_free
        cs_sem = WSL[3]
        GE.wait(vV, ln_t)
        ctok = cs_sem.nxt(16)
        GE.ops.append(lambda e: e.dma_start(out=SHRD[:, 0:32 * NT],
                                            in_=d_cos.ap()[:, :]).then_inc(cs_sem.h, 16))
        stok = cs_sem.nxt(16)
        GE.ops.append(lambda e: e.dma_start(out=SHRD[:, 512:512 + 32 * NT],
                                            in_=d_sin.ap()[:, :]).then_inc(cs_sem.h, 16))
        last_attn = None
        quad_free = wa_free
        v_free2 = wa_p0
        rt_by = {}
        tt_by = {}
        VBP = NKT * 520
        ro_buf = [(PT, 0, PTW), (PT, 1024, PTW), (VB, 0, VBP)]
        for quad in range(2):
            wsem = WSL[quad % 2]
            qwtok = dma_w(wsem, WA[:, 0:4096],
                          AP(d_bwqT, 512 * quad, [[D, 128], [D * 128, 8], [1, 512]]),
                          after_tok=v_free2 if quad else wa_p0)
            kwtok = dma_w(wsem, WA[:, 4096:8192],
                          AP(d_bwkT, 512 * quad, [[D, 128], [D * 128, 8], [1, 512]]),
                          after_tok=None if quad else wa_p1)
            TE.wait(wsem, kwtok)
            VE.wait(cs_sem, stok)
            pair_tok = None

            def flush_pend(i, g, quad=None):
                pt_ = P[i % 2]
                buf, base, _ = ro_buf[g % 3]
                TE.wait(vV, rt_by[g])
                p_write_ready(TE, pt_)
                for b in range(8):
                    TE.do(lambda e, b=b, pt_=pt_, buf=buf, base=base: e.matmul(
                        pt_.t[:, 128 * b:128 * (b + 1)], buf[:, base + 128 * b:base + 128 * (b + 1)],
                        IDNB[:, :], start=True, stop=True))
                ttok = TE.mark(tT)
                p_mark_w(pt_, ttok)
                tt_by[g] = ttok
                qdst = AP(QS, (4 * quad) * T + 128 * i, [[NPAIR * T, 128], [T, 4], [1, 128]])
                kdst = AP(KS, 128 * i, [[4 * T, 128], [T, 4], [1, 128]])
                SE.wait(tT, ttok)
                SE.do(lambda e, d=qdst, pt_=pt_: e.activation(out=d, in_=pt_.t[:, 0:512], func=AF.Identity))
                ptok = SE.do(lambda e, d=kdst, pt_=pt_: e.activation(out=d, in_=pt_.t[:, 512:1024], func=AF.Identity), inc=aA)
                p_mark_r(pt_, SE, ptok)
                return ptok

            for i in range(NT):
                g = quad * NT + i
                pq = P[2 + i % 2]
                if quad == 0:
                    TE.wait(vV, ln_evs[0][i])
                p_write_ready(TE, pq)
                for ch in range(8):
                    lhsT = XLN[:, ch * T + 128 * i: ch * T + 128 * (i + 1)]
                    TE.do(lambda e, o=pq.t[:, 0:512], l=lhsT, r=WA[:, 512 * ch:512 * (ch + 1)],
                          s=(ch == 0), p2=(ch == 7): e.matmul(o, l, r, start=s, stop=p2))
                for ch in range(8):
                    lhsT = XLN[:, ch * T + 128 * i: ch * T + 128 * (i + 1)]
                    TE.do(lambda e, o=pq.t[:, 512:1024], l=lhsT, r=WA[:, 4096 + 512 * ch:4096 + 512 * (ch + 1)],
                          s=(ch == 0), p2=(ch == 7): e.matmul(o, l, r, start=s, stop=p2))
                mtok = TE.mark(tT)
                p_mark_w(pq, mtok)
                EV = 2048 + 1024 * (i % 2)
                SE.wait(tT, mtok)
                if g - 2 in rt_by:
                    SE.wait(vV, rt_by[g - 2])
                pqh = getattr(pq.t, "handle", pq.t)
                pqo = getattr(pq.t, "off", 0)
                ppitch = 2048 if pqh is PAB else 1024
                inev = AP(pqh, pqo, [[ppitch, 128], [64, 16], [2, 32]])
                inod = AP(pqh, pqo + 1, [[ppitch, 128], [64, 16], [2, 32]])
                evev = AP(PT, EV, [[PTW, 128], [64, 16], [1, 32]])
                evod = AP(PT, EV + 32, [[PTW, 128], [64, 16], [1, 32]])
                SE.do(lambda e, o_=evev, i_=inev: e.activation(out=o_, in_=i_, func=AF.Identity))
                a2 = SE.do(lambda e, o_=evod, i_=inod: e.activation(out=o_, in_=i_, func=AF.Identity), inc=aA)
                p_mark_r(pq, SE, a2)
                co = 32 * i
                rbuf, rbase, rpitch = ro_buf[g % 3]
                sin_r = AP(SHRD, 512 + co, [[2048, 128], [0, 16], [1, 32]])
                cos_r = AP(SHRD, co, [[2048, 128], [0, 16], [1, 32]])
                t1 = ROWS[:, 0:512]
                t2 = ROWS[:, 512:1024]
                bev = AP(rbuf, rbase, [[rpitch, 128], [64, 16], [1, 32]])
                bod = AP(rbuf, rbase + 32, [[rpitch, 128], [64, 16], [1, 32]])
                VE.wait(aA, a2)
                if g - 3 in tt_by:
                    VE.wait(tT, tt_by[g - 3])
                VE.do(lambda e, a=t1, b=evev, s=sin_r: e.tensor_tensor(a, b, s, ALU.mult))
                VE.do(lambda e, a=t2, b=evod, s=sin_r: e.tensor_tensor(a, b, s, ALU.mult))
                VE.raw(lambda e: e.drain())
                VE.do(lambda e, a=evev, c_=cos_r: e.tensor_tensor(a, a, c_, ALU.mult))
                VE.do(lambda e, a=evod, c_=cos_r: e.tensor_tensor(a, a, c_, ALU.mult))
                VE.raw(lambda e: e.drain())
                VE.do(lambda e, o_=bev, a=evev, b=t2: e.tensor_tensor(o_, a, b, ALU.subtract))
                VE.do(lambda e, o_=bod, a=evod, b=t1: e.tensor_tensor(o_, a, b, ALU.add))
                rtok = VE.mark(vV)
                rt_by[g] = rtok
                if i >= 2:
                    pair_tok = flush_pend(i - 2, g - 2, quad=quad)
            pair_tok = flush_pend(NT - 2, quad * NT + NT - 2, quad=quad)
            pair_tok = flush_pend(NT - 1, quad * NT + NT - 1, quad=quad)
            quad_free = tT.n
            vwtok = dma_w(WSL[2], WA[:, 0:4096],
                          AP(d_bwvT, 512 * quad, [[D, 128], [D * 128, 8], [1, 512]]),
                          after_tok=quad_free)
            VE.wait(tT, tt_by[quad * NT + NT - 1])
            for kt_ in (0, 1):
                VE.do(lambda e, kt=kt_: e.memset(
                    AP(VB, 520 * kt + 64, [[VBP, 128], [130, 4], [65, 2]]), 1.0))
            v_tok2 = emit_v_half(0, None, WSL[2], vwtok, None)
            v_free2 = tT.n
            for cl in range(4):
                c = 4 * quad + cl
                last_attn = emit_attention_pair(c, True, 1, (aA, pair_tok) if cl == 0 else None, v_tok2)
        wslots = [(WSL[4], 0, tT.n), (WSL[5], 4096, None)]
        lvb_, ttsb_, pmtb_ = emit_out_proj(QS, d_bwoT, wslots, None, [(vV, last_attn)])
        return lvb_, ttsb_, pmtb_

    def emit_store(last_tok):
        SY.wait(vV, last_tok)
        if DBG & 512:
            GE.wait(aA, aA.n)
            GE.wait(vV, vV.n)
            dO.n += 16
            GE.ops.append(lambda e: e.dma_start(
                out=AP(d_y, 0, [[128, 128], [1, 128]]),
                in_=SML[:, :]).then_inc(dO.h, 16))
            return
        if DBG & 256:
            GE.wait(aA, aA.n)
            GE.wait(vV, vV.n)
            GE.wait(tT, tT.n)
            dO.n += 16
            GE.ops.append(lambda e, PTW=PTW: e.dma_start(
                out=AP(d_y, 0, [[D, 128], [128 * D, PTW // 1024], [1, D]]),
                in_=AP(PT, 0, [[PTW, 128], [1024, PTW // 1024], [1, 1024]])).then_inc(dO.h, 16))
            return
        if DBG & 128:
            GE.wait(aA, aA.n)
            GE.wait(vV, vV.n)
            dO.n += 16
            GE.ops.append(lambda e: e.dma_start(
                out=AP(d_y, 0, [[D, 128], [128 * D, (NCH * T) // 1024], [1, D]]),
                in_=AP(XLN, 0, [[NCH * T, 128], [1024, (NCH * T) // 1024], [1, 1024]])).then_inc(dO.h, 16))
            return
        if DBG & 64:
            GE.wait(aA, aA.n)
            GE.wait(vV, vV.n)
            GE.wait(dF, dF.n)
            dO.n += 16
            GE.ops.append(lambda e: e.dma_start(
                out=AP(d_y, 0, [[D, 128], [128 * D, NPAIR * T // 1024, ], [1, D]]) if False else AP(d_y, 0, [[D, 128], [128 * D, (NPAIR * T) // 1024], [1, D]]),
                in_=AP(QS, 0, [[NPAIR * T, 128], [1024, (NPAIR * T) // 1024], [1, 1024]])).then_inc(dO.h, 16))
            return
        for i in range(NT):
            dO.n += 16
            SY.ops.append(lambda e, i=i: e.dma_start(
                out=AP(d_y, 128 * i * D, [[D, 128], [1, D]]),
                in_=X[:, 1024 * i:1024 * (i + 1)]).then_inc(dO.h, 16))

    if n_sub >= 3:
        attb_tok, attb_tts, attb_pmt = emit_attn_b(ffn1_tok, tile_toks=ffn1_tts, prev_pmt=ffn1_pmt)
    if n_sub >= 4:
        ffn2_tok = emit_ffn(attb_tok, attb_tok, store=True, tile_toks=attb_tts, prev_pmt=attb_pmt)[0]
    elif n_sub == 3:
        emit_store(attb_tok)
    elif n_sub == 2:
        emit_store(ffn1_tok)
    else:
        emit_store(wo_tok)

    SY.raw(lambda e, n=dO.n: e.wait_ge(dO.h, n))

    import os as _os
    if _os.environ.get("KCOUNT"):
        print("opcounts:", {nm: len(EG.ops) for nm, EG in (("t", TE), ("v", VE), ("s", SE), ("g", GE), ("y", SY))})
    _tr = _os.environ.get("KBISECT", "")
    if _tr:
        eng_sel, frac = _tr.split(":")
        frac = float(frac)
        for nm, EG in (("t", TE), ("v", VE), ("s", SE), ("g", GE), ("y", SY)):
            if eng_sel != "all" and nm not in eng_sel:
                EG.ops = []
            else:
                EG.ops = EG.ops[:int(len(EG.ops) * frac)]
    with es:
        block = es.enter_context(nc.Block())

        @block.gpsimd
        def _(g):
            for f in GE.ops:
                f(g)

        @block.vector
        def _(v):
            for f in VE.ops:
                f(v)

        @block.scalar
        def _(s):
            for f in SE.ops:
                f(s)

        @block.tensor
        def _(t):
            for f in TE.ops:
                f(t)

        @block.sync
        def _(s):
            for f in SY.ops:
                f(s)

    return nc


def _prep_consts(T):
    NT = T // 128
    ident = np.eye(128, dtype=np.float32)
    tri = (np.arange(128)[None, :] >= np.arange(128)[:, None]).astype(np.float32)
    pos = np.arange(T, dtype=np.float32)
    pair = np.arange(32, dtype=np.float32)
    rates = (np.float32(10000.0) ** (-(2.0 * pair) / np.float32(64.0))).astype(np.float32)
    ang = (pos[:, None] * rates[None, :]).astype(np.float32)
    cos = np.cos(ang).astype(np.float32)
    sin = np.sin(ang).astype(np.float32)
    cos_t = np.zeros((128, 32 * NT), dtype=np.float32)
    sin_t = np.zeros((128, 32 * NT), dtype=np.float32)
    for i in range(NT):
        cos_t[:, 32 * i:32 * (i + 1)] = cos[128 * i:128 * (i + 1)]
        sin_t[:, 32 * i:32 * (i + 1)] = sin[128 * i:128 * (i + 1)]
    ones = np.ones((128, 1), dtype=np.float32)
    onesf = np.ones((128, 128), dtype=np.float32)
    csh = np.zeros((128, 2), dtype=np.float32)
    csh[:, 0] = -C_SHIFT_A
    csh[:, 1] = -C_SHIFT_B
    return ident, tri, cos_t, sin_t, ones, csh, onesf


def make_in_map(xc, w, T):
    ident, tri, cos_t, sin_t, ones, csh, onesf = _prep_consts(T)
    asc = lambda a: np.ascontiguousarray(a, dtype=np.float32)
    return {
        "x_in": asc(xc),
        "wqkvT": asc(w["L0_attn_a_qkv_w"].T),
        "woT": asc(w["L0_attn_a_out_w"].T),
        "qkvb_t": asc(w["L0_attn_a_qkv_b"].reshape(24, 128).T),
        "vb_row": asc(w["L0_attn_a_qkv_b"][2048:3072].reshape(1, 1024)),
        "ob_row": asc(w["L0_attn_a_out_b"].reshape(1, 1024)),
        "bwqT": asc(w["L0_attn_b_Wq"].T),
        "bwkT": asc(w["L0_attn_b_Wk"].T),
        "bwvT": asc(w["L0_attn_b_Wv"].T),
        "bwoT": asc(w["L0_attn_b_Wo"].T),
        "fwqT": asc(w["L0_ffn_Wq"].T),
        "routT": asc(w["L0_ffn_routers"].T),
        "lv_w": asc(w["L0_ffn_leaf_values"]),
        "fwoutT": asc(w["L0_ffn_Wout_w"].T),
        "fb_row": asc(w["L0_ffn_Wout_b"].reshape(1, 1024)),
        "ident_c": ident, "tri_c": tri, "cos_c": cos_t, "sin_c": sin_t,
        "ones_c": ones, "csh_c": csh, "onesf_c": onesf, "onesb_c": onesf[:, 0:64].copy(),
    }


def kernel(**inputs):
    x = np.asarray(inputs["L0_attn_a_x"], dtype=np.float32)
    B, T, _ = x.shape
    w = {k: np.asarray(v, dtype=np.float32) for k, v in inputs.items()
         if k != "L0_attn_a_x"}
    nc = build_nc(T)
    in_maps = [make_in_map(x[i], w, T) for i in range(B)]
    res = run_bass_kernel_spmd(nc, in_maps, core_ids=list(range(B)))
    out = np.stack([np.asarray(res.results[i]["y_out"]) for i in range(B)], axis=0)
    return out.astype(np.float32)


# version 31
# speedup vs baseline: 1.0271x; 1.0066x over previous
import contextlib
import numpy as np

import concourse.bass as bass
import concourse.mybir as mybir
from concourse.bass_utils import run_bass_kernel_spmd

F32 = mybir.dt.float32
BF16 = mybir.dt.bfloat16
AF = mybir.ActivationFunctionType
ALU = mybir.AluOpType

D = 1024
NCH = 8
NPAIR = 8
C_SHIFT_A = 12.0
C_SHIFT_B = 26.0

AP = bass.AP
DBG = 0


class Sem:
    def __init__(self, h):
        self.h = h
        self.n = 0

    def nxt(self, k=1):
        self.n += k
        return self.n


class Eng:

    def __init__(self, name):
        self.name = name
        self.ops = []

    def do(self, fn, inc=None, k=1):
        if inc is None:
            self.ops.append(fn)
            return None
        tok = inc.nxt(k)
        self.ops.append(lambda e, fn=fn, sem=inc.h, k=k: fn(e).then_inc(sem, k))
        return tok

    def mark(self, sem):
        tok = sem.nxt(1)
        prev = self.ops[-1]

        def wrapped(e, p=prev, s=sem.h):
            r = p(e)
            assert r is not None, "mark() must follow an instruction-emitting op"
            return r.then_inc(s, 1)

        self.ops[-1] = wrapped
        return tok

    def wait(self, sem, tok):
        if tok is None:
            return
        self.ops.append(lambda e, s=sem.h, t=tok: e.wait_ge(s, t))

    def raw(self, fn):
        self.ops.append(fn)


class Psum:
    def __init__(self, t):
        self.t = t
        self.wtok = None
        self.rtoks = []


def build_nc(T, n_sub=4, span_ovr=None):
    NT = T // 128
    SPAN = span_ovr or min(1024, T)
    NSP = T // SPAN
    NKT = T // 128
    SP512 = min(512, T)
    NSP512 = T // SP512
    PTW = max(4 * SPAN, 4096)

    nc = bass.Bass("TRN2", target_bir_lowering=False, detect_race_conditions=False)

    dram = {}

    def dr(name, shape, kind="ExternalInput"):
        dram[name] = nc.dram_tensor(name, shape, F32, kind=kind)
        return dram[name]

    d_x = dr("x_in", [T, D])
    d_y = dr("y_out", [T, D], "ExternalOutput")
    d_wqkvT = dr("wqkvT", [D, 3 * D])
    d_woT = dr("woT", [D, D])
    d_qkvb = dr("qkvb_t", [128, 24])
    d_vbrow = dr("vb_row", [1, D])
    d_obrow = dr("ob_row", [1, D])
    d_bwqT = dr("bwqT", [D, D])
    d_bwkT = dr("bwkT", [D, D])
    d_bwvT = dr("bwvT", [D, D])
    d_bwoT = dr("bwoT", [D, D])
    d_fwqT = dr("fwqT", [D, D])
    d_routT = dr("routT", [D, 255])
    d_lv = dr("lv_w", [256, D])
    d_fwoutT = dr("fwoutT", [D, D])
    d_fbrow = dr("fb_row", [1, D])
    d_ident = dr("ident_c", [128, 128])
    d_tri = dr("tri_c", [128, 128])
    d_cos = dr("cos_c", [128, 32 * NT])
    d_sin = dr("sin_c", [128, 32 * NT])
    d_ones = dr("ones_c", [128, 1])
    d_onesf = dr("onesf_c", [128, 128])
    d_onesb = dr("onesb_c", [128, 64])
    d_csh = dr("csh_c", [128, 2])

    es = contextlib.ExitStack()
    sb = lambda name, shape, dt: es.enter_context(nc.sbuf_tensor(name, shape, dt))
    sem = lambda name: Sem(es.enter_context(nc.semaphore(name)))

    X = sb("X", [128, NT * 1024], F32)
    SHRD = sb("SHRD", [128, 2048], F32)
    BIAS = sb("BIAS", [128, 1024], BF16)
    RB = sb("RB", [128, SPAN], F32)
    ROWS = sb("ROWS", [128, 1024], F32)
    SML = sb("SML", [128, 128], F32)
    STAT = sb("STAT", [128, 12 * 16], F32)
    QKVB = sb("QKVB", [128, 24], F32)
    XLN = sb("XLN", [128, NCH * T], BF16)
    QS = sb("QS", [128, NPAIR * T], BF16)
    KS = sb("KS", [128, 4 * T], BF16)
    VB = sb("VB", [128, NKT * 520], BF16)
    PT = sb("PT", [128, PTW], BF16)
    WA = sb("WA", [128, 8192], BF16)
    IDNB = sb("IDNB", [128, 128], BF16)
    TRI = sb("TRI", [128, 128], BF16)
    ONES = sb("ONES", [128, 1], BF16)
    ONESF = sb("ONESF", [128, 128], F32)
    ONESB = sb("ONESB", [128, 64], BF16)
    CSH = sb("CSH", [128, 2], F32)
    STG = sb("STG", [128, SPAN], BF16)

    class TV:

        def __init__(self, t, off):
            self._t = t
            self._o = off
            self.handle = t
            self.off = off

        def __getitem__(self, idx):
            r, c = idx
            return self._t[r, c.start + self._o:c.stop + self._o]

    PAB = es.enter_context(nc.psum_tensor("PAB", [128, 2048], F32))
    P = [Psum(TV(PAB, 0)), Psum(TV(PAB, 1024)),
         Psum(es.enter_context(nc.psum_tensor("P2", [128, 1024], F32))),
         Psum(es.enter_context(nc.psum_tensor("P3", [128, 1024], F32)))]

    tT, aA, vV, gG = sem("tT"), sem("aA"), sem("vV"), sem("gG")
    dX, dC, dO, dF = sem("dX"), sem("dC"), sem("dO"), sem("dF")
    WSL = [sem(f"ws{i}") for i in range(8)]

    TE, VE, SE, GE, SY = Eng("t"), Eng("v"), Eng("s"), Eng("g"), Eng("y")
    MARK = {id(TE): tT, id(VE): vV, id(SE): aA, id(GE): gG}

    def p_write_ready(eng, p):
        for s, t in p.rtoks:
            eng.wait(s, t)
        p.rtoks = []

    def p_read_ready(eng, p):
        if p.wtok is not None:
            eng.wait(*p.wtok)

    def p_mark_w(p, tok):
        p.wtok = (tT, tok)

    def p_mark_r(p, eng, tok):
        p.rtoks.append((MARK[id(eng)], tok))

    def dma_w(slot_sem, dst_ap, src_ap, after_tok=None):
        if after_tok is not None:
            GE.wait(tT, after_tok)
        tok = slot_sem.nxt(16)
        GE.ops.append(lambda e, d=dst_ap, s=src_ap, sm=slot_sem.h:
                      e.dma_start(out=d, in_=s).then_inc(sm, 16))
        return tok


    def dma_c(dst, src):
        GE.ops.append(lambda e, d=dst, s=src: e.dma_start(out=d, in_=s).then_inc(dC.h, 16))
        dC.n += 16

    dma_c(IDNB[:, :], d_ident.ap()[:, :])
    dma_c(TRI[:, :], d_tri.ap()[:, :])
    dma_c(ONES[:, :], d_ones.ap()[:, :])
    dma_c(QKVB[:, :], d_qkvb.ap()[:, :])
    dma_c(CSH[:, :], d_csh.ap()[:, :])
    dma_c(ONESF[:, :], d_onesf.ap()[:, :])
    dma_c(ONESB[:, :], d_onesb.ap()[:, :])
    tok_consts = dC.n

    for _kt in range(NKT):
        VE.ops.append(lambda e, kt=_kt: e.memset(
            AP(VB, 520 * kt + 64, [[NKT * 520, 128], [130, 4], [65, 2]]), 1.0))
    NXC = min(4, NT)
    TPC = NT // NXC
    x_chunk_tok = []
    for c_ in range(NXC):
        SY.ops.append(lambda e, c_=c_: e.dma_start(
            out=X[:, 1024 * TPC * c_:1024 * TPC * (c_ + 1)],
            in_=AP(d_x, 128 * TPC * c_ * D, [[D, 128], [128 * D, TPC], [1, D]])).then_inc(dX.h, 16))
        dX.n += 16
        x_chunk_tok.append(dX.n)
    tok_x = dX.n

    ln_evs = [None]

    def emit_ln(prev_x_tok=None, tile_toks=None):
        if DBG & 16:
            VE.wait(dX, tok_x)
            mtk = VE.do(lambda e: e.memset(SML[:, 0:1], 0.0), inc=vV)
            ln_evs[0] = [mtk] * NT
            return mtk
        first = prev_x_tok is None and tile_toks is None
        if not first:
            VE.wait(dX, tok_x)
        if tile_toks is None and prev_x_tok is not None:
            VE.wait(vV, prev_x_tok)
        VE.raw(lambda e: e.drain())
        for i in range(NT):
            xt = X[:, 1024 * i:1024 * (i + 1)]
            if tile_toks is not None:
                VE.wait(vV, tile_toks[i])
            if first:
                VE.wait(dX, x_chunk_tok[i // TPC])
            VE.do(lambda e, xt=xt, i=i: e.bn_stats(STAT[:, 12 * i:12 * i + 6], xt[:, 0:512]))
            VE.do(lambda e, xt=xt, i=i: e.bn_stats(STAT[:, 12 * i + 6:12 * i + 12], xt[:, 512:1024]))
        VE.raw(lambda e: e.drain())
        for i in range(NT):
            VE.do(lambda e, i=i: e.bn_aggr(SML[:, 80 + 2 * i:80 + 2 * i + 2], STAT[:, 12 * i:12 * i + 12]))
        VE.raw(lambda e: e.drain())
        VE.do(lambda e: e.tensor_scalar(SML[:, 0:NT], AP(SML, 81, [[128, 128], [2, NT]]),
                                        1e-5, None, ALU.add))
        vtok = VE.mark(vV)
        SE.wait(vV, vtok)
        stok = SE.do(lambda e: e.activation(out=SML[:, 16:16 + NT], in_=SML[:, 0:NT], func=AF.Sqrt), inc=aA)
        VE.wait(aA, stok)
        VE.raw(lambda e: e.drain())
        VE.do(lambda e: e.reciprocal(SML[:, 32:32 + NT], SML[:, 16:16 + NT]))
        VE.raw(lambda e: e.drain())
        for _ in range(2):
            VE.do(lambda e: e.tensor_tensor(SML[:, 48:48 + NT], SML[:, 32:32 + NT], SML[:, 32:32 + NT], ALU.mult))
            VE.raw(lambda e: e.drain())
            VE.do(lambda e: e.tensor_tensor(SML[:, 48:48 + NT], SML[:, 48:48 + NT], SML[:, 0:NT], ALU.mult))
            VE.raw(lambda e: e.drain())
            VE.do(lambda e: e.tensor_scalar(SML[:, 48:48 + NT], SML[:, 48:48 + NT], -0.5, 1.5, ALU.mult, ALU.add))
            VE.raw(lambda e: e.drain())
            VE.do(lambda e: e.tensor_tensor(SML[:, 32:32 + NT], SML[:, 32:32 + NT], SML[:, 48:48 + NT], ALU.mult))
            VE.raw(lambda e: e.drain())
        VE.do(lambda e: e.tensor_tensor(SML[:, 64:64 + NT], AP(SML, 80, [[128, 128], [2, NT]]),
                                        SML[:, 32:32 + NT], ALU.mult))
        VE.raw(lambda e: e.drain())
        VE.do(lambda e: e.tensor_scalar(SML[:, 64:64 + NT], SML[:, 64:64 + NT], -1.0, None, ALU.mult))
        ntok = VE.mark(vV)
        tr_tok = [None, None]
        last_ev = None
        ev_sems = []
        SE.wait(vV, ntok)
        for i in range(NT):
            xt = X[:, 1024 * i:1024 * (i + 1)]
            half = 1024 * (i % 2)
            xo = PT[:, half:half + 1024]
            if tr_tok[i % 2] is not None:
                SE.wait(tT, tr_tok[i % 2])
            atok = SE.do(lambda e, xt=xt, xo=xo, i=i: e.activation(
                out=xo, in_=xt, func=AF.Identity, bias=SML[:, 64 + i:65 + i],
                scale=SML[:, 32 + i:33 + i]), inc=aA)
            pt = P[i % 2]
            TE.wait(aA, atok)
            p_write_ready(TE, pt)
            for c in range(8):
                TE.do(lambda e, xo=xo, c=c, pt=pt: e.matmul(
                    pt.t[:, 128 * c:128 * (c + 1)], xo[:, 128 * c:128 * (c + 1)],
                    IDNB[:, :], start=True, stop=True))
            ttok = TE.mark(tT)
            p_mark_w(pt, ttok)
            tr_tok[i % 2] = ttok
            dst = AP(XLN, 128 * i, [[NCH * T, 128], [T, 8], [1, 128]])
            VE.wait(tT, ttok)
            etok = VE.do(lambda e, pt=pt, dst=dst: e.tensor_copy(dst, pt.t[:, 0:1024]), inc=vV)
            p_mark_r(pt, VE, etok)
            ev_sems.append(etok)
            last_ev = etok
        ln_evs[0] = ev_sems
        return last_ev

    rows_free = [None]
    rb_free = [None]
    stg_free = [0]
    rowA_free = [None]
    rowB_free = [None]

    def emit_bias_bcast(src_row, reader_toks):
        if DBG & 32:
            return None
        s = WSL[6]
        if rows_free[0] is not None:
            GE.wait(tT, rows_free[0])
        tok = s.nxt(16)
        GE.ops.append(lambda e, sr=src_row: e.dma_start(
            out=ROWS[0:1, :], in_=sr).then_inc(s.h, 16))
        pp = P[0]
        TE.wait(WSL[6], tok)
        p_write_ready(TE, pp)
        for f in range(2):
            btok = TE.do(lambda e, f=f: e.matmul(pp.t[:, 512 * f:512 * (f + 1)],
                                                 ONESF[0:1, 0:128],
                                                 ROWS[0:1, 512 * f:512 * (f + 1)],
                                                 start=True, stop=True), inc=tT)
        p_mark_w(pp, btok)
        rows_free[0] = btok
        SE.wait(tT, btok)
        for st, tk in reader_toks:
            SE.wait(st, tk)
        atok = SE.do(lambda e: e.activation(out=BIAS[:, :], in_=pp.t[:, 0:1024], func=AF.Identity), inc=aA)
        p_mark_r(pp, SE, atok)
        return atok

    def emit_pair_proj(wsem, wcol, dst_sl, bias_col, wtok, pps, extra_wait=None,
                       span_evs=None):
        if DBG & 8:
            return None
        TE.wait(wsem, wtok)
        if extra_wait is not None:
            TE.wait(*extra_wait)
        last_atok = None
        tps = SP512 // 128
        for sp in range(NSP512):
            pp = pps[sp % len(pps)]
            half = SP512 * ((sp // len(pps)) % 2) if SP512 == 512 else 0
            if span_evs is not None:
                TE.wait(vV, span_evs[min(tps * (sp + 1) - 1, NT - 1)])
            p_write_ready(TE, pp)
            for ch in range(8):
                lhsT = WA[:, wcol + 128 * ch: wcol + 128 * (ch + 1)]
                rhs = XLN[:, ch * T + SP512 * sp: ch * T + SP512 * (sp + 1)]
                TE.do(lambda e, o=pp.t[:, half:half + SP512], l=lhsT, r=rhs,
                      s=(ch == 0), p2=(ch == 7): e.matmul(o, l, r, start=s, stop=p2))
            mtok = TE.mark(tT)
            p_mark_w(pp, mtok)
            SE.wait(tT, mtok)
            out = dst_sl[:, SP512 * sp:SP512 * (sp + 1)]
            if bias_col is not None:
                atok = SE.do(lambda e, o=out, i=pp.t[:, half:half + SP512],
                             b=QKVB[:, bias_col:bias_col + 1]:
                             e.activation(out=o, in_=i, func=AF.Identity, bias=b), inc=aA)
            else:
                atok = SE.do(lambda e, o=out, i=pp.t[:, half:half + SP512]:
                             e.activation(out=o, in_=i, func=AF.Identity), inc=aA)
            p_mark_r(pp, SE, atok)
            last_atok = atok
        return last_atok

    def emit_v_half(wcol, bias_half, wsem, wtok, bias_tok):
        if DBG & 8:
            return None
        TE.wait(wsem, wtok)
        last = None
        for kt in range(NKT):
            pp = P[kt % 2]
            p_write_ready(TE, pp)
            for ch in range(8):
                lhsT = XLN[:, ch * T + 128 * kt: ch * T + 128 * (kt + 1)]
                rhs = WA[:, wcol + 512 * ch: wcol + 512 * (ch + 1)]
                TE.do(lambda e, o=pp.t[:, 0:512], l=lhsT, r=rhs, s=(ch == 0), p2=(ch == 7):
                      e.matmul(o, l, r, start=s, stop=p2))
            mtok = TE.mark(tT)
            p_mark_w(pp, mtok)
            VE.wait(tT, mtok)
            if bias_tok is not None and kt == 0:
                VE.wait(aA, bias_tok)
            dst = AP(VB, 520 * kt, [[NKT * 520, 128], [130, 4], [65, 2], [1, 64]])
            srcp = AP(pp.t.handle, pp.t.off, [[2048, 128], [128, 4], [64, 2], [1, 64]])
            if bias_half is not None:
                bsrc = AP(BIAS, 512 * bias_half, [[1024, 128], [128, 4], [64, 2], [1, 64]])
                vtok = VE.do(lambda e, d=dst, i=srcp, b=bsrc:
                             e.tensor_tensor(d, i, b, ALU.add), inc=vV)
            else:
                vtok = VE.do(lambda e, d=dst, i=srcp: e.tensor_copy(d, i), inc=vV)
            p_mark_r(pp, VE, vtok)
            last = vtok
        return last

    def emit_attention_pair(c, causal, cshift, qk_tok, v_tok):
        if DBG & 4:
            return v_tok
        qsl = QS[:, c * T:(c + 1) * T]
        koff = ((c % 4) if causal else (c % 2)) * T
        ksl = KS[:, koff:koff + T]
        vbase = 130 * (c % 4)
        S_A, S_B, PV_A, PV_B = P[0], P[1], P[2], P[3]
        TE.wait(vV, v_tok)
        if qk_tok is not None:
            TE.wait(aA, qk_tok)
        last_ev = None
        for sp in range(NSP):
            kts = [kt for kt in range(NKT)
                   if not (causal and 128 * kt > SPAN * sp + SPAN - 1)]
            def _off(kt):
                return max(0, 128 * kt - SPAN * sp) if causal else 0
            last_kt_bank = {}
            for bi in range((SPAN + 511) // 512):
                contrib = [kt for kt in kts if _off(kt) < 512 * bi + 512]
                last_kt_bank[bi] = contrib[-1]
            def emit_pv(hh, kt, ki, segs, ptok, slot):
                pv = PV_A if hh == 0 else PV_B
                TE.wait(*ptok)
                if ki == 0 and hh == 0:
                    p_write_ready(TE, PV_A)
                    p_write_ready(TE, PV_B)
                vsl = VB[:, 520 * kt + vbase + 65 * hh:520 * kt + vbase + 65 * hh + 65]
                st = (ki == 0)
                for (a, b) in segs:
                    prt = PT[:, slot + SPAN * hh + a:slot + SPAN * hh + b]
                    fin = (kt == last_kt_bank[a // 512])
                    TE.do(lambda e, o_=pv.t[0:65, a:b], l=vsl, r=prt, s=st, f=fin:
                          e.matmul(o_, l, r, start=s, stop=f))
                pvtok = TE.mark(tT)
                p_mark_w(pv, pvtok)

            pending = None
            for ki, kt in enumerate(kts):
                o = max(0, 128 * kt - SPAN * sp) if causal else 0
                segs = []
                n0 = o
                while n0 < SPAN:
                    n1 = min((n0 // 512 + 1) * 512, SPAN)
                    segs.append((n0, n1))
                    n0 = n1
                slot = 2 * SPAN * (kt % 2)
                mtoks = [None, None]
                for hh in range(2):
                    pp = S_A if hh == 0 else S_B
                    rowlo = 64 * hh
                    p_write_ready(TE, pp)
                    lhsT = ksl[rowlo:rowlo + 64, 128 * kt:128 * (kt + 1)]
                    for (a, b) in segs:
                        rhs = qsl[rowlo:rowlo + 64, SPAN * sp + a:SPAN * sp + b]
                        TE.do(lambda e, o_=pp.t[0:128, a:b], l=lhsT, r=rhs, tp=(rowlo, 0):
                              e.matmul(o_, l, r, start=True, stop=True, tile_position=tp))
                    mtoks[hh] = TE.mark(tT)
                    p_mark_w(pp, mtoks[hh])
                ptoks = [None, None]
                if False:
                    SE.wait(tT, mtoks[1])
                    etok = SE.do(lambda e, o_=PT[:, slot:slot + 2 * SPAN],
                                 i=PAB[:, 0:2 * SPAN], cs=cshift:
                                 e.activation(out=o_, in_=i, func=AF.Exp,
                                              bias=CSH[:, cs:cs + 1], scale=0.125),
                                 inc=aA)
                    p_mark_r(S_A, SE, etok)
                    p_mark_r(S_B, SE, etok)
                    etoks = [etok, etok]
                else:
                    etoks = [None, None]
                    for hh in range(2):
                        pp = S_A if hh == 0 else S_B
                        SE.wait(tT, mtoks[hh])
                        etoks[hh] = SE.do(lambda e, o_=PT[:, slot + SPAN * hh + o:slot + SPAN * (hh + 1)],
                                          i=pp.t[:, o:SPAN], cs=cshift:
                                          e.activation(out=o_, in_=i, func=AF.Exp,
                                                       bias=CSH[:, cs:cs + 1], scale=0.125),
                                          inc=aA)
                        p_mark_r(pp, SE, etoks[hh])
                for hh in range(2):
                    if causal and 128 * kt >= SPAN * sp:
                        VE.wait(aA, etoks[hh])
                        mtok2 = VE.do(lambda e, o_=PT[:, slot + SPAN * hh + o:slot + SPAN * hh + o + 128]:
                                      e.tensor_tensor(o_, o_, TRI[:, :], ALU.mult), inc=vV)
                        ptoks[hh] = (vV, mtok2)
                    else:
                        ptoks[hh] = (aA, etoks[hh])
                if pending is not None:
                    emit_pv(0, pending[0], pending[1], pending[2], pending[3], pending[5])
                    emit_pv(1, pending[0], pending[1], pending[2], pending[4], pending[5])
                pending = (kt, ki, segs, ptoks[0], ptoks[1], slot)
            emit_pv(0, pending[0], pending[1], pending[2], pending[3], pending[5])
            emit_pv(1, pending[0], pending[1], pending[2], pending[4], pending[5])
            if DBG & 1:
                last_ev = vV.n if vV.n > 0 else None
                continue
            spc0, spc1 = SPAN * sp, SPAN * (sp + 1)
            SE.wait(*PV_A.wtok)
            SE.wait(tT, rowA_free[0])
            SE.do(lambda e: e.activation(out=ROWS[64:65, 0:SPAN], in_=PV_A.t[64:65, 0:SPAN], func=AF.Ln))
            SE.wait(*PV_B.wtok)
            SE.wait(tT, rowB_free[0])
            SE.do(lambda e: e.activation(out=RB[64:65, 0:SPAN], in_=PV_B.t[64:65, 0:SPAN], func=AF.Ln))
            SE.raw(lambda e: e.drain())
            r1A = SE.do(lambda e: e.activation(out=ROWS[64:65, 0:SPAN], in_=ROWS[64:65, 0:SPAN], func=AF.Exp, scale=-1.0), inc=aA)
            p_mark_r(PV_A, SE, r1A)
            r1B = SE.do(lambda e: e.activation(out=RB[64:65, 0:SPAN], in_=RB[64:65, 0:SPAN], func=AF.Exp, scale=-1.0), inc=aA)
            p_mark_r(PV_B, SE, r1B)
            for hh, (pv, bp, r1, srow) in enumerate(((PV_A, S_A, r1A, ROWS), (PV_B, S_B, r1B, RB))):
                TE.wait(aA, r1)
                p_write_ready(TE, bp)
                for (a, b) in ((x, min(x + 512, SPAN)) for x in range(0, SPAN, 512)):
                    btok = TE.do(lambda e, bp=bp, a=a, b=b, srow=srow: e.matmul(
                        bp.t[0:64, a:b], ONESF[64:65, 0:64], srow[64:65, a:b],
                        start=True, stop=True, tile_position=(64, 0)), inc=tT)
                p_mark_w(bp, btok)
                if hh == 0:
                    rowA_free[0] = btok
                else:
                    rowB_free[0] = btok
                SE.wait(tT, btok)
                if hh == 1:
                    SE.wait(vV, e1)
                else:
                    SE.wait(vV, rb_free[0])
                ctok = SE.do(lambda e, bp=bp: e.activation(out=RB[0:64, 0:SPAN], in_=bp.t[0:64, 0:SPAN], func=AF.Identity), inc=aA)
                p_mark_r(bp, SE, ctok)
                VE.wait(aA, ctok)
                VE.wait(*pv.wtok)
                if hh == 0:
                    e1 = VE.do(lambda e, d=qsl[0:64, spc0:spc1], pv=pv:
                               e.tensor_tensor(d, pv.t[0:64, 0:SPAN], RB[0:64, 0:SPAN], ALU.mult), inc=vV)
                    p_mark_r(pv, VE, e1)
                else:
                    VE.wait(dF, stg_free[0])
                    e2 = VE.do(lambda e, pv=pv:
                               e.tensor_tensor(STG[0:64, 0:SPAN], pv.t[0:64, 0:SPAN], RB[0:64, 0:SPAN], ALU.mult), inc=vV)
                    p_mark_r(pv, VE, e2)
                    rb_free[0] = e2
                    GE.wait(vV, e2)
                    dF.n += 16
                    stg_free[0] = dF.n
                    GE.ops.append(lambda e, d=qsl[64:128, spc0:spc1]:
                                  e.dma_start(out=d, in_=STG[0:64, 0:SPAN]).then_inc(dF.h, 16))
            last_ev = e2
        return last_ev

    def emit_out_proj(src_buf, w_dram, wslots, bias_tok, ready_toks, store=False):
        last_vtok = None
        pass_mm_tok = [None, None]
        p1_toks = [None] * NT
        if DBG & 2:
            return (vV.n if vV.n else None), [None] * NT, pass_mm_tok
        for p_ in range(2):
            wsem, wcol, afree = wslots[p_]
            src_ap = AP(w_dram, 512 * p_ * D, [[D, 128], [128 * D, 4], [1, D]])
            wtok = dma_w(wsem, WA[:, wcol:wcol + 4096], src_ap, after_tok=afree)
            TE.wait(wsem, wtok)
            if p_ == 0:
                TE.wait(dF, dF.n)
                for s, t in ready_toks:
                    TE.wait(s, t)
            for i in range(NT):
                pp = P[i % 2] if p_ == 0 else P[2 + i % 2]
                p_write_ready(TE, pp)
                for ch4 in range(4):
                    ch = 4 * p_ + ch4
                    lhsT = src_buf[:, ch * T + 128 * i: ch * T + 128 * (i + 1)]
                    for f in range(2):
                        rhs = WA[:, wcol + 1024 * ch4 + 512 * f: wcol + 1024 * ch4 + 512 * (f + 1)]
                        TE.do(lambda e, o=pp.t[:, 512 * f:512 * (f + 1)], l=lhsT, r=rhs,
                              s=(ch4 == 0), p2=(ch4 == 3): e.matmul(o, l, r, start=s, stop=p2))
                mtok = TE.mark(tT)
                p_mark_w(pp, mtok)
                pass_mm_tok[p_] = mtok
                VE.wait(tT, mtok)
                xsl = X[:, 1024 * i:1024 * (i + 1)]
                vt = VE.do(lambda e, x=xsl, i_=pp.t[:, 0:1024]:
                           e.tensor_tensor(x, x, i_, ALU.add), inc=vV)
                p_mark_r(pp, VE, vt)
                last_vtok = vt
                if p_ == 1:
                    p1_toks[i] = vt
            if p_ == 0 and bias_tok is not None:
                VE.wait(aA, bias_tok)
                VE.raw(lambda e: e.drain())
                for i in range(NT):
                    xsl = X[:, 1024 * i:1024 * (i + 1)]
                    VE.do(lambda e, x=xsl: e.tensor_tensor(x, x, BIAS[:, :], ALU.add))
        tile_toks = p1_toks
        if store:
            for i in range(NT):
                SY.wait(vV, tile_toks[i])
                dO.n += 16
                SY.ops.append(lambda e, i=i: e.dma_start(
                    out=AP(d_y, 128 * i * D, [[D, 128], [1, D]]),
                    in_=X[:, 1024 * i:1024 * (i + 1)]).then_inc(dO.h, 16))
        return last_vtok, tile_toks, pass_mm_tok

    GE.wait(dC, tok_consts)
    TE.wait(dC, tok_consts)
    VE.wait(dC, tok_consts)
    SE.wait(dC, tok_consts)
    ln_tok = emit_ln()
    vb_tok = emit_bias_bcast(d_vbrow.ap()[:, :], [])

    pair_free = [None, None]
    v_free = None
    v_tok = None
    last_attn = None
    for c in range(NPAIR):
        wsem = WSL[c % 2]
        wcol = 2048 * (c % 2)
        qtok = dma_w(wsem, WA[:, wcol:wcol + 1024],
                     AP(d_wqkvT, 128 * c, [[3 * D, 128], [3 * D * 128, 8], [1, 128]]),
                     after_tok=pair_free[c % 2])
        ktok = dma_w(wsem, WA[:, wcol + 1024:wcol + 2048],
                     AP(d_wqkvT, D + 128 * c, [[3 * D, 128], [3 * D * 128, 8], [1, 128]]))
        emit_pair_proj(wsem, wcol, QS[:, c * T:(c + 1) * T], c, ktok, [P[2], P[3]],
                       span_evs=ln_evs[0] if c == 0 else None)
        ak = emit_pair_proj(wsem, wcol + 1024, KS[:, (c % 2) * T:(c % 2) * T + T],
                            8 + c, ktok, [P[2], P[3]])
        pair_free[c % 2] = tT.n
        if c in (0, 4):
            vwtok = dma_w(WSL[2], WA[:, 4096:8192],
                          AP(d_wqkvT, 2 * D + 512 * (c // 4),
                             [[3 * D, 128], [3 * D * 128, 8], [1, 512]]),
                          after_tok=v_free)
            v_tok = emit_v_half(4096, c // 4, WSL[2], vwtok, vb_tok if c == 0 else None)
            v_free = tT.n
        last_attn = emit_attention_pair(c, False, 0, ak, v_tok)

    ob_tok = emit_bias_bcast(d_obrow.ap()[:, :], [(vV, v_tok)])
    wo_slots = [(WSL[4], 0, tT.n), (WSL[5], 4096, tT.n)]
    wo_tok, wo_tts, wo_pmt = emit_out_proj(QS, d_woT, wo_slots, ob_tok, [(vV, last_attn)])

    def emit_ffn(prev_tok, bias_reader_tok, store, tile_toks=None, prev_pmt=None):
        ln_t = emit_ln(prev_x_tok=prev_tok, tile_toks=tile_toks)
        wa_free = tT.n
        wa_p0 = prev_pmt[0] if prev_pmt and prev_pmt[0] else wa_free
        wa_p1 = prev_pmt[1] if prev_pmt and prev_pmt[1] else wa_free
        rtok = dma_w(WSL[2], WA[:, 2048:2048 + 2040],
                     AP(d_routT, 0, [[255, 128], [255 * 128, 8], [1, 255]]),
                     after_tok=wa_p0)
        lvtok = dma_w(WSL[3], WA[:, 4096:6144],
                      AP(d_lv, 0, [[D, 128], [D * 128, 2], [1, D]]),
                      after_tok=wa_p1)
        fb_tok = emit_bias_bcast(d_fbrow.ap()[:, :], [(vV, bias_reader_tok)])
        qf_tok = None
        slot_free = [wa_p0, wa_p0]
        for et in range(8):
            wsem = WSL[et % 2]
            wcol = 1024 * (et % 2)
            wtok = dma_w(wsem, WA[:, wcol:wcol + 1024],
                         AP(d_fwqT, 128 * et, [[D, 128], [D * 128, 8], [1, 128]]),
                         after_tok=slot_free[et % 2])
            qf_tok = emit_pair_proj(wsem, wcol, QS[:, et * T:(et + 1) * T], None,
                                    wtok, [P[0], P[1]],
                                    span_evs=ln_evs[0] if et == 0 else None)
            slot_free[et % 2] = tT.n
        TE.wait(WSL[2], rtok)
        TE.wait(WSL[3], lvtok)

        NSPW = T // SP512
        TPS = SP512 // 128
        lp_read = {}
        tr_toks = {}
        sig_tok = {}
        ret_done = {}

        def stage_a(i):
            pz = P[2 + i % 2]
            p_write_ready(TE, pz)
            if i == 0:
                TE.wait(aA, qf_tok)
            for ch in range(8):
                lhsT = QS[:, ch * T + 128 * i: ch * T + 128 * (i + 1)]
                rhs = WA[:, 2048 + 255 * ch: 2048 + 255 * (ch + 1)]
                TE.do(lambda e, o=pz.t[:, 0:255], l=lhsT, r=rhs, s_=(ch == 0), p2=(ch == 7):
                      e.matmul(o, l, r, start=s_, stop=p2))
            ztok = TE.mark(tT)
            p_mark_w(pz, ztok)
            g2o = 512 * (i % 2)
            SE.wait(tT, ztok)
            if i >= 2 and (i - 2) in lp_read:
                SE.wait(vV, lp_read[i - 2])
            SE.do(lambda e, o=SHRD[:, g2o:g2o + 255], i_=pz.t[:, 0:255]:
                  e.activation(out=o, in_=i_, func=AF.Sigmoid, scale=-1.0))
            gtok = SE.do(lambda e, o=SHRD[:, g2o + 255:g2o + 510], i_=pz.t[:, 0:255]:
                         e.activation(out=o, in_=i_, func=AF.Sigmoid), inc=aA)
            p_mark_r(pz, SE, gtok)
            sig_tok[i] = gtok

        def stage_b(i):
            g2o = 512 * (i % 2)
            EN = GE if i % 2 == 0 else VE
            EN.wait(aA, sig_tok[i])
            if i >= 2 and (i - 2) in tr_toks:
                EN.wait(tT, tr_toks[i - 2])
            lpA = 1024 + 512 * (i % 2)
            lpB = lpA + 256
            ptf = 256 * (i % 2)
            EN.do(lambda e, g2o=g2o, lpA=lpA: e.tensor_copy(
                AP(SHRD, lpA, [[2048, 128], [1, 2]]),
                AP(SHRD, g2o, [[2048, 128], [255, 2]])))
            if EN is VE:
                EN.raw(lambda e: e.drain())
            cur, oth = lpA, lpB
            for lv_ in range(1, 8):
                n = 1 << lv_
                last = lv_ == 7
                o_ev = AP(PT, ptf, [[PTW, 128], [2, n]]) if last else AP(SHRD, oth, [[2048, 128], [2, n]])
                o_od = AP(PT, ptf + 1, [[PTW, 128], [2, n]]) if last else AP(SHRD, oth + 1, [[2048, 128], [2, n]])
                EN.do(lambda e, cur=cur, o_=o_ev, n=n, g2o=g2o: e.tensor_tensor(
                    o_, AP(SHRD, cur, [[2048, 128], [1, n]]),
                    AP(SHRD, g2o + n - 1, [[2048, 128], [1, n]]), ALU.mult))
                EN.do(lambda e, cur=cur, o_=o_od, n=n, g2o=g2o: e.tensor_tensor(
                    o_, AP(SHRD, cur, [[2048, 128], [1, n]]),
                    AP(SHRD, g2o + 255 + n - 1, [[2048, 128], [1, n]]), ALU.mult))
                if EN is VE:
                    EN.raw(lambda e: e.drain())
                cur, oth = oth, cur
            lptok = EN.mark(gG if EN is GE else vV)
            lpsem = gG if EN is GE else vV
            pt_ = P[(i + 1) % 2]
            TE.wait(lpsem, lptok)
            p_write_ready(TE, pt_)
            for b in range(2):
                TE.do(lambda e, b=b, pt_=pt_, ptf=ptf: e.matmul(
                    pt_.t[:, 256 + 128 * b:256 + 128 * (b + 1)],
                    PT[:, ptf + 128 * b:ptf + 128 * (b + 1)], IDNB[:, :], start=True, stop=True))
            ttk = TE.mark(tT)
            p_mark_w(pt_, ttk)
            tr_toks[i] = ttk
            VE.wait(tT, ttk)
            dst = AP(KS, 128 * i, [[4 * T, 128], [T, 2], [1, 128]])
            lp_read[i] = VE.do(lambda e, dst=dst, pt_=pt_: e.tensor_copy(dst, pt_.t[:, 256:512]), inc=vV)
            p_mark_r(pt_, VE, lp_read[i])

        def emit_ret(s):
            TE.wait(vV, lp_read[TPS * (s + 1) - 1])
            for dt_ in range(8):
                pp = P[dt_ % 2]
                p_write_ready(TE, pp)
                for m in range(2):
                    lhsT = WA[:, 4096 + 1024 * m + 128 * dt_: 4096 + 1024 * m + 128 * (dt_ + 1)]
                    rhs = KS[:, m * T + SP512 * s: m * T + SP512 * (s + 1)]
                    TE.do(lambda e, o=pp.t[:, 0:SP512], l=lhsT, r=rhs, s_=(m == 0), p2=(m == 1):
                          e.matmul(o, l, r, start=s_, stop=p2))
                mtok = TE.mark(tT)
                p_mark_w(pp, mtok)
                SE.wait(tT, mtok)
                atok = SE.do(lambda e, o=XLN[:, dt_ * T + SP512 * s: dt_ * T + SP512 * (s + 1)],
                             i_=pp.t[:, 0:SP512]: e.activation(out=o, in_=i_, func=AF.Identity),
                             inc=aA)
                p_mark_r(pp, SE, atok)
                ret_done[s] = atok

        for s in range(NSPW):
            for i in range(TPS * s, TPS * (s + 1)):
                stage_a(i)
                if i >= 1:
                    stage_b(i - 1)
            if s >= 1:
                emit_ret(s - 1)
        stage_b(TPS * NSPW - 1)
        emit_ret(NSPW - 1)
        wslots = [(WSL[4], 0, tT.n), (WSL[5], 4096, tT.n)]
        lv_, tts_, pmt_ = emit_out_proj(XLN, d_fwoutT, wslots, fb_tok, [(aA, ret_done[NSPW - 1])], store=store)
        return lv_, tts_, pmt_

    ffn1_res = emit_ffn(wo_tok, wo_tok, store=False, tile_toks=wo_tts, prev_pmt=wo_pmt) if n_sub >= 2 else None
    ffn1_tok, ffn1_tts, ffn1_pmt = ffn1_res if ffn1_res else (None, None, [None, None])

    def emit_attn_b(prev_tok, tile_toks=None, prev_pmt=None):
        ln_t = emit_ln(prev_x_tok=prev_tok, tile_toks=tile_toks)
        wa_free = tT.n
        wa_p0 = prev_pmt[0] if prev_pmt and prev_pmt[0] else wa_free
        wa_p1 = prev_pmt[1] if prev_pmt and prev_pmt[1] else wa_free
        cs_sem = WSL[3]
        pre_q = dma_w(WSL[0], WA[:, 0:4096],
                      AP(d_bwqT, 0, [[D, 128], [D * 128, 8], [1, 512]]),
                      after_tok=wa_p0)
        pre_k = dma_w(WSL[0], WA[:, 4096:8192],
                      AP(d_bwkT, 0, [[D, 128], [D * 128, 8], [1, 512]]),
                      after_tok=wa_p1)
        GE.wait(vV, ln_t)
        ctok = cs_sem.nxt(16)
        GE.ops.append(lambda e: e.dma_start(out=SHRD[:, 0:32 * NT],
                                            in_=d_cos.ap()[:, :]).then_inc(cs_sem.h, 16))
        stok = cs_sem.nxt(16)
        GE.ops.append(lambda e: e.dma_start(out=SHRD[:, 512:512 + 32 * NT],
                                            in_=d_sin.ap()[:, :]).then_inc(cs_sem.h, 16))
        last_attn = None
        quad_free = wa_free
        v_free2 = wa_p0
        rt_by = {}
        tt_by = {}
        VBP = NKT * 520
        ro_buf = [(PT, 0, PTW), (PT, 1024, PTW), (VB, 0, VBP)]
        for quad in range(2):
            wsem = WSL[quad % 2]
            if quad == 0:
                wsem = WSL[0]
                kwtok = pre_k
            else:
                qwtok = dma_w(wsem, WA[:, 0:4096],
                              AP(d_bwqT, 512 * quad, [[D, 128], [D * 128, 8], [1, 512]]),
                              after_tok=v_free2)
                kwtok = dma_w(wsem, WA[:, 4096:8192],
                              AP(d_bwkT, 512 * quad, [[D, 128], [D * 128, 8], [1, 512]]))
            TE.wait(wsem, kwtok)
            VE.wait(cs_sem, stok)
            pair_tok = None

            def flush_pend(i, g, quad=None):
                pt_ = P[i % 2]
                buf, base, _ = ro_buf[g % 3]
                TE.wait(vV, rt_by[g])
                p_write_ready(TE, pt_)
                for b in range(8):
                    TE.do(lambda e, b=b, pt_=pt_, buf=buf, base=base: e.matmul(
                        pt_.t[:, 128 * b:128 * (b + 1)], buf[:, base + 128 * b:base + 128 * (b + 1)],
                        IDNB[:, :], start=True, stop=True))
                ttok = TE.mark(tT)
                p_mark_w(pt_, ttok)
                tt_by[g] = ttok
                qdst = AP(QS, (4 * quad) * T + 128 * i, [[NPAIR * T, 128], [T, 4], [1, 128]])
                kdst = AP(KS, 128 * i, [[4 * T, 128], [T, 4], [1, 128]])
                SE.wait(tT, ttok)
                SE.do(lambda e, d=qdst, pt_=pt_: e.activation(out=d, in_=pt_.t[:, 0:512], func=AF.Identity))
                ptok = SE.do(lambda e, d=kdst, pt_=pt_: e.activation(out=d, in_=pt_.t[:, 512:1024], func=AF.Identity), inc=aA)
                p_mark_r(pt_, SE, ptok)
                return ptok

            for i in range(NT):
                g = quad * NT + i
                pq = P[2 + i % 2]
                if quad == 0:
                    TE.wait(vV, ln_evs[0][i])
                p_write_ready(TE, pq)
                for ch in range(8):
                    lhsT = XLN[:, ch * T + 128 * i: ch * T + 128 * (i + 1)]
                    TE.do(lambda e, o=pq.t[:, 0:512], l=lhsT, r=WA[:, 512 * ch:512 * (ch + 1)],
                          s=(ch == 0), p2=(ch == 7): e.matmul(o, l, r, start=s, stop=p2))
                for ch in range(8):
                    lhsT = XLN[:, ch * T + 128 * i: ch * T + 128 * (i + 1)]
                    TE.do(lambda e, o=pq.t[:, 512:1024], l=lhsT, r=WA[:, 4096 + 512 * ch:4096 + 512 * (ch + 1)],
                          s=(ch == 0), p2=(ch == 7): e.matmul(o, l, r, start=s, stop=p2))
                mtok = TE.mark(tT)
                p_mark_w(pq, mtok)
                EV = 2048 + 1024 * (i % 2)
                SE.wait(tT, mtok)
                if g - 2 in rt_by:
                    SE.wait(vV, rt_by[g - 2])
                pqh = getattr(pq.t, "handle", pq.t)
                pqo = getattr(pq.t, "off", 0)
                ppitch = 2048 if pqh is PAB else 1024
                inev = AP(pqh, pqo, [[ppitch, 128], [64, 16], [2, 32]])
                inod = AP(pqh, pqo + 1, [[ppitch, 128], [64, 16], [2, 32]])
                evev = AP(PT, EV, [[PTW, 128], [64, 16], [1, 32]])
                evod = AP(PT, EV + 32, [[PTW, 128], [64, 16], [1, 32]])
                SE.do(lambda e, o_=evev, i_=inev: e.activation(out=o_, in_=i_, func=AF.Identity))
                a2 = SE.do(lambda e, o_=evod, i_=inod: e.activation(out=o_, in_=i_, func=AF.Identity), inc=aA)
                p_mark_r(pq, SE, a2)
                co = 32 * i
                rbuf, rbase, rpitch = ro_buf[g % 3]
                sin_r = AP(SHRD, 512 + co, [[2048, 128], [0, 16], [1, 32]])
                cos_r = AP(SHRD, co, [[2048, 128], [0, 16], [1, 32]])
                t1 = ROWS[:, 0:512]
                t2 = ROWS[:, 512:1024]
                bev = AP(rbuf, rbase, [[rpitch, 128], [64, 16], [1, 32]])
                bod = AP(rbuf, rbase + 32, [[rpitch, 128], [64, 16], [1, 32]])
                VE.wait(aA, a2)
                if g - 3 in tt_by:
                    VE.wait(tT, tt_by[g - 3])
                VE.do(lambda e, a=t1, b=evev, s=sin_r: e.tensor_tensor(a, b, s, ALU.mult))
                VE.do(lambda e, a=t2, b=evod, s=sin_r: e.tensor_tensor(a, b, s, ALU.mult))
                VE.raw(lambda e: e.drain())
                VE.do(lambda e, a=evev, c_=cos_r: e.tensor_tensor(a, a, c_, ALU.mult))
                VE.do(lambda e, a=evod, c_=cos_r: e.tensor_tensor(a, a, c_, ALU.mult))
                VE.raw(lambda e: e.drain())
                VE.do(lambda e, o_=bev, a=evev, b=t2: e.tensor_tensor(o_, a, b, ALU.subtract))
                VE.do(lambda e, o_=bod, a=evod, b=t1: e.tensor_tensor(o_, a, b, ALU.add))
                rtok = VE.mark(vV)
                rt_by[g] = rtok
                if i >= 2:
                    pair_tok = flush_pend(i - 2, g - 2, quad=quad)
            pair_tok = flush_pend(NT - 2, quad * NT + NT - 2, quad=quad)
            pair_tok = flush_pend(NT - 1, quad * NT + NT - 1, quad=quad)
            quad_free = tT.n
            vwtok = dma_w(WSL[2], WA[:, 0:4096],
                          AP(d_bwvT, 512 * quad, [[D, 128], [D * 128, 8], [1, 512]]),
                          after_tok=quad_free)
            VE.wait(tT, tt_by[quad * NT + NT - 1])
            for kt_ in (0, 1):
                VE.do(lambda e, kt=kt_: e.memset(
                    AP(VB, 520 * kt + 64, [[VBP, 128], [130, 4], [65, 2]]), 1.0))
            v_tok2 = emit_v_half(0, None, WSL[2], vwtok, None)
            v_free2 = tT.n
            for cl in range(4):
                c = 4 * quad + cl
                last_attn = emit_attention_pair(c, True, 1, pair_tok if cl == 0 else None, v_tok2)
        wslots = [(WSL[4], 0, tT.n), (WSL[5], 4096, None)]
        lvb_, ttsb_, pmtb_ = emit_out_proj(QS, d_bwoT, wslots, None, [(vV, last_attn)])
        return lvb_, ttsb_, pmtb_

    def emit_store(last_tok):
        SY.wait(vV, last_tok)
        if DBG & 512:
            GE.wait(aA, aA.n)
            GE.wait(vV, vV.n)
            dO.n += 16
            GE.ops.append(lambda e: e.dma_start(
                out=AP(d_y, 0, [[128, 128], [1, 128]]),
                in_=SML[:, :]).then_inc(dO.h, 16))
            return
        if DBG & 256:
            GE.wait(aA, aA.n)
            GE.wait(vV, vV.n)
            GE.wait(tT, tT.n)
            dO.n += 16
            GE.ops.append(lambda e, PTW=PTW: e.dma_start(
                out=AP(d_y, 0, [[D, 128], [128 * D, PTW // 1024], [1, D]]),
                in_=AP(PT, 0, [[PTW, 128], [1024, PTW // 1024], [1, 1024]])).then_inc(dO.h, 16))
            return
        if DBG & 128:
            GE.wait(aA, aA.n)
            GE.wait(vV, vV.n)
            dO.n += 16
            GE.ops.append(lambda e: e.dma_start(
                out=AP(d_y, 0, [[D, 128], [128 * D, (NCH * T) // 1024], [1, D]]),
                in_=AP(XLN, 0, [[NCH * T, 128], [1024, (NCH * T) // 1024], [1, 1024]])).then_inc(dO.h, 16))
            return
        if DBG & 64:
            GE.wait(aA, aA.n)
            GE.wait(vV, vV.n)
            GE.wait(dF, dF.n)
            dO.n += 16
            GE.ops.append(lambda e: e.dma_start(
                out=AP(d_y, 0, [[D, 128], [128 * D, NPAIR * T // 1024, ], [1, D]]) if False else AP(d_y, 0, [[D, 128], [128 * D, (NPAIR * T) // 1024], [1, D]]),
                in_=AP(QS, 0, [[NPAIR * T, 128], [1024, (NPAIR * T) // 1024], [1, 1024]])).then_inc(dO.h, 16))
            return
        for i in range(NT):
            dO.n += 16
            SY.ops.append(lambda e, i=i: e.dma_start(
                out=AP(d_y, 128 * i * D, [[D, 128], [1, D]]),
                in_=X[:, 1024 * i:1024 * (i + 1)]).then_inc(dO.h, 16))

    if n_sub >= 3:
        attb_tok, attb_tts, attb_pmt = emit_attn_b(ffn1_tok, tile_toks=ffn1_tts, prev_pmt=ffn1_pmt)
    if n_sub >= 4:
        ffn2_tok = emit_ffn(attb_tok, attb_tok, store=True, tile_toks=attb_tts, prev_pmt=attb_pmt)[0]
    elif n_sub == 3:
        emit_store(attb_tok)
    elif n_sub == 2:
        emit_store(ffn1_tok)
    else:
        emit_store(wo_tok)

    SY.raw(lambda e, n=dO.n: e.wait_ge(dO.h, n))

    import os as _os
    if _os.environ.get("KCOUNT"):
        print("opcounts:", {nm: len(EG.ops) for nm, EG in (("t", TE), ("v", VE), ("s", SE), ("g", GE), ("y", SY))})
    _tr = _os.environ.get("KBISECT", "")
    if _tr:
        eng_sel, frac = _tr.split(":")
        frac = float(frac)
        for nm, EG in (("t", TE), ("v", VE), ("s", SE), ("g", GE), ("y", SY)):
            if eng_sel != "all" and nm not in eng_sel:
                EG.ops = []
            else:
                EG.ops = EG.ops[:int(len(EG.ops) * frac)]
    with es:
        block = es.enter_context(nc.Block())

        @block.gpsimd
        def _(g):
            for f in GE.ops:
                f(g)

        @block.vector
        def _(v):
            for f in VE.ops:
                f(v)

        @block.scalar
        def _(s):
            for f in SE.ops:
                f(s)

        @block.tensor
        def _(t):
            for f in TE.ops:
                f(t)

        @block.sync
        def _(s):
            for f in SY.ops:
                f(s)

    return nc


def _prep_consts(T):
    NT = T // 128
    ident = np.eye(128, dtype=np.float32)
    tri = (np.arange(128)[None, :] >= np.arange(128)[:, None]).astype(np.float32)
    pos = np.arange(T, dtype=np.float32)
    pair = np.arange(32, dtype=np.float32)
    rates = (np.float32(10000.0) ** (-(2.0 * pair) / np.float32(64.0))).astype(np.float32)
    ang = (pos[:, None] * rates[None, :]).astype(np.float32)
    cos = np.cos(ang).astype(np.float32)
    sin = np.sin(ang).astype(np.float32)
    cos_t = np.zeros((128, 32 * NT), dtype=np.float32)
    sin_t = np.zeros((128, 32 * NT), dtype=np.float32)
    for i in range(NT):
        cos_t[:, 32 * i:32 * (i + 1)] = cos[128 * i:128 * (i + 1)]
        sin_t[:, 32 * i:32 * (i + 1)] = sin[128 * i:128 * (i + 1)]
    ones = np.ones((128, 1), dtype=np.float32)
    onesf = np.ones((128, 128), dtype=np.float32)
    csh = np.zeros((128, 2), dtype=np.float32)
    csh[:, 0] = -C_SHIFT_A
    csh[:, 1] = -C_SHIFT_B
    return ident, tri, cos_t, sin_t, ones, csh, onesf


def make_in_map(xc, w, T):
    ident, tri, cos_t, sin_t, ones, csh, onesf = _prep_consts(T)
    asc = lambda a: np.ascontiguousarray(a, dtype=np.float32)
    return {
        "x_in": asc(xc),
        "wqkvT": asc(w["L0_attn_a_qkv_w"].T),
        "woT": asc(w["L0_attn_a_out_w"].T),
        "qkvb_t": asc(w["L0_attn_a_qkv_b"].reshape(24, 128).T),
        "vb_row": asc(w["L0_attn_a_qkv_b"][2048:3072].reshape(1, 1024)),
        "ob_row": asc(w["L0_attn_a_out_b"].reshape(1, 1024)),
        "bwqT": asc(w["L0_attn_b_Wq"].T),
        "bwkT": asc(w["L0_attn_b_Wk"].T),
        "bwvT": asc(w["L0_attn_b_Wv"].T),
        "bwoT": asc(w["L0_attn_b_Wo"].T),
        "fwqT": asc(w["L0_ffn_Wq"].T),
        "routT": asc(w["L0_ffn_routers"].T),
        "lv_w": asc(w["L0_ffn_leaf_values"]),
        "fwoutT": asc(w["L0_ffn_Wout_w"].T),
        "fb_row": asc(w["L0_ffn_Wout_b"].reshape(1, 1024)),
        "ident_c": ident, "tri_c": tri, "cos_c": cos_t, "sin_c": sin_t,
        "ones_c": ones, "csh_c": csh, "onesf_c": onesf, "onesb_c": onesf[:, 0:64].copy(),
    }


def kernel(**inputs):
    x = np.asarray(inputs["L0_attn_a_x"], dtype=np.float32)
    B, T, _ = x.shape
    w = {k: np.asarray(v, dtype=np.float32) for k, v in inputs.items()
         if k != "L0_attn_a_x"}
    nc = build_nc(T)
    in_maps = [make_in_map(x[i], w, T) for i in range(B)]
    res = run_bass_kernel_spmd(nc, in_maps, core_ids=list(range(B)))
    out = np.stack([np.asarray(res.results[i]["y_out"]) for i in range(B)], axis=0)
    return out.astype(np.float32)


# version 32
# speedup vs baseline: 1.0284x; 1.0012x over previous
import contextlib
import numpy as np

import concourse.bass as bass
import concourse.mybir as mybir
from concourse.bass_utils import run_bass_kernel_spmd

F32 = mybir.dt.float32
BF16 = mybir.dt.bfloat16
AF = mybir.ActivationFunctionType
ALU = mybir.AluOpType

D = 1024
NCH = 8
NPAIR = 8
C_SHIFT_A = 12.0
C_SHIFT_B = 26.0

AP = bass.AP
DBG = 0


class Sem:
    def __init__(self, h):
        self.h = h
        self.n = 0

    def nxt(self, k=1):
        self.n += k
        return self.n


class Eng:

    def __init__(self, name):
        self.name = name
        self.ops = []

    def do(self, fn, inc=None, k=1):
        if inc is None:
            self.ops.append(fn)
            return None
        tok = inc.nxt(k)
        self.ops.append(lambda e, fn=fn, sem=inc.h, k=k: fn(e).then_inc(sem, k))
        return tok

    def mark(self, sem):
        tok = sem.nxt(1)
        prev = self.ops[-1]

        def wrapped(e, p=prev, s=sem.h):
            r = p(e)
            assert r is not None, "mark() must follow an instruction-emitting op"
            return r.then_inc(s, 1)

        self.ops[-1] = wrapped
        return tok

    def wait(self, sem, tok):
        if tok is None:
            return
        self.ops.append(lambda e, s=sem.h, t=tok: e.wait_ge(s, t))

    def raw(self, fn):
        self.ops.append(fn)


class Psum:
    def __init__(self, t):
        self.t = t
        self.wtok = None
        self.rtoks = []


def build_nc(T, n_sub=4, span_ovr=None):
    NT = T // 128
    SPAN = span_ovr or min(1024, T)
    NSP = T // SPAN
    NKT = T // 128
    SP512 = min(512, T)
    NSP512 = T // SP512
    PTW = max(4 * SPAN, 4096)

    nc = bass.Bass("TRN2", target_bir_lowering=False, detect_race_conditions=False)

    dram = {}

    def dr(name, shape, kind="ExternalInput"):
        dram[name] = nc.dram_tensor(name, shape, F32, kind=kind)
        return dram[name]

    d_x = dr("x_in", [T, D])
    d_y = dr("y_out", [T, D], "ExternalOutput")
    d_wqkvT = dr("wqkvT", [D, 3 * D])
    d_woT = dr("woT", [D, D])
    d_qkvb = dr("qkvb_t", [128, 24])
    d_vbrow = dr("vb_row", [1, D])
    d_obrow = dr("ob_row", [1, D])
    d_bwqT = dr("bwqT", [D, D])
    d_bwkT = dr("bwkT", [D, D])
    d_bwvT = dr("bwvT", [D, D])
    d_bwoT = dr("bwoT", [D, D])
    d_fwqT = dr("fwqT", [D, D])
    d_routT = dr("routT", [D, 255])
    d_lv = dr("lv_w", [256, D])
    d_fwoutT = dr("fwoutT", [D, D])
    d_fbrow = dr("fb_row", [1, D])
    d_ident = dr("ident_c", [128, 128])
    d_tri = dr("tri_c", [128, 128])
    d_cos = dr("cos_c", [128, 32 * NT])
    d_sin = dr("sin_c", [128, 32 * NT])
    d_ones = dr("ones_c", [128, 1])
    d_onesf = dr("onesf_c", [128, 128])
    d_onesb = dr("onesb_c", [128, 64])
    d_csh = dr("csh_c", [128, 2])

    es = contextlib.ExitStack()
    sb = lambda name, shape, dt: es.enter_context(nc.sbuf_tensor(name, shape, dt))
    sem = lambda name: Sem(es.enter_context(nc.semaphore(name)))

    X = sb("X", [128, NT * 1024], F32)
    SHRD = sb("SHRD", [128, 2048], F32)
    BIAS = sb("BIAS", [128, 1024], BF16)
    RB = sb("RB", [128, SPAN], F32)
    ROWS = sb("ROWS", [128, 1024], F32)
    SML = sb("SML", [128, 128], F32)
    STAT = sb("STAT", [128, 12 * 16], F32)
    QKVB = sb("QKVB", [128, 24], F32)
    XLN = sb("XLN", [128, NCH * T], BF16)
    QS = sb("QS", [128, NPAIR * T], BF16)
    KS = sb("KS", [128, 4 * T], BF16)
    VB = sb("VB", [128, NKT * 520], BF16)
    PT = sb("PT", [128, PTW], BF16)
    WA = sb("WA", [128, 8192], BF16)
    IDNB = sb("IDNB", [128, 128], BF16)
    TRI = sb("TRI", [128, 128], BF16)
    ONES = sb("ONES", [128, 1], BF16)
    ONESF = sb("ONESF", [128, 128], F32)
    ONESB = sb("ONESB", [128, 64], BF16)
    CSH = sb("CSH", [128, 2], F32)
    STG = sb("STG", [128, SPAN], BF16)

    class TV:

        def __init__(self, t, off):
            self._t = t
            self._o = off
            self.handle = t
            self.off = off

        def __getitem__(self, idx):
            r, c = idx
            return self._t[r, c.start + self._o:c.stop + self._o]

    PAB = es.enter_context(nc.psum_tensor("PAB", [128, 2048], F32))
    P = [Psum(TV(PAB, 0)), Psum(TV(PAB, 1024)),
         Psum(es.enter_context(nc.psum_tensor("P2", [128, 1024], F32))),
         Psum(es.enter_context(nc.psum_tensor("P3", [128, 1024], F32)))]

    tT, aA, vV, gG = sem("tT"), sem("aA"), sem("vV"), sem("gG")
    dX, dC, dO, dF = sem("dX"), sem("dC"), sem("dO"), sem("dF")
    WSL = [sem(f"ws{i}") for i in range(8)]

    TE, VE, SE, GE, SY = Eng("t"), Eng("v"), Eng("s"), Eng("g"), Eng("y")
    MARK = {id(TE): tT, id(VE): vV, id(SE): aA, id(GE): gG}

    def p_write_ready(eng, p):
        for s, t in p.rtoks:
            eng.wait(s, t)
        p.rtoks = []

    def p_read_ready(eng, p):
        if p.wtok is not None:
            eng.wait(*p.wtok)

    def p_mark_w(p, tok):
        p.wtok = (tT, tok)

    def p_mark_r(p, eng, tok):
        p.rtoks.append((MARK[id(eng)], tok))

    def dma_w(slot_sem, dst_ap, src_ap, after_tok=None):
        if after_tok is not None:
            GE.wait(tT, after_tok)
        tok = slot_sem.nxt(16)
        GE.ops.append(lambda e, d=dst_ap, s=src_ap, sm=slot_sem.h:
                      e.dma_start(out=d, in_=s).then_inc(sm, 16))
        return tok


    def dma_c(dst, src):
        GE.ops.append(lambda e, d=dst, s=src: e.dma_start(out=d, in_=s).then_inc(dC.h, 16))
        dC.n += 16

    dma_c(IDNB[:, :], d_ident.ap()[:, :])
    dma_c(TRI[:, :], d_tri.ap()[:, :])
    dma_c(ONES[:, :], d_ones.ap()[:, :])
    dma_c(QKVB[:, :], d_qkvb.ap()[:, :])
    dma_c(CSH[:, :], d_csh.ap()[:, :])
    dma_c(ONESF[:, :], d_onesf.ap()[:, :])
    dma_c(ONESB[:, :], d_onesb.ap()[:, :])
    tok_consts = dC.n

    for _kt in range(NKT):
        VE.ops.append(lambda e, kt=_kt: e.memset(
            AP(VB, 520 * kt + 64, [[NKT * 520, 128], [130, 4], [65, 2]]), 1.0))
    NXC = min(4, NT)
    TPC = NT // NXC
    x_chunk_tok = []
    for c_ in range(NXC):
        SY.ops.append(lambda e, c_=c_: e.dma_start(
            out=X[:, 1024 * TPC * c_:1024 * TPC * (c_ + 1)],
            in_=AP(d_x, 128 * TPC * c_ * D, [[D, 128], [128 * D, TPC], [1, D]])).then_inc(dX.h, 16))
        dX.n += 16
        x_chunk_tok.append(dX.n)
    tok_x = dX.n

    ln_evs = [None]

    def emit_ln(prev_x_tok=None, tile_toks=None):
        if DBG & 16:
            VE.wait(dX, tok_x)
            mtk = VE.do(lambda e: e.memset(SML[:, 0:1], 0.0), inc=vV)
            ln_evs[0] = [mtk] * NT
            return mtk
        first = prev_x_tok is None and tile_toks is None
        if not first:
            VE.wait(dX, tok_x)
        if tile_toks is None and prev_x_tok is not None:
            VE.wait(vV, prev_x_tok)
        VE.raw(lambda e: e.drain())
        for i in range(NT):
            xt = X[:, 1024 * i:1024 * (i + 1)]
            if tile_toks is not None:
                VE.wait(vV, tile_toks[i])
            if first:
                VE.wait(dX, x_chunk_tok[i // TPC])
            VE.do(lambda e, xt=xt, i=i: e.bn_stats(STAT[:, 12 * i:12 * i + 6], xt[:, 0:512]))
            VE.do(lambda e, xt=xt, i=i: e.bn_stats(STAT[:, 12 * i + 6:12 * i + 12], xt[:, 512:1024]))
        VE.raw(lambda e: e.drain())
        for i in range(NT):
            VE.do(lambda e, i=i: e.bn_aggr(SML[:, 80 + 2 * i:80 + 2 * i + 2], STAT[:, 12 * i:12 * i + 12]))
        VE.raw(lambda e: e.drain())
        VE.do(lambda e: e.tensor_scalar(SML[:, 0:NT], AP(SML, 81, [[128, 128], [2, NT]]),
                                        1e-5, None, ALU.add))
        vtok = VE.mark(vV)
        SE.wait(vV, vtok)
        stok = SE.do(lambda e: e.activation(out=SML[:, 16:16 + NT], in_=SML[:, 0:NT], func=AF.Sqrt), inc=aA)
        VE.wait(aA, stok)
        VE.raw(lambda e: e.drain())
        VE.do(lambda e: e.reciprocal(SML[:, 32:32 + NT], SML[:, 16:16 + NT]))
        VE.raw(lambda e: e.drain())
        for _ in range(2):
            VE.do(lambda e: e.tensor_tensor(SML[:, 48:48 + NT], SML[:, 32:32 + NT], SML[:, 32:32 + NT], ALU.mult))
            VE.raw(lambda e: e.drain())
            VE.do(lambda e: e.tensor_tensor(SML[:, 48:48 + NT], SML[:, 48:48 + NT], SML[:, 0:NT], ALU.mult))
            VE.raw(lambda e: e.drain())
            VE.do(lambda e: e.tensor_scalar(SML[:, 48:48 + NT], SML[:, 48:48 + NT], -0.5, 1.5, ALU.mult, ALU.add))
            VE.raw(lambda e: e.drain())
            VE.do(lambda e: e.tensor_tensor(SML[:, 32:32 + NT], SML[:, 32:32 + NT], SML[:, 48:48 + NT], ALU.mult))
            VE.raw(lambda e: e.drain())
        VE.do(lambda e: e.tensor_tensor(SML[:, 64:64 + NT], AP(SML, 80, [[128, 128], [2, NT]]),
                                        SML[:, 32:32 + NT], ALU.mult))
        VE.raw(lambda e: e.drain())
        VE.do(lambda e: e.tensor_scalar(SML[:, 64:64 + NT], SML[:, 64:64 + NT], -1.0, None, ALU.mult))
        ntok = VE.mark(vV)
        tr_tok = [None, None]
        last_ev = None
        ev_sems = []
        SE.wait(vV, ntok)
        for i in range(NT):
            xt = X[:, 1024 * i:1024 * (i + 1)]
            half = 1024 * (i % 2)
            xo = PT[:, half:half + 1024]
            if tr_tok[i % 2] is not None:
                SE.wait(tT, tr_tok[i % 2])
            atok = SE.do(lambda e, xt=xt, xo=xo, i=i: e.activation(
                out=xo, in_=xt, func=AF.Identity, bias=SML[:, 64 + i:65 + i],
                scale=SML[:, 32 + i:33 + i]), inc=aA)
            pt = P[i % 2]
            TE.wait(aA, atok)
            p_write_ready(TE, pt)
            for c in range(8):
                TE.do(lambda e, xo=xo, c=c, pt=pt: e.matmul(
                    pt.t[:, 128 * c:128 * (c + 1)], xo[:, 128 * c:128 * (c + 1)],
                    IDNB[:, :], start=True, stop=True))
            ttok = TE.mark(tT)
            p_mark_w(pt, ttok)
            tr_tok[i % 2] = ttok
            dst = AP(XLN, 128 * i, [[NCH * T, 128], [T, 8], [1, 128]])
            VE.wait(tT, ttok)
            etok = VE.do(lambda e, pt=pt, dst=dst: e.tensor_copy(dst, pt.t[:, 0:1024]), inc=vV)
            p_mark_r(pt, VE, etok)
            ev_sems.append(etok)
            last_ev = etok
        ln_evs[0] = ev_sems
        return last_ev

    rows_free = [None]
    rb_free = [None]
    stg_free = [0]
    rowA_free = [None]
    rowB_free = [None]

    def emit_bias_bcast(src_row, reader_toks):
        if DBG & 32:
            return None
        s = WSL[6]
        if rows_free[0] is not None:
            GE.wait(tT, rows_free[0])
        tok = s.nxt(16)
        GE.ops.append(lambda e, sr=src_row: e.dma_start(
            out=ROWS[0:1, :], in_=sr).then_inc(s.h, 16))
        pp = P[0]
        TE.wait(WSL[6], tok)
        p_write_ready(TE, pp)
        for f in range(2):
            btok = TE.do(lambda e, f=f: e.matmul(pp.t[:, 512 * f:512 * (f + 1)],
                                                 ONESF[0:1, 0:128],
                                                 ROWS[0:1, 512 * f:512 * (f + 1)],
                                                 start=True, stop=True), inc=tT)
        p_mark_w(pp, btok)
        rows_free[0] = btok
        SE.wait(tT, btok)
        for st, tk in reader_toks:
            SE.wait(st, tk)
        atok = SE.do(lambda e: e.activation(out=BIAS[:, :], in_=pp.t[:, 0:1024], func=AF.Identity), inc=aA)
        p_mark_r(pp, SE, atok)
        return atok

    def emit_pair_proj(wsem, wcol, dst_sl, bias_col, wtok, pps, extra_wait=None,
                       span_evs=None):
        if DBG & 8:
            return None
        TE.wait(wsem, wtok)
        if extra_wait is not None:
            TE.wait(*extra_wait)
        last_atok = None
        tps = SP512 // 128
        for sp in range(NSP512):
            pp = pps[sp % len(pps)]
            half = SP512 * ((sp // len(pps)) % 2) if SP512 == 512 else 0
            if span_evs is not None:
                TE.wait(vV, span_evs[min(tps * (sp + 1) - 1, NT - 1)])
            p_write_ready(TE, pp)
            for ch in range(8):
                lhsT = WA[:, wcol + 128 * ch: wcol + 128 * (ch + 1)]
                rhs = XLN[:, ch * T + SP512 * sp: ch * T + SP512 * (sp + 1)]
                TE.do(lambda e, o=pp.t[:, half:half + SP512], l=lhsT, r=rhs,
                      s=(ch == 0), p2=(ch == 7): e.matmul(o, l, r, start=s, stop=p2))
            mtok = TE.mark(tT)
            p_mark_w(pp, mtok)
            SE.wait(tT, mtok)
            out = dst_sl[:, SP512 * sp:SP512 * (sp + 1)]
            if bias_col is not None:
                atok = SE.do(lambda e, o=out, i=pp.t[:, half:half + SP512],
                             b=QKVB[:, bias_col:bias_col + 1]:
                             e.activation(out=o, in_=i, func=AF.Identity, bias=b), inc=aA)
            else:
                atok = SE.do(lambda e, o=out, i=pp.t[:, half:half + SP512]:
                             e.activation(out=o, in_=i, func=AF.Identity), inc=aA)
            p_mark_r(pp, SE, atok)
            last_atok = atok
        return last_atok

    def emit_v_half(wcol, bias_half, wsem, wtok, bias_tok):
        if DBG & 8:
            return None
        TE.wait(wsem, wtok)
        last = None
        for kt in range(NKT):
            pp = P[kt % 2]
            p_write_ready(TE, pp)
            for ch in range(8):
                lhsT = XLN[:, ch * T + 128 * kt: ch * T + 128 * (kt + 1)]
                rhs = WA[:, wcol + 512 * ch: wcol + 512 * (ch + 1)]
                TE.do(lambda e, o=pp.t[:, 0:512], l=lhsT, r=rhs, s=(ch == 0), p2=(ch == 7):
                      e.matmul(o, l, r, start=s, stop=p2))
            mtok = TE.mark(tT)
            p_mark_w(pp, mtok)
            VE.wait(tT, mtok)
            if bias_tok is not None and kt == 0:
                VE.wait(aA, bias_tok)
            dst = AP(VB, 520 * kt, [[NKT * 520, 128], [130, 4], [65, 2], [1, 64]])
            srcp = AP(pp.t.handle, pp.t.off, [[2048, 128], [128, 4], [64, 2], [1, 64]])
            if bias_half is not None:
                bsrc = AP(BIAS, 512 * bias_half, [[1024, 128], [128, 4], [64, 2], [1, 64]])
                vtok = VE.do(lambda e, d=dst, i=srcp, b=bsrc:
                             e.tensor_tensor(d, i, b, ALU.add), inc=vV)
            else:
                vtok = VE.do(lambda e, d=dst, i=srcp: e.tensor_copy(d, i), inc=vV)
            p_mark_r(pp, VE, vtok)
            last = vtok
        return last

    def emit_attention_pair(c, causal, cshift, qk_tok, v_tok):
        if DBG & 4:
            return v_tok
        qsl = QS[:, c * T:(c + 1) * T]
        koff = ((c % 4) if causal else (c % 2)) * T
        ksl = KS[:, koff:koff + T]
        vbase = 130 * (c % 4)
        S_A, S_B, PV_A, PV_B = P[0], P[1], P[2], P[3]
        TE.wait(vV, v_tok)
        if qk_tok is not None:
            TE.wait(aA, qk_tok)
        last_ev = None
        for sp in range(NSP):
            kts = [kt for kt in range(NKT)
                   if not (causal and 128 * kt > SPAN * sp + SPAN - 1)]
            def _off(kt):
                return max(0, 128 * kt - SPAN * sp) if causal else 0
            last_kt_bank = {}
            for bi in range((SPAN + 511) // 512):
                contrib = [kt for kt in kts if _off(kt) < 512 * bi + 512]
                last_kt_bank[bi] = contrib[-1]
            def emit_pv(hh, kt, ki, segs, ptok, slot):
                pv = PV_A if hh == 0 else PV_B
                TE.wait(*ptok)
                if ki == 0 and hh == 0:
                    p_write_ready(TE, PV_A)
                    p_write_ready(TE, PV_B)
                vsl = VB[:, 520 * kt + vbase + 65 * hh:520 * kt + vbase + 65 * hh + 65]
                st = (ki == 0)
                for (a, b) in segs:
                    prt = PT[:, slot + SPAN * hh + a:slot + SPAN * hh + b]
                    fin = (kt == last_kt_bank[a // 512])
                    TE.do(lambda e, o_=pv.t[0:65, a:b], l=vsl, r=prt, s=st, f=fin:
                          e.matmul(o_, l, r, start=s, stop=f))
                pvtok = TE.mark(tT)
                p_mark_w(pv, pvtok)

            pending = None
            for ki, kt in enumerate(kts):
                o = max(0, 128 * kt - SPAN * sp) if causal else 0
                segs = []
                n0 = o
                while n0 < SPAN:
                    n1 = min((n0 // 512 + 1) * 512, SPAN)
                    segs.append((n0, n1))
                    n0 = n1
                slot = 2 * SPAN * (kt % 2)
                mtoks = [None, None]
                for hh in range(2):
                    pp = S_A if hh == 0 else S_B
                    rowlo = 64 * hh
                    p_write_ready(TE, pp)
                    lhsT = ksl[rowlo:rowlo + 64, 128 * kt:128 * (kt + 1)]
                    for (a, b) in segs:
                        rhs = qsl[rowlo:rowlo + 64, SPAN * sp + a:SPAN * sp + b]
                        TE.do(lambda e, o_=pp.t[0:128, a:b], l=lhsT, r=rhs, tp=(rowlo, 0):
                              e.matmul(o_, l, r, start=True, stop=True, tile_position=tp))
                    mtoks[hh] = TE.mark(tT)
                    p_mark_w(pp, mtoks[hh])
                ptoks = [None, None]
                if False:
                    SE.wait(tT, mtoks[1])
                    etok = SE.do(lambda e, o_=PT[:, slot:slot + 2 * SPAN],
                                 i=PAB[:, 0:2 * SPAN], cs=cshift:
                                 e.activation(out=o_, in_=i, func=AF.Exp,
                                              bias=CSH[:, cs:cs + 1], scale=0.125),
                                 inc=aA)
                    p_mark_r(S_A, SE, etok)
                    p_mark_r(S_B, SE, etok)
                    etoks = [etok, etok]
                else:
                    etoks = [None, None]
                    for hh in range(2):
                        pp = S_A if hh == 0 else S_B
                        SE.wait(tT, mtoks[hh])
                        etoks[hh] = SE.do(lambda e, o_=PT[:, slot + SPAN * hh + o:slot + SPAN * (hh + 1)],
                                          i=pp.t[:, o:SPAN], cs=cshift:
                                          e.activation(out=o_, in_=i, func=AF.Exp,
                                                       bias=CSH[:, cs:cs + 1], scale=0.125),
                                          inc=aA)
                        p_mark_r(pp, SE, etoks[hh])
                for hh in range(2):
                    if causal and 128 * kt >= SPAN * sp:
                        VE.wait(aA, etoks[hh])
                        mtok2 = VE.do(lambda e, o_=PT[:, slot + SPAN * hh + o:slot + SPAN * hh + o + 128]:
                                      e.tensor_tensor(o_, o_, TRI[:, :], ALU.mult), inc=vV)
                        ptoks[hh] = (vV, mtok2)
                    else:
                        ptoks[hh] = (aA, etoks[hh])
                if pending is not None:
                    emit_pv(0, pending[0], pending[1], pending[2], pending[3], pending[5])
                    emit_pv(1, pending[0], pending[1], pending[2], pending[4], pending[5])
                pending = (kt, ki, segs, ptoks[0], ptoks[1], slot)
            emit_pv(0, pending[0], pending[1], pending[2], pending[3], pending[5])
            emit_pv(1, pending[0], pending[1], pending[2], pending[4], pending[5])
            if DBG & 1:
                last_ev = vV.n if vV.n > 0 else None
                continue
            spc0, spc1 = SPAN * sp, SPAN * (sp + 1)
            SE.wait(*PV_A.wtok)
            SE.wait(tT, rowA_free[0])
            SE.do(lambda e: e.activation(out=ROWS[64:65, 0:SPAN], in_=PV_A.t[64:65, 0:SPAN], func=AF.Ln))
            SE.wait(*PV_B.wtok)
            SE.wait(tT, rowB_free[0])
            SE.do(lambda e: e.activation(out=RB[64:65, 0:SPAN], in_=PV_B.t[64:65, 0:SPAN], func=AF.Ln))
            SE.raw(lambda e: e.drain())
            r1A = SE.do(lambda e: e.activation(out=ROWS[64:65, 0:SPAN], in_=ROWS[64:65, 0:SPAN], func=AF.Exp, scale=-1.0), inc=aA)
            p_mark_r(PV_A, SE, r1A)
            r1B = SE.do(lambda e: e.activation(out=RB[64:65, 0:SPAN], in_=RB[64:65, 0:SPAN], func=AF.Exp, scale=-1.0), inc=aA)
            p_mark_r(PV_B, SE, r1B)
            for hh, (pv, bp, r1, srow) in enumerate(((PV_A, S_A, r1A, ROWS), (PV_B, S_B, r1B, RB))):
                TE.wait(aA, r1)
                p_write_ready(TE, bp)
                for (a, b) in ((x, min(x + 512, SPAN)) for x in range(0, SPAN, 512)):
                    btok = TE.do(lambda e, bp=bp, a=a, b=b, srow=srow: e.matmul(
                        bp.t[0:64, a:b], ONESF[64:65, 0:64], srow[64:65, a:b],
                        start=True, stop=True, tile_position=(64, 0)), inc=tT)
                p_mark_w(bp, btok)
                if hh == 0:
                    rowA_free[0] = btok
                else:
                    rowB_free[0] = btok
                SE.wait(tT, btok)
                if hh == 1:
                    SE.wait(vV, e1)
                else:
                    SE.wait(vV, rb_free[0])
                ctok = SE.do(lambda e, bp=bp: e.activation(out=RB[0:64, 0:SPAN], in_=bp.t[0:64, 0:SPAN], func=AF.Identity), inc=aA)
                p_mark_r(bp, SE, ctok)
                VE.wait(aA, ctok)
                VE.wait(*pv.wtok)
                if hh == 0:
                    e1 = VE.do(lambda e, d=qsl[0:64, spc0:spc1], pv=pv:
                               e.tensor_tensor(d, pv.t[0:64, 0:SPAN], RB[0:64, 0:SPAN], ALU.mult), inc=vV)
                    p_mark_r(pv, VE, e1)
                else:
                    VE.wait(dF, stg_free[0])
                    e2 = VE.do(lambda e, pv=pv:
                               e.tensor_tensor(STG[0:64, 0:SPAN], pv.t[0:64, 0:SPAN], RB[0:64, 0:SPAN], ALU.mult), inc=vV)
                    p_mark_r(pv, VE, e2)
                    rb_free[0] = e2
                    GE.wait(vV, e2)
                    dF.n += 16
                    stg_free[0] = dF.n
                    GE.ops.append(lambda e, d=qsl[64:128, spc0:spc1]:
                                  e.dma_start(out=d, in_=STG[0:64, 0:SPAN]).then_inc(dF.h, 16))
            last_ev = e2
        return last_ev

    def emit_out_proj(src_buf, w_dram, wslots, bias_tok, ready_toks, store=False):
        last_vtok = None
        pass_mm_tok = [None, None]
        p1_toks = [None] * NT
        if DBG & 2:
            return (vV.n if vV.n else None), [None] * NT, pass_mm_tok
        for p_ in range(2):
            wsem, wcol, afree = wslots[p_]
            src_ap = AP(w_dram, 512 * p_ * D, [[D, 128], [128 * D, 4], [1, D]])
            wtok = dma_w(wsem, WA[:, wcol:wcol + 4096], src_ap, after_tok=afree)
            TE.wait(wsem, wtok)
            if p_ == 0:
                TE.wait(dF, dF.n)
                for s, t in ready_toks:
                    TE.wait(s, t)
            for i in range(NT):
                pp = P[i % 2] if p_ == 0 else P[2 + i % 2]
                p_write_ready(TE, pp)
                for ch4 in range(4):
                    ch = 4 * p_ + ch4
                    lhsT = src_buf[:, ch * T + 128 * i: ch * T + 128 * (i + 1)]
                    for f in range(2):
                        rhs = WA[:, wcol + 1024 * ch4 + 512 * f: wcol + 1024 * ch4 + 512 * (f + 1)]
                        TE.do(lambda e, o=pp.t[:, 512 * f:512 * (f + 1)], l=lhsT, r=rhs,
                              s=(ch4 == 0), p2=(ch4 == 3): e.matmul(o, l, r, start=s, stop=p2))
                mtok = TE.mark(tT)
                p_mark_w(pp, mtok)
                pass_mm_tok[p_] = mtok
                VE.wait(tT, mtok)
                xsl = X[:, 1024 * i:1024 * (i + 1)]
                vt = VE.do(lambda e, x=xsl, i_=pp.t[:, 0:1024]:
                           e.tensor_tensor(x, x, i_, ALU.add), inc=vV)
                p_mark_r(pp, VE, vt)
                last_vtok = vt
                if p_ == 1:
                    p1_toks[i] = vt
            if p_ == 0 and bias_tok is not None:
                VE.wait(aA, bias_tok)
                VE.raw(lambda e: e.drain())
                for i in range(NT):
                    xsl = X[:, 1024 * i:1024 * (i + 1)]
                    VE.do(lambda e, x=xsl: e.tensor_tensor(x, x, BIAS[:, :], ALU.add))
        tile_toks = p1_toks
        if store:
            for i in range(NT):
                SY.wait(vV, tile_toks[i])
                dO.n += 16
                SY.ops.append(lambda e, i=i: e.dma_start(
                    out=AP(d_y, 128 * i * D, [[D, 128], [1, D]]),
                    in_=X[:, 1024 * i:1024 * (i + 1)]).then_inc(dO.h, 16))
        return last_vtok, tile_toks, pass_mm_tok

    GE.wait(dC, tok_consts)
    TE.wait(dC, tok_consts)
    VE.wait(dC, tok_consts)
    SE.wait(dC, tok_consts)
    ln_tok = emit_ln()
    vb_tok = emit_bias_bcast(d_vbrow.ap()[:, :], [])

    pair_free = [None, None]
    v_free = None
    v_tok = None
    last_attn = None
    for c in range(NPAIR):
        wsem = WSL[c % 2]
        wcol = 2048 * (c % 2)
        qtok = dma_w(wsem, WA[:, wcol:wcol + 1024],
                     AP(d_wqkvT, 128 * c, [[3 * D, 128], [3 * D * 128, 8], [1, 128]]),
                     after_tok=pair_free[c % 2])
        ktok = dma_w(wsem, WA[:, wcol + 1024:wcol + 2048],
                     AP(d_wqkvT, D + 128 * c, [[3 * D, 128], [3 * D * 128, 8], [1, 128]]))
        emit_pair_proj(wsem, wcol, QS[:, c * T:(c + 1) * T], c, ktok, [P[2], P[3]],
                       span_evs=ln_evs[0] if c == 0 else None)
        ak = emit_pair_proj(wsem, wcol + 1024, KS[:, (c % 2) * T:(c % 2) * T + T],
                            8 + c, ktok, [P[2], P[3]])
        pair_free[c % 2] = tT.n
        if c in (0, 4):
            vwtok = dma_w(WSL[2], WA[:, 4096:8192],
                          AP(d_wqkvT, 2 * D + 512 * (c // 4),
                             [[3 * D, 128], [3 * D * 128, 8], [1, 512]]),
                          after_tok=v_free)
            v_tok = emit_v_half(4096, c // 4, WSL[2], vwtok, vb_tok if c == 0 else None)
            v_free = tT.n
        last_attn = emit_attention_pair(c, False, 0, ak, v_tok)

    ob_tok = emit_bias_bcast(d_obrow.ap()[:, :], [(vV, v_tok)])
    wo_slots = [(WSL[4], 0, pair_free[1]), (WSL[5], 4096, v_free)]
    wo_tok, wo_tts, wo_pmt = emit_out_proj(QS, d_woT, wo_slots, ob_tok, [(vV, last_attn)])

    def emit_ffn(prev_tok, bias_reader_tok, store, tile_toks=None, prev_pmt=None):
        ln_t = emit_ln(prev_x_tok=prev_tok, tile_toks=tile_toks)
        wa_free = tT.n
        wa_p0 = prev_pmt[0] if prev_pmt and prev_pmt[0] else wa_free
        wa_p1 = prev_pmt[1] if prev_pmt and prev_pmt[1] else wa_free
        rtok = dma_w(WSL[2], WA[:, 2048:2048 + 2040],
                     AP(d_routT, 0, [[255, 128], [255 * 128, 8], [1, 255]]),
                     after_tok=wa_p0)
        lvtok = dma_w(WSL[3], WA[:, 4096:6144],
                      AP(d_lv, 0, [[D, 128], [D * 128, 2], [1, D]]),
                      after_tok=wa_p1)
        fb_tok = emit_bias_bcast(d_fbrow.ap()[:, :], [(vV, bias_reader_tok)])
        qf_tok = None
        slot_free = [wa_p0, wa_p0]
        for et in range(8):
            wsem = WSL[et % 2]
            wcol = 1024 * (et % 2)
            wtok = dma_w(wsem, WA[:, wcol:wcol + 1024],
                         AP(d_fwqT, 128 * et, [[D, 128], [D * 128, 8], [1, 128]]),
                         after_tok=slot_free[et % 2])
            qf_tok = emit_pair_proj(wsem, wcol, QS[:, et * T:(et + 1) * T], None,
                                    wtok, [P[0], P[1]],
                                    span_evs=ln_evs[0] if et == 0 else None)
            slot_free[et % 2] = tT.n
        TE.wait(WSL[2], rtok)
        TE.wait(WSL[3], lvtok)

        NSPW = T // SP512
        TPS = SP512 // 128
        lp_read = {}
        tr_toks = {}
        sig_tok = {}
        ret_done = {}

        def stage_a(i):
            pz = P[2 + i % 2]
            p_write_ready(TE, pz)
            if i == 0:
                TE.wait(aA, qf_tok)
            for ch in range(8):
                lhsT = QS[:, ch * T + 128 * i: ch * T + 128 * (i + 1)]
                rhs = WA[:, 2048 + 255 * ch: 2048 + 255 * (ch + 1)]
                TE.do(lambda e, o=pz.t[:, 0:255], l=lhsT, r=rhs, s_=(ch == 0), p2=(ch == 7):
                      e.matmul(o, l, r, start=s_, stop=p2))
            ztok = TE.mark(tT)
            p_mark_w(pz, ztok)
            g2o = 512 * (i % 2)
            SE.wait(tT, ztok)
            if i >= 2 and (i - 2) in lp_read:
                SE.wait(vV, lp_read[i - 2])
            SE.do(lambda e, o=SHRD[:, g2o:g2o + 255], i_=pz.t[:, 0:255]:
                  e.activation(out=o, in_=i_, func=AF.Sigmoid, scale=-1.0))
            gtok = SE.do(lambda e, o=SHRD[:, g2o + 255:g2o + 510], i_=pz.t[:, 0:255]:
                         e.activation(out=o, in_=i_, func=AF.Sigmoid), inc=aA)
            p_mark_r(pz, SE, gtok)
            sig_tok[i] = gtok

        def stage_b(i):
            g2o = 512 * (i % 2)
            EN = GE if i % 2 == 0 else VE
            EN.wait(aA, sig_tok[i])
            if i >= 2 and (i - 2) in tr_toks:
                EN.wait(tT, tr_toks[i - 2])
            lpA = 1024 + 512 * (i % 2)
            lpB = lpA + 256
            ptf = 256 * (i % 2)
            EN.do(lambda e, g2o=g2o, lpA=lpA: e.tensor_copy(
                AP(SHRD, lpA, [[2048, 128], [1, 2]]),
                AP(SHRD, g2o, [[2048, 128], [255, 2]])))
            if EN is VE:
                EN.raw(lambda e: e.drain())
            cur, oth = lpA, lpB
            for lv_ in range(1, 8):
                n = 1 << lv_
                last = lv_ == 7
                o_ev = AP(PT, ptf, [[PTW, 128], [2, n]]) if last else AP(SHRD, oth, [[2048, 128], [2, n]])
                o_od = AP(PT, ptf + 1, [[PTW, 128], [2, n]]) if last else AP(SHRD, oth + 1, [[2048, 128], [2, n]])
                EN.do(lambda e, cur=cur, o_=o_ev, n=n, g2o=g2o: e.tensor_tensor(
                    o_, AP(SHRD, cur, [[2048, 128], [1, n]]),
                    AP(SHRD, g2o + n - 1, [[2048, 128], [1, n]]), ALU.mult))
                EN.do(lambda e, cur=cur, o_=o_od, n=n, g2o=g2o: e.tensor_tensor(
                    o_, AP(SHRD, cur, [[2048, 128], [1, n]]),
                    AP(SHRD, g2o + 255 + n - 1, [[2048, 128], [1, n]]), ALU.mult))
                if EN is VE:
                    EN.raw(lambda e: e.drain())
                cur, oth = oth, cur
            lptok = EN.mark(gG if EN is GE else vV)
            lpsem = gG if EN is GE else vV
            pt_ = P[(i + 1) % 2]
            TE.wait(lpsem, lptok)
            p_write_ready(TE, pt_)
            for b in range(2):
                TE.do(lambda e, b=b, pt_=pt_, ptf=ptf: e.matmul(
                    pt_.t[:, 256 + 128 * b:256 + 128 * (b + 1)],
                    PT[:, ptf + 128 * b:ptf + 128 * (b + 1)], IDNB[:, :], start=True, stop=True))
            ttk = TE.mark(tT)
            p_mark_w(pt_, ttk)
            tr_toks[i] = ttk
            VE.wait(tT, ttk)
            dst = AP(KS, 128 * i, [[4 * T, 128], [T, 2], [1, 128]])
            lp_read[i] = VE.do(lambda e, dst=dst, pt_=pt_: e.tensor_copy(dst, pt_.t[:, 256:512]), inc=vV)
            p_mark_r(pt_, VE, lp_read[i])

        def emit_ret(s):
            TE.wait(vV, lp_read[TPS * (s + 1) - 1])
            for dt_ in range(8):
                pp = P[dt_ % 2]
                p_write_ready(TE, pp)
                for m in range(2):
                    lhsT = WA[:, 4096 + 1024 * m + 128 * dt_: 4096 + 1024 * m + 128 * (dt_ + 1)]
                    rhs = KS[:, m * T + SP512 * s: m * T + SP512 * (s + 1)]
                    TE.do(lambda e, o=pp.t[:, 0:SP512], l=lhsT, r=rhs, s_=(m == 0), p2=(m == 1):
                          e.matmul(o, l, r, start=s_, stop=p2))
                mtok = TE.mark(tT)
                p_mark_w(pp, mtok)
                SE.wait(tT, mtok)
                atok = SE.do(lambda e, o=XLN[:, dt_ * T + SP512 * s: dt_ * T + SP512 * (s + 1)],
                             i_=pp.t[:, 0:SP512]: e.activation(out=o, in_=i_, func=AF.Identity),
                             inc=aA)
                p_mark_r(pp, SE, atok)
                ret_done[s] = atok

        for s in range(NSPW):
            for i in range(TPS * s, TPS * (s + 1)):
                stage_a(i)
                if i >= 1:
                    stage_b(i - 1)
            if s >= 1:
                emit_ret(s - 1)
        stage_b(TPS * NSPW - 1)
        emit_ret(NSPW - 1)
        wslots = [(WSL[4], 0, tT.n), (WSL[5], 4096, tT.n)]
        lv_, tts_, pmt_ = emit_out_proj(XLN, d_fwoutT, wslots, fb_tok, [(aA, ret_done[NSPW - 1])], store=store)
        return lv_, tts_, pmt_

    ffn1_res = emit_ffn(wo_tok, wo_tok, store=False, tile_toks=wo_tts, prev_pmt=wo_pmt) if n_sub >= 2 else None
    ffn1_tok, ffn1_tts, ffn1_pmt = ffn1_res if ffn1_res else (None, None, [None, None])

    def emit_attn_b(prev_tok, tile_toks=None, prev_pmt=None):
        ln_t = emit_ln(prev_x_tok=prev_tok, tile_toks=tile_toks)
        wa_free = tT.n
        wa_p0 = prev_pmt[0] if prev_pmt and prev_pmt[0] else wa_free
        wa_p1 = prev_pmt[1] if prev_pmt and prev_pmt[1] else wa_free
        cs_sem = WSL[3]
        pre_q = dma_w(WSL[0], WA[:, 0:4096],
                      AP(d_bwqT, 0, [[D, 128], [D * 128, 8], [1, 512]]),
                      after_tok=wa_p0)
        pre_k = dma_w(WSL[0], WA[:, 4096:8192],
                      AP(d_bwkT, 0, [[D, 128], [D * 128, 8], [1, 512]]),
                      after_tok=wa_p1)
        GE.wait(vV, ln_t)
        ctok = cs_sem.nxt(16)
        GE.ops.append(lambda e: e.dma_start(out=SHRD[:, 0:32 * NT],
                                            in_=d_cos.ap()[:, :]).then_inc(cs_sem.h, 16))
        stok = cs_sem.nxt(16)
        GE.ops.append(lambda e: e.dma_start(out=SHRD[:, 512:512 + 32 * NT],
                                            in_=d_sin.ap()[:, :]).then_inc(cs_sem.h, 16))
        last_attn = None
        quad_free = wa_free
        v_free2 = wa_p0
        rt_by = {}
        tt_by = {}
        VBP = NKT * 520
        ro_buf = [(PT, 0, PTW), (PT, 1024, PTW), (VB, 0, VBP)]
        for quad in range(2):
            wsem = WSL[quad % 2]
            if quad == 0:
                wsem = WSL[0]
                kwtok = pre_k
            else:
                qwtok = dma_w(wsem, WA[:, 0:4096],
                              AP(d_bwqT, 512 * quad, [[D, 128], [D * 128, 8], [1, 512]]),
                              after_tok=v_free2)
                kwtok = dma_w(wsem, WA[:, 4096:8192],
                              AP(d_bwkT, 512 * quad, [[D, 128], [D * 128, 8], [1, 512]]))
            TE.wait(wsem, kwtok)
            VE.wait(cs_sem, stok)
            pair_tok = None

            def flush_pend(i, g, quad=None):
                pt_ = P[i % 2]
                buf, base, _ = ro_buf[g % 3]
                TE.wait(vV, rt_by[g])
                p_write_ready(TE, pt_)
                for b in range(8):
                    TE.do(lambda e, b=b, pt_=pt_, buf=buf, base=base: e.matmul(
                        pt_.t[:, 128 * b:128 * (b + 1)], buf[:, base + 128 * b:base + 128 * (b + 1)],
                        IDNB[:, :], start=True, stop=True))
                ttok = TE.mark(tT)
                p_mark_w(pt_, ttok)
                tt_by[g] = ttok
                qdst = AP(QS, (4 * quad) * T + 128 * i, [[NPAIR * T, 128], [T, 4], [1, 128]])
                kdst = AP(KS, 128 * i, [[4 * T, 128], [T, 4], [1, 128]])
                SE.wait(tT, ttok)
                SE.do(lambda e, d=qdst, pt_=pt_: e.activation(out=d, in_=pt_.t[:, 0:512], func=AF.Identity))
                ptok = SE.do(lambda e, d=kdst, pt_=pt_: e.activation(out=d, in_=pt_.t[:, 512:1024], func=AF.Identity), inc=aA)
                p_mark_r(pt_, SE, ptok)
                return ptok

            for i in range(NT):
                g = quad * NT + i
                pq = P[2 + i % 2]
                if quad == 0:
                    TE.wait(vV, ln_evs[0][i])
                p_write_ready(TE, pq)
                for ch in range(8):
                    lhsT = XLN[:, ch * T + 128 * i: ch * T + 128 * (i + 1)]
                    TE.do(lambda e, o=pq.t[:, 0:512], l=lhsT, r=WA[:, 512 * ch:512 * (ch + 1)],
                          s=(ch == 0), p2=(ch == 7): e.matmul(o, l, r, start=s, stop=p2))
                for ch in range(8):
                    lhsT = XLN[:, ch * T + 128 * i: ch * T + 128 * (i + 1)]
                    TE.do(lambda e, o=pq.t[:, 512:1024], l=lhsT, r=WA[:, 4096 + 512 * ch:4096 + 512 * (ch + 1)],
                          s=(ch == 0), p2=(ch == 7): e.matmul(o, l, r, start=s, stop=p2))
                mtok = TE.mark(tT)
                p_mark_w(pq, mtok)
                EV = 2048 + 1024 * (i % 2)
                SE.wait(tT, mtok)
                if g - 2 in rt_by:
                    SE.wait(vV, rt_by[g - 2])
                pqh = getattr(pq.t, "handle", pq.t)
                pqo = getattr(pq.t, "off", 0)
                ppitch = 2048 if pqh is PAB else 1024
                inev = AP(pqh, pqo, [[ppitch, 128], [64, 16], [2, 32]])
                inod = AP(pqh, pqo + 1, [[ppitch, 128], [64, 16], [2, 32]])
                evev = AP(PT, EV, [[PTW, 128], [64, 16], [1, 32]])
                evod = AP(PT, EV + 32, [[PTW, 128], [64, 16], [1, 32]])
                SE.do(lambda e, o_=evev, i_=inev: e.activation(out=o_, in_=i_, func=AF.Identity))
                a2 = SE.do(lambda e, o_=evod, i_=inod: e.activation(out=o_, in_=i_, func=AF.Identity), inc=aA)
                p_mark_r(pq, SE, a2)
                co = 32 * i
                rbuf, rbase, rpitch = ro_buf[g % 3]
                sin_r = AP(SHRD, 512 + co, [[2048, 128], [0, 16], [1, 32]])
                cos_r = AP(SHRD, co, [[2048, 128], [0, 16], [1, 32]])
                t1 = ROWS[:, 0:512]
                t2 = ROWS[:, 512:1024]
                bev = AP(rbuf, rbase, [[rpitch, 128], [64, 16], [1, 32]])
                bod = AP(rbuf, rbase + 32, [[rpitch, 128], [64, 16], [1, 32]])
                VE.wait(aA, a2)
                if g - 3 in tt_by:
                    VE.wait(tT, tt_by[g - 3])
                VE.do(lambda e, a=t1, b=evev, s=sin_r: e.tensor_tensor(a, b, s, ALU.mult))
                VE.do(lambda e, a=t2, b=evod, s=sin_r: e.tensor_tensor(a, b, s, ALU.mult))
                VE.raw(lambda e: e.drain())
                VE.do(lambda e, a=evev, c_=cos_r: e.tensor_tensor(a, a, c_, ALU.mult))
                VE.do(lambda e, a=evod, c_=cos_r: e.tensor_tensor(a, a, c_, ALU.mult))
                VE.raw(lambda e: e.drain())
                VE.do(lambda e, o_=bev, a=evev, b=t2: e.tensor_tensor(o_, a, b, ALU.subtract))
                VE.do(lambda e, o_=bod, a=evod, b=t1: e.tensor_tensor(o_, a, b, ALU.add))
                rtok = VE.mark(vV)
                rt_by[g] = rtok
                if i >= 2:
                    pair_tok = flush_pend(i - 2, g - 2, quad=quad)
            pair_tok = flush_pend(NT - 2, quad * NT + NT - 2, quad=quad)
            pair_tok = flush_pend(NT - 1, quad * NT + NT - 1, quad=quad)
            quad_free = tT.n
            vwtok = dma_w(WSL[2], WA[:, 0:4096],
                          AP(d_bwvT, 512 * quad, [[D, 128], [D * 128, 8], [1, 512]]),
                          after_tok=quad_free)
            VE.wait(tT, tt_by[quad * NT + NT - 1])
            for kt_ in (0, 1):
                VE.do(lambda e, kt=kt_: e.memset(
                    AP(VB, 520 * kt + 64, [[VBP, 128], [130, 4], [65, 2]]), 1.0))
            v_tok2 = emit_v_half(0, None, WSL[2], vwtok, None)
            v_free2 = tT.n
            for cl in range(4):
                c = 4 * quad + cl
                last_attn = emit_attention_pair(c, True, 1, pair_tok if cl == 0 else None, v_tok2)
        wslots = [(WSL[4], 0, tT.n), (WSL[5], 4096, None)]
        lvb_, ttsb_, pmtb_ = emit_out_proj(QS, d_bwoT, wslots, None, [(vV, last_attn)])
        return lvb_, ttsb_, pmtb_

    def emit_store(last_tok):
        SY.wait(vV, last_tok)
        if DBG & 512:
            GE.wait(aA, aA.n)
            GE.wait(vV, vV.n)
            dO.n += 16
            GE.ops.append(lambda e: e.dma_start(
                out=AP(d_y, 0, [[128, 128], [1, 128]]),
                in_=SML[:, :]).then_inc(dO.h, 16))
            return
        if DBG & 256:
            GE.wait(aA, aA.n)
            GE.wait(vV, vV.n)
            GE.wait(tT, tT.n)
            dO.n += 16
            GE.ops.append(lambda e, PTW=PTW: e.dma_start(
                out=AP(d_y, 0, [[D, 128], [128 * D, PTW // 1024], [1, D]]),
                in_=AP(PT, 0, [[PTW, 128], [1024, PTW // 1024], [1, 1024]])).then_inc(dO.h, 16))
            return
        if DBG & 128:
            GE.wait(aA, aA.n)
            GE.wait(vV, vV.n)
            dO.n += 16
            GE.ops.append(lambda e: e.dma_start(
                out=AP(d_y, 0, [[D, 128], [128 * D, (NCH * T) // 1024], [1, D]]),
                in_=AP(XLN, 0, [[NCH * T, 128], [1024, (NCH * T) // 1024], [1, 1024]])).then_inc(dO.h, 16))
            return
        if DBG & 64:
            GE.wait(aA, aA.n)
            GE.wait(vV, vV.n)
            GE.wait(dF, dF.n)
            dO.n += 16
            GE.ops.append(lambda e: e.dma_start(
                out=AP(d_y, 0, [[D, 128], [128 * D, NPAIR * T // 1024, ], [1, D]]) if False else AP(d_y, 0, [[D, 128], [128 * D, (NPAIR * T) // 1024], [1, D]]),
                in_=AP(QS, 0, [[NPAIR * T, 128], [1024, (NPAIR * T) // 1024], [1, 1024]])).then_inc(dO.h, 16))
            return
        for i in range(NT):
            dO.n += 16
            SY.ops.append(lambda e, i=i: e.dma_start(
                out=AP(d_y, 128 * i * D, [[D, 128], [1, D]]),
                in_=X[:, 1024 * i:1024 * (i + 1)]).then_inc(dO.h, 16))

    if n_sub >= 3:
        attb_tok, attb_tts, attb_pmt = emit_attn_b(ffn1_tok, tile_toks=ffn1_tts, prev_pmt=ffn1_pmt)
    if n_sub >= 4:
        ffn2_tok = emit_ffn(attb_tok, attb_tok, store=True, tile_toks=attb_tts, prev_pmt=attb_pmt)[0]
    elif n_sub == 3:
        emit_store(attb_tok)
    elif n_sub == 2:
        emit_store(ffn1_tok)
    else:
        emit_store(wo_tok)

    SY.raw(lambda e, n=dO.n: e.wait_ge(dO.h, n))

    import os as _os
    if _os.environ.get("KCOUNT"):
        print("opcounts:", {nm: len(EG.ops) for nm, EG in (("t", TE), ("v", VE), ("s", SE), ("g", GE), ("y", SY))})
    _tr = _os.environ.get("KBISECT", "")
    if _tr:
        eng_sel, frac = _tr.split(":")
        frac = float(frac)
        for nm, EG in (("t", TE), ("v", VE), ("s", SE), ("g", GE), ("y", SY)):
            if eng_sel != "all" and nm not in eng_sel:
                EG.ops = []
            else:
                EG.ops = EG.ops[:int(len(EG.ops) * frac)]
    with es:
        block = es.enter_context(nc.Block())

        @block.gpsimd
        def _(g):
            for f in GE.ops:
                f(g)

        @block.vector
        def _(v):
            for f in VE.ops:
                f(v)

        @block.scalar
        def _(s):
            for f in SE.ops:
                f(s)

        @block.tensor
        def _(t):
            for f in TE.ops:
                f(t)

        @block.sync
        def _(s):
            for f in SY.ops:
                f(s)

    return nc


def _prep_consts(T):
    NT = T // 128
    ident = np.eye(128, dtype=np.float32)
    tri = (np.arange(128)[None, :] >= np.arange(128)[:, None]).astype(np.float32)
    pos = np.arange(T, dtype=np.float32)
    pair = np.arange(32, dtype=np.float32)
    rates = (np.float32(10000.0) ** (-(2.0 * pair) / np.float32(64.0))).astype(np.float32)
    ang = (pos[:, None] * rates[None, :]).astype(np.float32)
    cos = np.cos(ang).astype(np.float32)
    sin = np.sin(ang).astype(np.float32)
    cos_t = np.zeros((128, 32 * NT), dtype=np.float32)
    sin_t = np.zeros((128, 32 * NT), dtype=np.float32)
    for i in range(NT):
        cos_t[:, 32 * i:32 * (i + 1)] = cos[128 * i:128 * (i + 1)]
        sin_t[:, 32 * i:32 * (i + 1)] = sin[128 * i:128 * (i + 1)]
    ones = np.ones((128, 1), dtype=np.float32)
    onesf = np.ones((128, 128), dtype=np.float32)
    csh = np.zeros((128, 2), dtype=np.float32)
    csh[:, 0] = -C_SHIFT_A
    csh[:, 1] = -C_SHIFT_B
    return ident, tri, cos_t, sin_t, ones, csh, onesf


def make_in_map(xc, w, T):
    ident, tri, cos_t, sin_t, ones, csh, onesf = _prep_consts(T)
    asc = lambda a: np.ascontiguousarray(a, dtype=np.float32)
    return {
        "x_in": asc(xc),
        "wqkvT": asc(w["L0_attn_a_qkv_w"].T),
        "woT": asc(w["L0_attn_a_out_w"].T),
        "qkvb_t": asc(w["L0_attn_a_qkv_b"].reshape(24, 128).T),
        "vb_row": asc(w["L0_attn_a_qkv_b"][2048:3072].reshape(1, 1024)),
        "ob_row": asc(w["L0_attn_a_out_b"].reshape(1, 1024)),
        "bwqT": asc(w["L0_attn_b_Wq"].T),
        "bwkT": asc(w["L0_attn_b_Wk"].T),
        "bwvT": asc(w["L0_attn_b_Wv"].T),
        "bwoT": asc(w["L0_attn_b_Wo"].T),
        "fwqT": asc(w["L0_ffn_Wq"].T),
        "routT": asc(w["L0_ffn_routers"].T),
        "lv_w": asc(w["L0_ffn_leaf_values"]),
        "fwoutT": asc(w["L0_ffn_Wout_w"].T),
        "fb_row": asc(w["L0_ffn_Wout_b"].reshape(1, 1024)),
        "ident_c": ident, "tri_c": tri, "cos_c": cos_t, "sin_c": sin_t,
        "ones_c": ones, "csh_c": csh, "onesf_c": onesf, "onesb_c": onesf[:, 0:64].copy(),
    }


def kernel(**inputs):
    x = np.asarray(inputs["L0_attn_a_x"], dtype=np.float32)
    B, T, _ = x.shape
    w = {k: np.asarray(v, dtype=np.float32) for k, v in inputs.items()
         if k != "L0_attn_a_x"}
    nc = build_nc(T)
    in_maps = [make_in_map(x[i], w, T) for i in range(B)]
    res = run_bass_kernel_spmd(nc, in_maps, core_ids=list(range(B)))
    out = np.stack([np.asarray(res.results[i]["y_out"]) for i in range(B)], axis=0)
    return out.astype(np.float32)


# version 33
# speedup vs baseline: 1.0337x; 1.0052x over previous
import contextlib
import numpy as np

import concourse.bass as bass
import concourse.mybir as mybir
from concourse.bass_utils import run_bass_kernel_spmd

F32 = mybir.dt.float32
BF16 = mybir.dt.bfloat16
AF = mybir.ActivationFunctionType
ALU = mybir.AluOpType

D = 1024
NCH = 8
NPAIR = 8
C_SHIFT_A = 12.0
C_SHIFT_B = 26.0

AP = bass.AP
DBG = 0


class Sem:
    def __init__(self, h):
        self.h = h
        self.n = 0

    def nxt(self, k=1):
        self.n += k
        return self.n


class Eng:

    def __init__(self, name):
        self.name = name
        self.ops = []

    def do(self, fn, inc=None, k=1):
        if inc is None:
            self.ops.append(fn)
            return None
        tok = inc.nxt(k)
        self.ops.append(lambda e, fn=fn, sem=inc.h, k=k: fn(e).then_inc(sem, k))
        return tok

    def mark(self, sem):
        tok = sem.nxt(1)
        prev = self.ops[-1]

        def wrapped(e, p=prev, s=sem.h):
            r = p(e)
            assert r is not None, "mark() must follow an instruction-emitting op"
            return r.then_inc(s, 1)

        self.ops[-1] = wrapped
        return tok

    def wait(self, sem, tok):
        if tok is None:
            return
        self.ops.append(lambda e, s=sem.h, t=tok: e.wait_ge(s, t))

    def raw(self, fn):
        self.ops.append(fn)


class Psum:
    def __init__(self, t):
        self.t = t
        self.wtok = None
        self.rtoks = []


def build_nc(T, n_sub=4, span_ovr=None):
    NT = T // 128
    SPAN = span_ovr or min(1024, T)
    NSP = T // SPAN
    NKT = T // 128
    SP512 = min(512, T)
    NSP512 = T // SP512
    PTW = max(4 * SPAN, 4096)

    nc = bass.Bass("TRN2", target_bir_lowering=False, detect_race_conditions=False)

    dram = {}

    def dr(name, shape, kind="ExternalInput"):
        dram[name] = nc.dram_tensor(name, shape, F32, kind=kind)
        return dram[name]

    d_x = dr("x_in", [T, D])
    d_y = dr("y_out", [T, D], "ExternalOutput")
    d_wqkvT = dr("wqkvT", [D, 3 * D])
    d_woT = dr("woT", [D, D])
    d_qkvb = dr("qkvb_t", [128, 24])
    d_vbrow = dr("vb_row", [1, D])
    d_obrow = dr("ob_row", [1, D])
    d_bwqT = dr("bwqT", [D, D])
    d_bwkT = dr("bwkT", [D, D])
    d_bwvT = dr("bwvT", [D, D])
    d_bwoT = dr("bwoT", [D, D])
    d_fwqT = dr("fwqT", [D, D])
    d_routT = dr("routT", [D, 255])
    d_lv = dr("lv_w", [256, D])
    d_fwoutT = dr("fwoutT", [D, D])
    d_fbrow = dr("fb_row", [1, D])
    d_ident = dr("ident_c", [128, 128])
    d_tri = dr("tri_c", [128, 128])
    d_cos = dr("cos_c", [128, 32 * NT])
    d_sin = dr("sin_c", [128, 32 * NT])
    d_ones = dr("ones_c", [128, 1])
    d_onesf = dr("onesf_c", [128, 128])
    d_onesb = dr("onesb_c", [128, 64])
    d_csh = dr("csh_c", [128, 2])

    es = contextlib.ExitStack()
    sb = lambda name, shape, dt: es.enter_context(nc.sbuf_tensor(name, shape, dt))
    sem = lambda name: Sem(es.enter_context(nc.semaphore(name)))

    X = sb("X", [128, NT * 1024], F32)
    SHRD = sb("SHRD", [128, 2048], F32)
    BIAS = sb("BIAS", [128, 1024], BF16)
    RB = sb("RB", [128, SPAN], F32)
    ROWS = sb("ROWS", [128, 1024], F32)
    SML = sb("SML", [128, 128], F32)
    STAT = sb("STAT", [128, 12 * 16], F32)
    QKVB = sb("QKVB", [128, 24], F32)
    XLN = sb("XLN", [128, NCH * T], BF16)
    QS = sb("QS", [128, NPAIR * T], BF16)
    KS = sb("KS", [128, 4 * T], BF16)
    VB = sb("VB", [128, NKT * 520], BF16)
    PT = sb("PT", [128, PTW], BF16)
    WA = sb("WA", [128, 8192], BF16)
    IDNB = sb("IDNB", [128, 128], BF16)
    TRI = sb("TRI", [128, 128], BF16)
    ONES = sb("ONES", [128, 1], BF16)
    ONESF = sb("ONESF", [128, 128], F32)
    ONESB = sb("ONESB", [128, 64], BF16)
    CSH = sb("CSH", [128, 2], F32)
    STG = sb("STG", [128, SPAN], BF16)

    class TV:

        def __init__(self, t, off):
            self._t = t
            self._o = off
            self.handle = t
            self.off = off

        def __getitem__(self, idx):
            r, c = idx
            return self._t[r, c.start + self._o:c.stop + self._o]

    PAB = es.enter_context(nc.psum_tensor("PAB", [128, 2048], F32))
    P = [Psum(TV(PAB, 0)), Psum(TV(PAB, 1024)),
         Psum(es.enter_context(nc.psum_tensor("P2", [128, 1024], F32))),
         Psum(es.enter_context(nc.psum_tensor("P3", [128, 1024], F32)))]

    tT, aA, vV, gG = sem("tT"), sem("aA"), sem("vV"), sem("gG")
    dX, dC, dO, dF = sem("dX"), sem("dC"), sem("dO"), sem("dF")
    WSL = [sem(f"ws{i}") for i in range(8)]

    TE, VE, SE, GE, SY = Eng("t"), Eng("v"), Eng("s"), Eng("g"), Eng("y")
    MARK = {id(TE): tT, id(VE): vV, id(SE): aA, id(GE): gG}

    def p_write_ready(eng, p):
        for s, t in p.rtoks:
            eng.wait(s, t)
        p.rtoks = []

    def p_read_ready(eng, p):
        if p.wtok is not None:
            eng.wait(*p.wtok)

    def p_mark_w(p, tok):
        p.wtok = (tT, tok)

    def p_mark_r(p, eng, tok):
        p.rtoks.append((MARK[id(eng)], tok))

    def dma_w(slot_sem, dst_ap, src_ap, after_tok=None):
        if after_tok is not None:
            GE.wait(tT, after_tok)
        tok = slot_sem.nxt(16)
        GE.ops.append(lambda e, d=dst_ap, s=src_ap, sm=slot_sem.h:
                      e.dma_start(out=d, in_=s).then_inc(sm, 16))
        return tok


    def dma_c(dst, src):
        GE.ops.append(lambda e, d=dst, s=src: e.dma_start(out=d, in_=s).then_inc(dC.h, 16))
        dC.n += 16

    dma_c(IDNB[:, :], d_ident.ap()[:, :])
    dma_c(TRI[:, :], d_tri.ap()[:, :])
    dma_c(ONES[:, :], d_ones.ap()[:, :])
    dma_c(QKVB[:, :], d_qkvb.ap()[:, :])
    dma_c(CSH[:, :], d_csh.ap()[:, :])
    dma_c(ONESF[:, :], d_onesf.ap()[:, :])
    dma_c(ONESB[:, :], d_onesb.ap()[:, :])
    tok_consts = dC.n

    for _kt in range(NKT):
        VE.ops.append(lambda e, kt=_kt: e.memset(
            AP(VB, 520 * kt + 64, [[NKT * 520, 128], [130, 4], [65, 2]]), 1.0))
    NXC = min(4, NT)
    TPC = NT // NXC
    x_chunk_tok = []
    for c_ in range(NXC):
        SY.ops.append(lambda e, c_=c_: e.dma_start(
            out=X[:, 1024 * TPC * c_:1024 * TPC * (c_ + 1)],
            in_=AP(d_x, 128 * TPC * c_ * D, [[D, 128], [128 * D, TPC], [1, D]])).then_inc(dX.h, 16))
        dX.n += 16
        x_chunk_tok.append(dX.n)
    tok_x = dX.n

    ln_evs = [None]

    def emit_ln(prev_x_tok=None, tile_toks=None):
        if DBG & 16:
            VE.wait(dX, tok_x)
            mtk = VE.do(lambda e: e.memset(SML[:, 0:1], 0.0), inc=vV)
            ln_evs[0] = [mtk] * NT
            return mtk
        first = prev_x_tok is None and tile_toks is None
        if not first:
            VE.wait(dX, tok_x)
        if tile_toks is None and prev_x_tok is not None:
            VE.wait(vV, prev_x_tok)
        VE.raw(lambda e: e.drain())
        for i in range(NT):
            xt = X[:, 1024 * i:1024 * (i + 1)]
            if tile_toks is not None:
                VE.wait(vV, tile_toks[i])
            if first:
                VE.wait(dX, x_chunk_tok[i // TPC])
            VE.do(lambda e, xt=xt, i=i: e.bn_stats(STAT[:, 12 * i:12 * i + 6], xt[:, 0:512]))
            VE.do(lambda e, xt=xt, i=i: e.bn_stats(STAT[:, 12 * i + 6:12 * i + 12], xt[:, 512:1024]))
        VE.raw(lambda e: e.drain())
        for i in range(NT):
            VE.do(lambda e, i=i: e.bn_aggr(SML[:, 80 + 2 * i:80 + 2 * i + 2], STAT[:, 12 * i:12 * i + 12]))
        VE.raw(lambda e: e.drain())
        VE.do(lambda e: e.tensor_scalar(SML[:, 0:NT], AP(SML, 81, [[128, 128], [2, NT]]),
                                        1e-5, None, ALU.add))
        vtok = VE.mark(vV)
        SE.wait(vV, vtok)
        stok = SE.do(lambda e: e.activation(out=SML[:, 16:16 + NT], in_=SML[:, 0:NT], func=AF.Sqrt), inc=aA)
        VE.wait(aA, stok)
        VE.raw(lambda e: e.drain())
        VE.do(lambda e: e.reciprocal(SML[:, 32:32 + NT], SML[:, 16:16 + NT]))
        VE.raw(lambda e: e.drain())
        for _ in range(2):
            VE.do(lambda e: e.tensor_tensor(SML[:, 48:48 + NT], SML[:, 32:32 + NT], SML[:, 32:32 + NT], ALU.mult))
            VE.raw(lambda e: e.drain())
            VE.do(lambda e: e.tensor_tensor(SML[:, 48:48 + NT], SML[:, 48:48 + NT], SML[:, 0:NT], ALU.mult))
            VE.raw(lambda e: e.drain())
            VE.do(lambda e: e.tensor_scalar(SML[:, 48:48 + NT], SML[:, 48:48 + NT], -0.5, 1.5, ALU.mult, ALU.add))
            VE.raw(lambda e: e.drain())
            VE.do(lambda e: e.tensor_tensor(SML[:, 32:32 + NT], SML[:, 32:32 + NT], SML[:, 48:48 + NT], ALU.mult))
            VE.raw(lambda e: e.drain())
        VE.do(lambda e: e.tensor_tensor(SML[:, 64:64 + NT], AP(SML, 80, [[128, 128], [2, NT]]),
                                        SML[:, 32:32 + NT], ALU.mult))
        VE.raw(lambda e: e.drain())
        VE.do(lambda e: e.tensor_scalar(SML[:, 64:64 + NT], SML[:, 64:64 + NT], -1.0, None, ALU.mult))
        ntok = VE.mark(vV)
        tr_tok = [None, None]
        last_ev = None
        ev_sems = []
        SE.wait(vV, ntok)
        for i in range(NT):
            xt = X[:, 1024 * i:1024 * (i + 1)]
            half = 1024 * (i % 2)
            xo = PT[:, half:half + 1024]
            if tr_tok[i % 2] is not None:
                SE.wait(tT, tr_tok[i % 2])
            atok = SE.do(lambda e, xt=xt, xo=xo, i=i: e.activation(
                out=xo, in_=xt, func=AF.Identity, bias=SML[:, 64 + i:65 + i],
                scale=SML[:, 32 + i:33 + i]), inc=aA)
            pt = P[i % 2]
            TE.wait(aA, atok)
            p_write_ready(TE, pt)
            for c in range(8):
                TE.do(lambda e, xo=xo, c=c, pt=pt: e.matmul(
                    pt.t[:, 128 * c:128 * (c + 1)], xo[:, 128 * c:128 * (c + 1)],
                    IDNB[:, :], start=True, stop=True))
            ttok = TE.mark(tT)
            p_mark_w(pt, ttok)
            tr_tok[i % 2] = ttok
            dst = AP(XLN, 128 * i, [[NCH * T, 128], [T, 8], [1, 128]])
            VE.wait(tT, ttok)
            etok = VE.do(lambda e, pt=pt, dst=dst: e.tensor_copy(dst, pt.t[:, 0:1024]), inc=vV)
            p_mark_r(pt, VE, etok)
            ev_sems.append(etok)
            last_ev = etok
        ln_evs[0] = ev_sems
        return last_ev

    rows_free = [None]
    rb_free = [None]
    stg_free = [0]
    rowA_free = [None]
    rowB_free = [None]

    def emit_bias_bcast(src_row, reader_toks):
        if DBG & 32:
            return None
        s = WSL[6]
        if rows_free[0] is not None:
            GE.wait(tT, rows_free[0])
        tok = s.nxt(16)
        GE.ops.append(lambda e, sr=src_row: e.dma_start(
            out=ROWS[0:1, :], in_=sr).then_inc(s.h, 16))
        pp = P[0]
        TE.wait(WSL[6], tok)
        p_write_ready(TE, pp)
        for f in range(2):
            btok = TE.do(lambda e, f=f: e.matmul(pp.t[:, 512 * f:512 * (f + 1)],
                                                 ONESF[0:1, 0:128],
                                                 ROWS[0:1, 512 * f:512 * (f + 1)],
                                                 start=True, stop=True), inc=tT)
        p_mark_w(pp, btok)
        rows_free[0] = btok
        SE.wait(tT, btok)
        for st, tk in reader_toks:
            SE.wait(st, tk)
        atok = SE.do(lambda e: e.activation(out=BIAS[:, :], in_=pp.t[:, 0:1024], func=AF.Identity), inc=aA)
        p_mark_r(pp, SE, atok)
        return atok

    def emit_pair_proj(wsem, wcol, dst_sl, bias_col, wtok, pps, extra_wait=None,
                       span_evs=None):
        if DBG & 8:
            return None
        TE.wait(wsem, wtok)
        if extra_wait is not None:
            TE.wait(*extra_wait)
        last_atok = None
        tps = SP512 // 128
        for sp in range(NSP512):
            pp = pps[sp % len(pps)]
            half = SP512 * ((sp // len(pps)) % 2) if SP512 == 512 else 0
            if span_evs is not None:
                TE.wait(vV, span_evs[min(tps * (sp + 1) - 1, NT - 1)])
            p_write_ready(TE, pp)
            for ch in range(8):
                lhsT = WA[:, wcol + 128 * ch: wcol + 128 * (ch + 1)]
                rhs = XLN[:, ch * T + SP512 * sp: ch * T + SP512 * (sp + 1)]
                TE.do(lambda e, o=pp.t[:, half:half + SP512], l=lhsT, r=rhs,
                      s=(ch == 0), p2=(ch == 7): e.matmul(o, l, r, start=s, stop=p2))
            mtok = TE.mark(tT)
            p_mark_w(pp, mtok)
            SE.wait(tT, mtok)
            out = dst_sl[:, SP512 * sp:SP512 * (sp + 1)]
            if bias_col is not None:
                atok = SE.do(lambda e, o=out, i=pp.t[:, half:half + SP512],
                             b=QKVB[:, bias_col:bias_col + 1]:
                             e.activation(out=o, in_=i, func=AF.Identity, bias=b), inc=aA)
            else:
                atok = SE.do(lambda e, o=out, i=pp.t[:, half:half + SP512]:
                             e.activation(out=o, in_=i, func=AF.Identity), inc=aA)
            p_mark_r(pp, SE, atok)
            last_atok = atok
        return last_atok

    def emit_v_half(wcol, bias_half, wsem, wtok, bias_tok):
        if DBG & 8:
            return None
        TE.wait(wsem, wtok)
        last = None
        for kt in range(NKT):
            pp = P[kt % 2]
            p_write_ready(TE, pp)
            for ch in range(8):
                lhsT = XLN[:, ch * T + 128 * kt: ch * T + 128 * (kt + 1)]
                rhs = WA[:, wcol + 512 * ch: wcol + 512 * (ch + 1)]
                TE.do(lambda e, o=pp.t[:, 0:512], l=lhsT, r=rhs, s=(ch == 0), p2=(ch == 7):
                      e.matmul(o, l, r, start=s, stop=p2))
            mtok = TE.mark(tT)
            p_mark_w(pp, mtok)
            VE.wait(tT, mtok)
            if bias_tok is not None and kt == 0:
                VE.wait(aA, bias_tok)
            dst = AP(VB, 520 * kt, [[NKT * 520, 128], [130, 4], [65, 2], [1, 64]])
            srcp = AP(pp.t.handle, pp.t.off, [[2048, 128], [128, 4], [64, 2], [1, 64]])
            if bias_half is not None:
                bsrc = AP(BIAS, 512 * bias_half, [[1024, 128], [128, 4], [64, 2], [1, 64]])
                vtok = VE.do(lambda e, d=dst, i=srcp, b=bsrc:
                             e.tensor_tensor(d, i, b, ALU.add), inc=vV)
            else:
                vtok = VE.do(lambda e, d=dst, i=srcp: e.tensor_copy(d, i), inc=vV)
            p_mark_r(pp, VE, vtok)
            last = vtok
        return last

    def emit_attention_pair(c, causal, cshift, qk_tok, v_tok):
        if DBG & 4:
            return v_tok
        qsl = QS[:, c * T:(c + 1) * T]
        koff = ((c % 4) if causal else (c % 2)) * T
        ksl = KS[:, koff:koff + T]
        vbase = 130 * (c % 4)
        S_A, S_B, PV_A, PV_B = P[0], P[1], P[2], P[3]
        TE.wait(vV, v_tok)
        if qk_tok is not None:
            TE.wait(aA, qk_tok)
        last_ev = None
        for sp in range(NSP):
            kts = [kt for kt in range(NKT)
                   if not (causal and 128 * kt > SPAN * sp + SPAN - 1)]
            def _off(kt):
                return max(0, 128 * kt - SPAN * sp) if causal else 0
            last_kt_bank = {}
            for bi in range((SPAN + 511) // 512):
                contrib = [kt for kt in kts if _off(kt) < 512 * bi + 512]
                last_kt_bank[bi] = contrib[-1]
            def emit_pv(hh, kt, ki, segs, ptok, slot):
                pv = PV_A if hh == 0 else PV_B
                TE.wait(*ptok)
                if ki == 0 and hh == 0:
                    p_write_ready(TE, PV_A)
                    p_write_ready(TE, PV_B)
                vsl = VB[:, 520 * kt + vbase + 65 * hh:520 * kt + vbase + 65 * hh + 65]
                st = (ki == 0)
                for (a, b) in segs:
                    prt = PT[:, slot + SPAN * hh + a:slot + SPAN * hh + b]
                    fin = (kt == last_kt_bank[a // 512])
                    TE.do(lambda e, o_=pv.t[0:65, a:b], l=vsl, r=prt, s=st, f=fin:
                          e.matmul(o_, l, r, start=s, stop=f))
                pvtok = TE.mark(tT)
                p_mark_w(pv, pvtok)

            pending = None
            for ki, kt in enumerate(kts):
                o = max(0, 128 * kt - SPAN * sp) if causal else 0
                segs = []
                n0 = o
                while n0 < SPAN:
                    n1 = min((n0 // 512 + 1) * 512, SPAN)
                    segs.append((n0, n1))
                    n0 = n1
                slot = 2 * SPAN * (kt % 2)
                mtoks = [None, None]
                for hh in range(2):
                    pp = S_A if hh == 0 else S_B
                    rowlo = 64 * hh
                    p_write_ready(TE, pp)
                    lhsT = ksl[rowlo:rowlo + 64, 128 * kt:128 * (kt + 1)]
                    for (a, b) in segs:
                        rhs = qsl[rowlo:rowlo + 64, SPAN * sp + a:SPAN * sp + b]
                        TE.do(lambda e, o_=pp.t[0:128, a:b], l=lhsT, r=rhs, tp=(rowlo, 0):
                              e.matmul(o_, l, r, start=True, stop=True, tile_position=tp))
                    mtoks[hh] = TE.mark(tT)
                    p_mark_w(pp, mtoks[hh])
                ptoks = [None, None]
                if False:
                    SE.wait(tT, mtoks[1])
                    etok = SE.do(lambda e, o_=PT[:, slot:slot + 2 * SPAN],
                                 i=PAB[:, 0:2 * SPAN], cs=cshift:
                                 e.activation(out=o_, in_=i, func=AF.Exp,
                                              bias=CSH[:, cs:cs + 1], scale=0.125),
                                 inc=aA)
                    p_mark_r(S_A, SE, etok)
                    p_mark_r(S_B, SE, etok)
                    etoks = [etok, etok]
                else:
                    etoks = [None, None]
                    for hh in range(2):
                        pp = S_A if hh == 0 else S_B
                        SE.wait(tT, mtoks[hh])
                        etoks[hh] = SE.do(lambda e, o_=PT[:, slot + SPAN * hh + o:slot + SPAN * (hh + 1)],
                                          i=pp.t[:, o:SPAN], cs=cshift:
                                          e.activation(out=o_, in_=i, func=AF.Exp,
                                                       bias=CSH[:, cs:cs + 1], scale=0.125),
                                          inc=aA)
                        p_mark_r(pp, SE, etoks[hh])
                for hh in range(2):
                    if causal and 128 * kt >= SPAN * sp:
                        VE.wait(aA, etoks[hh])
                        mtok2 = VE.do(lambda e, o_=PT[:, slot + SPAN * hh + o:slot + SPAN * hh + o + 128]:
                                      e.tensor_tensor(o_, o_, TRI[:, :], ALU.mult), inc=vV)
                        ptoks[hh] = (vV, mtok2)
                    else:
                        ptoks[hh] = (aA, etoks[hh])
                if pending is not None:
                    emit_pv(0, pending[0], pending[1], pending[2], pending[3], pending[5])
                    emit_pv(1, pending[0], pending[1], pending[2], pending[4], pending[5])
                pending = (kt, ki, segs, ptoks[0], ptoks[1], slot)
            emit_pv(0, pending[0], pending[1], pending[2], pending[3], pending[5])
            emit_pv(1, pending[0], pending[1], pending[2], pending[4], pending[5])
            if DBG & 1:
                last_ev = vV.n if vV.n > 0 else None
                continue
            spc0, spc1 = SPAN * sp, SPAN * (sp + 1)
            SE.wait(*PV_A.wtok)
            SE.wait(tT, rowA_free[0])
            SE.do(lambda e: e.activation(out=ROWS[64:65, 0:SPAN], in_=PV_A.t[64:65, 0:SPAN], func=AF.Ln))
            SE.wait(*PV_B.wtok)
            SE.wait(tT, rowB_free[0])
            SE.do(lambda e: e.activation(out=RB[64:65, 0:SPAN], in_=PV_B.t[64:65, 0:SPAN], func=AF.Ln))
            SE.raw(lambda e: e.drain())
            r1A = SE.do(lambda e: e.activation(out=ROWS[64:65, 0:SPAN], in_=ROWS[64:65, 0:SPAN], func=AF.Exp, scale=-1.0), inc=aA)
            p_mark_r(PV_A, SE, r1A)
            r1B = SE.do(lambda e: e.activation(out=RB[64:65, 0:SPAN], in_=RB[64:65, 0:SPAN], func=AF.Exp, scale=-1.0), inc=aA)
            p_mark_r(PV_B, SE, r1B)
            for hh, (pv, bp, r1, srow) in enumerate(((PV_A, S_A, r1A, ROWS), (PV_B, S_B, r1B, RB))):
                TE.wait(aA, r1)
                p_write_ready(TE, bp)
                for (a, b) in ((x, min(x + 512, SPAN)) for x in range(0, SPAN, 512)):
                    btok = TE.do(lambda e, bp=bp, a=a, b=b, srow=srow: e.matmul(
                        bp.t[0:64, a:b], ONESF[64:65, 0:64], srow[64:65, a:b],
                        start=True, stop=True, tile_position=(64, 0)), inc=tT)
                p_mark_w(bp, btok)
                if hh == 0:
                    rowA_free[0] = btok
                else:
                    rowB_free[0] = btok
                SE.wait(tT, btok)
                if hh == 1:
                    SE.wait(vV, e1)
                else:
                    SE.wait(vV, rb_free[0])
                ctok = SE.do(lambda e, bp=bp: e.activation(out=RB[0:64, 0:SPAN], in_=bp.t[0:64, 0:SPAN], func=AF.Identity), inc=aA)
                p_mark_r(bp, SE, ctok)
                VE.wait(aA, ctok)
                VE.wait(*pv.wtok)
                if hh == 0:
                    e1 = VE.do(lambda e, d=qsl[0:64, spc0:spc1], pv=pv:
                               e.tensor_tensor(d, pv.t[0:64, 0:SPAN], RB[0:64, 0:SPAN], ALU.mult), inc=vV)
                    p_mark_r(pv, VE, e1)
                else:
                    VE.wait(dF, stg_free[0])
                    e2 = VE.do(lambda e, pv=pv:
                               e.tensor_tensor(STG[0:64, 0:SPAN], pv.t[0:64, 0:SPAN], RB[0:64, 0:SPAN], ALU.mult), inc=vV)
                    p_mark_r(pv, VE, e2)
                    rb_free[0] = e2
                    GE.wait(vV, e2)
                    dF.n += 16
                    stg_free[0] = dF.n
                    GE.ops.append(lambda e, d=qsl[64:128, spc0:spc1]:
                                  e.dma_start(out=d, in_=STG[0:64, 0:SPAN]).then_inc(dF.h, 16))
            last_ev = e2
        return last_ev

    def emit_out_proj(src_buf, w_dram, wslots, bias_tok, ready_toks, store=False):
        last_vtok = None
        pass_mm_tok = [None, None]
        p1_toks = [None] * NT
        if DBG & 2:
            return (vV.n if vV.n else None), [None] * NT, pass_mm_tok
        for p_ in range(2):
            wsem, wcol, afree = wslots[p_]
            src_ap = AP(w_dram, 512 * p_ * D, [[D, 128], [128 * D, 4], [1, D]])
            wtok = dma_w(wsem, WA[:, wcol:wcol + 4096], src_ap, after_tok=afree)
            TE.wait(wsem, wtok)
            if p_ == 0:
                TE.wait(dF, dF.n)
                for s, t in ready_toks:
                    TE.wait(s, t)
            for i in range(NT):
                pp = P[i % 2] if p_ == 0 else P[2 + i % 2]
                p_write_ready(TE, pp)
                for ch4 in range(4):
                    ch = 4 * p_ + ch4
                    lhsT = src_buf[:, ch * T + 128 * i: ch * T + 128 * (i + 1)]
                    for f in range(2):
                        rhs = WA[:, wcol + 1024 * ch4 + 512 * f: wcol + 1024 * ch4 + 512 * (f + 1)]
                        TE.do(lambda e, o=pp.t[:, 512 * f:512 * (f + 1)], l=lhsT, r=rhs,
                              s=(ch4 == 0), p2=(ch4 == 3): e.matmul(o, l, r, start=s, stop=p2))
                mtok = TE.mark(tT)
                p_mark_w(pp, mtok)
                pass_mm_tok[p_] = mtok
                VE.wait(tT, mtok)
                xsl = X[:, 1024 * i:1024 * (i + 1)]
                vt = VE.do(lambda e, x=xsl, i_=pp.t[:, 0:1024]:
                           e.tensor_tensor(x, x, i_, ALU.add), inc=vV)
                p_mark_r(pp, VE, vt)
                last_vtok = vt
                if p_ == 1:
                    p1_toks[i] = vt
            if p_ == 0 and bias_tok is not None:
                VE.wait(aA, bias_tok)
                VE.raw(lambda e: e.drain())
                for i in range(NT):
                    xsl = X[:, 1024 * i:1024 * (i + 1)]
                    VE.do(lambda e, x=xsl: e.tensor_tensor(x, x, BIAS[:, :], ALU.add))
        tile_toks = p1_toks
        if store:
            for i in range(NT):
                SY.wait(vV, tile_toks[i])
                dO.n += 16
                SY.ops.append(lambda e, i=i: e.dma_start(
                    out=AP(d_y, 128 * i * D, [[D, 128], [1, D]]),
                    in_=X[:, 1024 * i:1024 * (i + 1)]).then_inc(dO.h, 16))
        return last_vtok, tile_toks, pass_mm_tok

    GE.wait(dC, tok_consts)
    TE.wait(dC, tok_consts)
    VE.wait(dC, tok_consts)
    SE.wait(dC, tok_consts)
    ln_tok = emit_ln()
    vb_tok = emit_bias_bcast(d_vbrow.ap()[:, :], [])

    pair_free = [None, None]
    v_free = None
    v_tok = None
    last_attn = None
    for c in range(NPAIR):
        wsem = WSL[c % 2]
        wcol = 2048 * (c % 2)
        qtok = dma_w(wsem, WA[:, wcol:wcol + 1024],
                     AP(d_wqkvT, 128 * c, [[3 * D, 128], [3 * D * 128, 8], [1, 128]]),
                     after_tok=pair_free[c % 2])
        ktok = dma_w(wsem, WA[:, wcol + 1024:wcol + 2048],
                     AP(d_wqkvT, D + 128 * c, [[3 * D, 128], [3 * D * 128, 8], [1, 128]]))
        emit_pair_proj(wsem, wcol, QS[:, c * T:(c + 1) * T], c, ktok, [P[2], P[3]],
                       span_evs=ln_evs[0] if c == 0 else None)
        ak = emit_pair_proj(wsem, wcol + 1024, KS[:, (c % 2) * T:(c % 2) * T + T],
                            8 + c, ktok, [P[2], P[3]])
        pair_free[c % 2] = tT.n
        if c in (0, 4):
            vwtok = dma_w(WSL[2], WA[:, 4096:8192],
                          AP(d_wqkvT, 2 * D + 512 * (c // 4),
                             [[3 * D, 128], [3 * D * 128, 8], [1, 512]]),
                          after_tok=v_free)
            v_tok = emit_v_half(4096, c // 4, WSL[2], vwtok, vb_tok if c == 0 else None)
            v_free = tT.n
        last_attn = emit_attention_pair(c, False, 0, ak, v_tok)

    ob_tok = emit_bias_bcast(d_obrow.ap()[:, :], [(vV, v_tok)])
    wo_slots = [(WSL[4], 0, tT.n), (WSL[5], 4096, tT.n)]
    wo_tok, wo_tts, wo_pmt = emit_out_proj(QS, d_woT, wo_slots, ob_tok, [(vV, last_attn)])

    def emit_ffn(prev_tok, bias_reader_tok, store, tile_toks=None, prev_pmt=None):
        ln_t = emit_ln(prev_x_tok=prev_tok, tile_toks=tile_toks)
        wa_free = tT.n
        wa_p0 = prev_pmt[0] if prev_pmt and prev_pmt[0] else wa_free
        wa_p1 = prev_pmt[1] if prev_pmt and prev_pmt[1] else wa_free
        rtok = dma_w(WSL[2], WA[:, 2048:2048 + 2040],
                     AP(d_routT, 0, [[255, 128], [255 * 128, 8], [1, 255]]),
                     after_tok=wa_p0)
        lvtok = dma_w(WSL[3], WA[:, 4096:6144],
                      AP(d_lv, 0, [[D, 128], [D * 128, 2], [1, D]]),
                      after_tok=wa_p1)
        fb_tok = emit_bias_bcast(d_fbrow.ap()[:, :], [(vV, bias_reader_tok)])
        qf_tok = None
        slot_free = [wa_p0, wa_p0]
        for et in range(8):
            wsem = WSL[et % 2]
            wcol = 1024 * (et % 2)
            wtok = dma_w(wsem, WA[:, wcol:wcol + 1024],
                         AP(d_fwqT, 128 * et, [[D, 128], [D * 128, 8], [1, 128]]),
                         after_tok=slot_free[et % 2])
            qf_tok = emit_pair_proj(wsem, wcol, QS[:, et * T:(et + 1) * T], None,
                                    wtok, [P[0], P[1]],
                                    span_evs=ln_evs[0] if et == 0 else None)
            slot_free[et % 2] = tT.n
        TE.wait(WSL[2], rtok)
        TE.wait(WSL[3], lvtok)

        NSPW = T // SP512
        TPS = SP512 // 128
        lp_read = {}
        tr_toks = {}
        sig_tok = {}
        ret_done = {}

        def stage_a(i):
            pz = P[2 + i % 2]
            p_write_ready(TE, pz)
            if i == 0:
                TE.wait(aA, qf_tok)
            for ch in range(8):
                lhsT = QS[:, ch * T + 128 * i: ch * T + 128 * (i + 1)]
                rhs = WA[:, 2048 + 255 * ch: 2048 + 255 * (ch + 1)]
                TE.do(lambda e, o=pz.t[:, 0:255], l=lhsT, r=rhs, s_=(ch == 0), p2=(ch == 7):
                      e.matmul(o, l, r, start=s_, stop=p2))
            ztok = TE.mark(tT)
            p_mark_w(pz, ztok)
            g2o = 512 * (i % 2)
            SE.wait(tT, ztok)
            if i >= 2 and (i - 2) in lp_read:
                SE.wait(vV, lp_read[i - 2])
            SE.do(lambda e, o=SHRD[:, g2o:g2o + 255], i_=pz.t[:, 0:255]:
                  e.activation(out=o, in_=i_, func=AF.Sigmoid, scale=-1.0))
            gtok = SE.do(lambda e, o=SHRD[:, g2o + 255:g2o + 510], i_=pz.t[:, 0:255]:
                         e.activation(out=o, in_=i_, func=AF.Sigmoid), inc=aA)
            p_mark_r(pz, SE, gtok)
            sig_tok[i] = gtok

        def stage_b(i):
            g2o = 512 * (i % 2)
            EN = GE if i % 2 == 0 else VE
            EN.wait(aA, sig_tok[i])
            if i >= 2 and (i - 2) in tr_toks:
                EN.wait(tT, tr_toks[i - 2])
            lpA = 1024 + 512 * (i % 2)
            lpB = lpA + 256
            ptf = 256 * (i % 2)
            EN.do(lambda e, g2o=g2o, lpA=lpA: e.tensor_copy(
                AP(SHRD, lpA, [[2048, 128], [1, 2]]),
                AP(SHRD, g2o, [[2048, 128], [255, 2]])))
            if EN is VE:
                EN.raw(lambda e: e.drain())
            cur, oth = lpA, lpB
            for lv_ in range(1, 8):
                n = 1 << lv_
                last = lv_ == 7
                o_ev = AP(PT, ptf, [[PTW, 128], [2, n]]) if last else AP(SHRD, oth, [[2048, 128], [2, n]])
                o_od = AP(PT, ptf + 1, [[PTW, 128], [2, n]]) if last else AP(SHRD, oth + 1, [[2048, 128], [2, n]])
                EN.do(lambda e, cur=cur, o_=o_ev, n=n, g2o=g2o: e.tensor_tensor(
                    o_, AP(SHRD, cur, [[2048, 128], [1, n]]),
                    AP(SHRD, g2o + n - 1, [[2048, 128], [1, n]]), ALU.mult))
                EN.do(lambda e, cur=cur, o_=o_od, n=n, g2o=g2o: e.tensor_tensor(
                    o_, AP(SHRD, cur, [[2048, 128], [1, n]]),
                    AP(SHRD, g2o + 255 + n - 1, [[2048, 128], [1, n]]), ALU.mult))
                if EN is VE:
                    EN.raw(lambda e: e.drain())
                cur, oth = oth, cur
            lptok = EN.mark(gG if EN is GE else vV)
            lpsem = gG if EN is GE else vV
            pt_ = P[(i + 1) % 2]
            TE.wait(lpsem, lptok)
            p_write_ready(TE, pt_)
            for b in range(2):
                TE.do(lambda e, b=b, pt_=pt_, ptf=ptf: e.matmul(
                    pt_.t[:, 256 + 128 * b:256 + 128 * (b + 1)],
                    PT[:, ptf + 128 * b:ptf + 128 * (b + 1)], IDNB[:, :], start=True, stop=True))
            ttk = TE.mark(tT)
            p_mark_w(pt_, ttk)
            tr_toks[i] = ttk
            VE.wait(tT, ttk)
            dst = AP(KS, 128 * i, [[4 * T, 128], [T, 2], [1, 128]])
            lp_read[i] = VE.do(lambda e, dst=dst, pt_=pt_: e.tensor_copy(dst, pt_.t[:, 256:512]), inc=vV)
            p_mark_r(pt_, VE, lp_read[i])

        def emit_ret(s):
            TE.wait(vV, lp_read[TPS * (s + 1) - 1])
            for dt_ in range(8):
                pp = P[dt_ % 2]
                p_write_ready(TE, pp)
                for m in range(2):
                    lhsT = WA[:, 4096 + 1024 * m + 128 * dt_: 4096 + 1024 * m + 128 * (dt_ + 1)]
                    rhs = KS[:, m * T + SP512 * s: m * T + SP512 * (s + 1)]
                    TE.do(lambda e, o=pp.t[:, 0:SP512], l=lhsT, r=rhs, s_=(m == 0), p2=(m == 1):
                          e.matmul(o, l, r, start=s_, stop=p2))
                mtok = TE.mark(tT)
                p_mark_w(pp, mtok)
                SE.wait(tT, mtok)
                atok = SE.do(lambda e, o=XLN[:, dt_ * T + SP512 * s: dt_ * T + SP512 * (s + 1)],
                             i_=pp.t[:, 0:SP512]: e.activation(out=o, in_=i_, func=AF.Identity),
                             inc=aA)
                p_mark_r(pp, SE, atok)
                ret_done[s] = atok

        for s in range(NSPW):
            for i in range(TPS * s, TPS * (s + 1)):
                stage_a(i)
                if i >= 1:
                    stage_b(i - 1)
            if s >= 1:
                emit_ret(s - 1)
        stage_b(TPS * NSPW - 1)
        emit_ret(NSPW - 1)
        wslots = [(WSL[4], 0, tT.n), (WSL[5], 4096, tT.n)]
        lv_, tts_, pmt_ = emit_out_proj(XLN, d_fwoutT, wslots, fb_tok, [(aA, ret_done[NSPW - 1])], store=store)
        return lv_, tts_, pmt_

    ffn1_res = emit_ffn(wo_tok, wo_tok, store=False, tile_toks=wo_tts, prev_pmt=wo_pmt) if n_sub >= 2 else None
    ffn1_tok, ffn1_tts, ffn1_pmt = ffn1_res if ffn1_res else (None, None, [None, None])

    def emit_attn_b(prev_tok, tile_toks=None, prev_pmt=None):
        ln_t = emit_ln(prev_x_tok=prev_tok, tile_toks=tile_toks)
        wa_free = tT.n
        wa_p0 = prev_pmt[0] if prev_pmt and prev_pmt[0] else wa_free
        wa_p1 = prev_pmt[1] if prev_pmt and prev_pmt[1] else wa_free
        cs_sem = WSL[3]
        pre_q = dma_w(WSL[0], WA[:, 0:4096],
                      AP(d_bwqT, 0, [[D, 128], [D * 128, 8], [1, 512]]),
                      after_tok=wa_p0)
        pre_k = dma_w(WSL[0], WA[:, 4096:8192],
                      AP(d_bwkT, 0, [[D, 128], [D * 128, 8], [1, 512]]),
                      after_tok=wa_p1)
        GE.wait(vV, ln_t)
        ctok = cs_sem.nxt(16)
        GE.ops.append(lambda e: e.dma_start(out=SHRD[:, 0:32 * NT],
                                            in_=d_cos.ap()[:, :]).then_inc(cs_sem.h, 16))
        stok = cs_sem.nxt(16)
        GE.ops.append(lambda e: e.dma_start(out=SHRD[:, 512:512 + 32 * NT],
                                            in_=d_sin.ap()[:, :]).then_inc(cs_sem.h, 16))
        last_attn = None
        quad_free = wa_free
        v_free2 = wa_p0
        rt_by = {}
        tt_by = {}
        VBP = NKT * 520
        ro_buf = [(PT, 0, PTW), (PT, 1024, PTW), (VB, 0, VBP)]
        pre_k1 = None
        for quad in range(2):
            wsem = WSL[0] if quad == 0 else WSL[1]
            kwtok = pre_k if quad == 0 else pre_k1
            TE.wait(wsem, kwtok)
            VE.wait(cs_sem, stok)
            pair_tok = None

            def flush_pend(i, g, quad=None):
                pt_ = P[i % 2]
                buf, base, _ = ro_buf[g % 3]
                TE.wait(vV, rt_by[g])
                p_write_ready(TE, pt_)
                for b in range(8):
                    TE.do(lambda e, b=b, pt_=pt_, buf=buf, base=base: e.matmul(
                        pt_.t[:, 128 * b:128 * (b + 1)], buf[:, base + 128 * b:base + 128 * (b + 1)],
                        IDNB[:, :], start=True, stop=True))
                ttok = TE.mark(tT)
                p_mark_w(pt_, ttok)
                tt_by[g] = ttok
                qdst = AP(QS, (4 * quad) * T + 128 * i, [[NPAIR * T, 128], [T, 4], [1, 128]])
                kdst = AP(KS, 128 * i, [[4 * T, 128], [T, 4], [1, 128]])
                SE.wait(tT, ttok)
                SE.do(lambda e, d=qdst, pt_=pt_: e.activation(out=d, in_=pt_.t[:, 0:512], func=AF.Identity))
                ptok = SE.do(lambda e, d=kdst, pt_=pt_: e.activation(out=d, in_=pt_.t[:, 512:1024], func=AF.Identity), inc=aA)
                p_mark_r(pt_, SE, ptok)
                return ptok

            for i in range(NT):
                g = quad * NT + i
                pq = P[2 + i % 2]
                if quad == 0:
                    TE.wait(vV, ln_evs[0][i])
                p_write_ready(TE, pq)
                for ch in range(8):
                    lhsT = XLN[:, ch * T + 128 * i: ch * T + 128 * (i + 1)]
                    TE.do(lambda e, o=pq.t[:, 0:512], l=lhsT, r=WA[:, 512 * ch:512 * (ch + 1)],
                          s=(ch == 0), p2=(ch == 7): e.matmul(o, l, r, start=s, stop=p2))
                for ch in range(8):
                    lhsT = XLN[:, ch * T + 128 * i: ch * T + 128 * (i + 1)]
                    TE.do(lambda e, o=pq.t[:, 512:1024], l=lhsT, r=WA[:, 4096 + 512 * ch:4096 + 512 * (ch + 1)],
                          s=(ch == 0), p2=(ch == 7): e.matmul(o, l, r, start=s, stop=p2))
                mtok = TE.mark(tT)
                p_mark_w(pq, mtok)
                EV = 2048 + 1024 * (i % 2)
                SE.wait(tT, mtok)
                if g - 2 in rt_by:
                    SE.wait(vV, rt_by[g - 2])
                pqh = getattr(pq.t, "handle", pq.t)
                pqo = getattr(pq.t, "off", 0)
                ppitch = 2048 if pqh is PAB else 1024
                inev = AP(pqh, pqo, [[ppitch, 128], [64, 16], [2, 32]])
                inod = AP(pqh, pqo + 1, [[ppitch, 128], [64, 16], [2, 32]])
                evev = AP(PT, EV, [[PTW, 128], [64, 16], [1, 32]])
                evod = AP(PT, EV + 32, [[PTW, 128], [64, 16], [1, 32]])
                SE.do(lambda e, o_=evev, i_=inev: e.activation(out=o_, in_=i_, func=AF.Identity))
                a2 = SE.do(lambda e, o_=evod, i_=inod: e.activation(out=o_, in_=i_, func=AF.Identity), inc=aA)
                p_mark_r(pq, SE, a2)
                co = 32 * i
                rbuf, rbase, rpitch = ro_buf[g % 3]
                sin_r = AP(SHRD, 512 + co, [[2048, 128], [0, 16], [1, 32]])
                cos_r = AP(SHRD, co, [[2048, 128], [0, 16], [1, 32]])
                t1 = ROWS[:, 0:512]
                t2 = ROWS[:, 512:1024]
                bev = AP(rbuf, rbase, [[rpitch, 128], [64, 16], [1, 32]])
                bod = AP(rbuf, rbase + 32, [[rpitch, 128], [64, 16], [1, 32]])
                VE.wait(aA, a2)
                if g - 3 in tt_by:
                    VE.wait(tT, tt_by[g - 3])
                VE.do(lambda e, a=t1, b=evev, s=sin_r: e.tensor_tensor(a, b, s, ALU.mult))
                VE.do(lambda e, a=t2, b=evod, s=sin_r: e.tensor_tensor(a, b, s, ALU.mult))
                VE.raw(lambda e: e.drain())
                VE.do(lambda e, a=evev, c_=cos_r: e.tensor_tensor(a, a, c_, ALU.mult))
                VE.do(lambda e, a=evod, c_=cos_r: e.tensor_tensor(a, a, c_, ALU.mult))
                VE.raw(lambda e: e.drain())
                VE.do(lambda e, o_=bev, a=evev, b=t2: e.tensor_tensor(o_, a, b, ALU.subtract))
                VE.do(lambda e, o_=bod, a=evod, b=t1: e.tensor_tensor(o_, a, b, ALU.add))
                rtok = VE.mark(vV)
                rt_by[g] = rtok
                if i >= 2:
                    pair_tok = flush_pend(i - 2, g - 2, quad=quad)
            pair_tok = flush_pend(NT - 2, quad * NT + NT - 2, quad=quad)
            pair_tok = flush_pend(NT - 1, quad * NT + NT - 1, quad=quad)
            quad_free = tT.n
            vwtok = dma_w(WSL[2], WA[:, 0:4096],
                          AP(d_bwvT, 512 * quad, [[D, 128], [D * 128, 8], [1, 512]]),
                          after_tok=quad_free)
            VE.wait(tT, tt_by[quad * NT + NT - 1])
            for kt_ in (0, 1):
                VE.do(lambda e, kt=kt_: e.memset(
                    AP(VB, 520 * kt + 64, [[VBP, 128], [130, 4], [65, 2]]), 1.0))
            v_tok2 = emit_v_half(0, None, WSL[2], vwtok, None)
            v_free2 = tT.n
            if quad == 0:
                dma_w(WSL[1], WA[:, 0:4096],
                      AP(d_bwqT, 512, [[D, 128], [D * 128, 8], [1, 512]]),
                      after_tok=v_free2)
                pre_k1 = dma_w(WSL[1], WA[:, 4096:8192],
                               AP(d_bwkT, 512, [[D, 128], [D * 128, 8], [1, 512]]))
            for cl in range(4):
                c = 4 * quad + cl
                last_attn = emit_attention_pair(c, True, 1, pair_tok if cl == 0 else None, v_tok2)
        wslots = [(WSL[4], 0, tT.n), (WSL[5], 4096, None)]
        lvb_, ttsb_, pmtb_ = emit_out_proj(QS, d_bwoT, wslots, None, [(vV, last_attn)])
        return lvb_, ttsb_, pmtb_

    def emit_store(last_tok):
        SY.wait(vV, last_tok)
        if DBG & 512:
            GE.wait(aA, aA.n)
            GE.wait(vV, vV.n)
            dO.n += 16
            GE.ops.append(lambda e: e.dma_start(
                out=AP(d_y, 0, [[128, 128], [1, 128]]),
                in_=SML[:, :]).then_inc(dO.h, 16))
            return
        if DBG & 256:
            GE.wait(aA, aA.n)
            GE.wait(vV, vV.n)
            GE.wait(tT, tT.n)
            dO.n += 16
            GE.ops.append(lambda e, PTW=PTW: e.dma_start(
                out=AP(d_y, 0, [[D, 128], [128 * D, PTW // 1024], [1, D]]),
                in_=AP(PT, 0, [[PTW, 128], [1024, PTW // 1024], [1, 1024]])).then_inc(dO.h, 16))
            return
        if DBG & 128:
            GE.wait(aA, aA.n)
            GE.wait(vV, vV.n)
            dO.n += 16
            GE.ops.append(lambda e: e.dma_start(
                out=AP(d_y, 0, [[D, 128], [128 * D, (NCH * T) // 1024], [1, D]]),
                in_=AP(XLN, 0, [[NCH * T, 128], [1024, (NCH * T) // 1024], [1, 1024]])).then_inc(dO.h, 16))
            return
        if DBG & 64:
            GE.wait(aA, aA.n)
            GE.wait(vV, vV.n)
            GE.wait(dF, dF.n)
            dO.n += 16
            GE.ops.append(lambda e: e.dma_start(
                out=AP(d_y, 0, [[D, 128], [128 * D, NPAIR * T // 1024, ], [1, D]]) if False else AP(d_y, 0, [[D, 128], [128 * D, (NPAIR * T) // 1024], [1, D]]),
                in_=AP(QS, 0, [[NPAIR * T, 128], [1024, (NPAIR * T) // 1024], [1, 1024]])).then_inc(dO.h, 16))
            return
        for i in range(NT):
            dO.n += 16
            SY.ops.append(lambda e, i=i: e.dma_start(
                out=AP(d_y, 128 * i * D, [[D, 128], [1, D]]),
                in_=X[:, 1024 * i:1024 * (i + 1)]).then_inc(dO.h, 16))

    if n_sub >= 3:
        attb_tok, attb_tts, attb_pmt = emit_attn_b(ffn1_tok, tile_toks=ffn1_tts, prev_pmt=ffn1_pmt)
    if n_sub >= 4:
        ffn2_tok = emit_ffn(attb_tok, attb_tok, store=True, tile_toks=attb_tts, prev_pmt=attb_pmt)[0]
    elif n_sub == 3:
        emit_store(attb_tok)
    elif n_sub == 2:
        emit_store(ffn1_tok)
    else:
        emit_store(wo_tok)

    SY.raw(lambda e, n=dO.n: e.wait_ge(dO.h, n))

    import os as _os
    if _os.environ.get("KCOUNT"):
        print("opcounts:", {nm: len(EG.ops) for nm, EG in (("t", TE), ("v", VE), ("s", SE), ("g", GE), ("y", SY))})
    _tr = _os.environ.get("KBISECT", "")
    if _tr:
        eng_sel, frac = _tr.split(":")
        frac = float(frac)
        for nm, EG in (("t", TE), ("v", VE), ("s", SE), ("g", GE), ("y", SY)):
            if eng_sel != "all" and nm not in eng_sel:
                EG.ops = []
            else:
                EG.ops = EG.ops[:int(len(EG.ops) * frac)]
    with es:
        block = es.enter_context(nc.Block())

        @block.gpsimd
        def _(g):
            for f in GE.ops:
                f(g)

        @block.vector
        def _(v):
            for f in VE.ops:
                f(v)

        @block.scalar
        def _(s):
            for f in SE.ops:
                f(s)

        @block.tensor
        def _(t):
            for f in TE.ops:
                f(t)

        @block.sync
        def _(s):
            for f in SY.ops:
                f(s)

    return nc


def _prep_consts(T):
    NT = T // 128
    ident = np.eye(128, dtype=np.float32)
    tri = (np.arange(128)[None, :] >= np.arange(128)[:, None]).astype(np.float32)
    pos = np.arange(T, dtype=np.float32)
    pair = np.arange(32, dtype=np.float32)
    rates = (np.float32(10000.0) ** (-(2.0 * pair) / np.float32(64.0))).astype(np.float32)
    ang = (pos[:, None] * rates[None, :]).astype(np.float32)
    cos = np.cos(ang).astype(np.float32)
    sin = np.sin(ang).astype(np.float32)
    cos_t = np.zeros((128, 32 * NT), dtype=np.float32)
    sin_t = np.zeros((128, 32 * NT), dtype=np.float32)
    for i in range(NT):
        cos_t[:, 32 * i:32 * (i + 1)] = cos[128 * i:128 * (i + 1)]
        sin_t[:, 32 * i:32 * (i + 1)] = sin[128 * i:128 * (i + 1)]
    ones = np.ones((128, 1), dtype=np.float32)
    onesf = np.ones((128, 128), dtype=np.float32)
    csh = np.zeros((128, 2), dtype=np.float32)
    csh[:, 0] = -C_SHIFT_A
    csh[:, 1] = -C_SHIFT_B
    return ident, tri, cos_t, sin_t, ones, csh, onesf


def make_in_map(xc, w, T):
    ident, tri, cos_t, sin_t, ones, csh, onesf = _prep_consts(T)
    asc = lambda a: np.ascontiguousarray(a, dtype=np.float32)
    return {
        "x_in": asc(xc),
        "wqkvT": asc(w["L0_attn_a_qkv_w"].T),
        "woT": asc(w["L0_attn_a_out_w"].T),
        "qkvb_t": asc(w["L0_attn_a_qkv_b"].reshape(24, 128).T),
        "vb_row": asc(w["L0_attn_a_qkv_b"][2048:3072].reshape(1, 1024)),
        "ob_row": asc(w["L0_attn_a_out_b"].reshape(1, 1024)),
        "bwqT": asc(w["L0_attn_b_Wq"].T),
        "bwkT": asc(w["L0_attn_b_Wk"].T),
        "bwvT": asc(w["L0_attn_b_Wv"].T),
        "bwoT": asc(w["L0_attn_b_Wo"].T),
        "fwqT": asc(w["L0_ffn_Wq"].T),
        "routT": asc(w["L0_ffn_routers"].T),
        "lv_w": asc(w["L0_ffn_leaf_values"]),
        "fwoutT": asc(w["L0_ffn_Wout_w"].T),
        "fb_row": asc(w["L0_ffn_Wout_b"].reshape(1, 1024)),
        "ident_c": ident, "tri_c": tri, "cos_c": cos_t, "sin_c": sin_t,
        "ones_c": ones, "csh_c": csh, "onesf_c": onesf, "onesb_c": onesf[:, 0:64].copy(),
    }


def kernel(**inputs):
    x = np.asarray(inputs["L0_attn_a_x"], dtype=np.float32)
    B, T, _ = x.shape
    w = {k: np.asarray(v, dtype=np.float32) for k, v in inputs.items()
         if k != "L0_attn_a_x"}
    nc = build_nc(T)
    in_maps = [make_in_map(x[i], w, T) for i in range(B)]
    res = run_bass_kernel_spmd(nc, in_maps, core_ids=list(range(B)))
    out = np.stack([np.asarray(res.results[i]["y_out"]) for i in range(B)], axis=0)
    return out.astype(np.float32)


# version 34 (final)
# speedup vs baseline: 1.0473x; 1.0131x over previous
import contextlib
import numpy as np

import concourse.bass as bass
import concourse.mybir as mybir
from concourse.bass_utils import run_bass_kernel_spmd

F32 = mybir.dt.float32
BF16 = mybir.dt.bfloat16
AF = mybir.ActivationFunctionType
ALU = mybir.AluOpType

D = 1024
NCH = 8
NPAIR = 8
C_SHIFT_A = 12.0
C_SHIFT_B = 26.0

AP = bass.AP
DBG = 0


class Sem:
    def __init__(self, h):
        self.h = h
        self.n = 0

    def nxt(self, k=1):
        self.n += k
        return self.n


class Eng:

    def __init__(self, name):
        self.name = name
        self.ops = []

    def do(self, fn, inc=None, k=1):
        if inc is None:
            self.ops.append(fn)
            return None
        tok = inc.nxt(k)
        self.ops.append(lambda e, fn=fn, sem=inc.h, k=k: fn(e).then_inc(sem, k))
        return tok

    def mark(self, sem):
        tok = sem.nxt(1)
        prev = self.ops[-1]

        def wrapped(e, p=prev, s=sem.h):
            r = p(e)
            assert r is not None, "mark() must follow an instruction-emitting op"
            return r.then_inc(s, 1)

        self.ops[-1] = wrapped
        return tok

    def wait(self, sem, tok):
        if tok is None:
            return
        self.ops.append(lambda e, s=sem.h, t=tok: e.wait_ge(s, t))

    def raw(self, fn):
        self.ops.append(fn)


class Psum:
    def __init__(self, t):
        self.t = t
        self.wtok = None
        self.rtoks = []


def build_nc(T, n_sub=4, span_ovr=None):
    NT = T // 128
    SPAN = span_ovr or min(1024, T)
    NSP = T // SPAN
    NKT = T // 128
    SP512 = min(512, T)
    NSP512 = T // SP512
    PTW = max(4 * SPAN, 4096)

    nc = bass.Bass("TRN2", target_bir_lowering=False, detect_race_conditions=False)

    dram = {}

    def dr(name, shape, kind="ExternalInput"):
        dram[name] = nc.dram_tensor(name, shape, F32, kind=kind)
        return dram[name]

    d_x = dr("x_in", [T, D])
    d_y = dr("y_out", [T, D], "ExternalOutput")
    d_wqkvT = dr("wqkvT", [D, 3 * D])
    d_woT = dr("woT", [D, D])
    d_qkvb = dr("qkvb_t", [128, 24])
    d_vbrow = dr("vb_row", [1, D])
    d_obrow = dr("ob_row", [1, D])
    d_bwqT = dr("bwqT", [D, D])
    d_bwkT = dr("bwkT", [D, D])
    d_bwvT = dr("bwvT", [D, D])
    d_bwoT = dr("bwoT", [D, D])
    d_fwqT = dr("fwqT", [D, D])
    d_routT = dr("routT", [D, 255])
    d_lv = dr("lv_w", [256, D])
    d_fwoutT = dr("fwoutT", [D, D])
    d_fbrow = dr("fb_row", [1, D])
    d_ident = dr("ident_c", [128, 128])
    d_tri = dr("tri_c", [128, 128])
    d_cos = dr("cos_c", [128, 32 * NT])
    d_sin = dr("sin_c", [128, 32 * NT])
    d_ones = dr("ones_c", [128, 1])
    d_onesf = dr("onesf_c", [128, 128])
    d_onesb = dr("onesb_c", [128, 64])
    d_csh = dr("csh_c", [128, 2])

    es = contextlib.ExitStack()
    sb = lambda name, shape, dt: es.enter_context(nc.sbuf_tensor(name, shape, dt))
    sem = lambda name: Sem(es.enter_context(nc.semaphore(name)))

    X = sb("X", [128, NT * 1024], F32)
    SHRD = sb("SHRD", [128, 2048], F32)
    BIAS = sb("BIAS", [128, 1024], BF16)
    RB = sb("RB", [128, SPAN], F32)
    ROWS = sb("ROWS", [128, 1024], F32)
    SML = sb("SML", [128, 128], F32)
    STAT = sb("STAT", [128, 12 * 16], F32)
    QKVB = sb("QKVB", [128, 24], F32)
    XLN = sb("XLN", [128, NCH * T], BF16)
    QS = sb("QS", [128, NPAIR * T], BF16)
    KS = sb("KS", [128, 4 * T], BF16)
    VB = sb("VB", [128, NKT * 520], BF16)
    PT = sb("PT", [128, PTW], BF16)
    WA = sb("WA", [128, 8192], BF16)
    IDNB = sb("IDNB", [128, 128], BF16)
    TRI = sb("TRI", [128, 128], BF16)
    ONES = sb("ONES", [128, 1], BF16)
    ONESF = sb("ONESF", [128, 128], F32)
    ONESB = sb("ONESB", [128, 64], BF16)
    CSH = sb("CSH", [128, 2], F32)
    STG = sb("STG", [128, SPAN], BF16)

    class TV:

        def __init__(self, t, off):
            self._t = t
            self._o = off
            self.handle = t
            self.off = off

        def __getitem__(self, idx):
            r, c = idx
            return self._t[r, c.start + self._o:c.stop + self._o]

    PAB = es.enter_context(nc.psum_tensor("PAB", [128, 2048], F32))
    P = [Psum(TV(PAB, 0)), Psum(TV(PAB, 1024)),
         Psum(es.enter_context(nc.psum_tensor("P2", [128, 1024], F32))),
         Psum(es.enter_context(nc.psum_tensor("P3", [128, 1024], F32)))]

    tT, aA, vV, gG = sem("tT"), sem("aA"), sem("vV"), sem("gG")
    dX, dC, dO, dF = sem("dX"), sem("dC"), sem("dO"), sem("dF")
    WSL = [sem(f"ws{i}") for i in range(8)]

    TE, VE, SE, GE, SY = Eng("t"), Eng("v"), Eng("s"), Eng("g"), Eng("y")
    MARK = {id(TE): tT, id(VE): vV, id(SE): aA, id(GE): gG}

    def p_write_ready(eng, p):
        for s, t in p.rtoks:
            eng.wait(s, t)
        p.rtoks = []

    def p_read_ready(eng, p):
        if p.wtok is not None:
            eng.wait(*p.wtok)

    def p_mark_w(p, tok):
        p.wtok = (tT, tok)

    def p_mark_r(p, eng, tok):
        p.rtoks.append((MARK[id(eng)], tok))

    def dma_w(slot_sem, dst_ap, src_ap, after_tok=None):
        if after_tok is not None:
            GE.wait(tT, after_tok)
        tok = slot_sem.nxt(16)
        GE.ops.append(lambda e, d=dst_ap, s=src_ap, sm=slot_sem.h:
                      e.dma_start(out=d, in_=s).then_inc(sm, 16))
        return tok


    def dma_c(dst, src):
        GE.ops.append(lambda e, d=dst, s=src: e.dma_start(out=d, in_=s).then_inc(dC.h, 16))
        dC.n += 16

    dma_c(IDNB[:, :], d_ident.ap()[:, :])
    dma_c(TRI[:, :], d_tri.ap()[:, :])
    dma_c(ONES[:, :], d_ones.ap()[:, :])
    dma_c(QKVB[:, :], d_qkvb.ap()[:, :])
    dma_c(CSH[:, :], d_csh.ap()[:, :])
    dma_c(ONESF[:, :], d_onesf.ap()[:, :])
    dma_c(ONESB[:, :], d_onesb.ap()[:, :])
    tok_consts = dC.n

    for _kt in range(NKT):
        VE.ops.append(lambda e, kt=_kt: e.memset(
            AP(VB, 520 * kt + 64, [[NKT * 520, 128], [130, 4], [65, 2]]), 1.0))
    NXC = min(4, NT)
    TPC = NT // NXC
    x_chunk_tok = []
    for c_ in range(NXC):
        SY.ops.append(lambda e, c_=c_: e.dma_start(
            out=X[:, 1024 * TPC * c_:1024 * TPC * (c_ + 1)],
            in_=AP(d_x, 128 * TPC * c_ * D, [[D, 128], [128 * D, TPC], [1, D]])).then_inc(dX.h, 16))
        dX.n += 16
        x_chunk_tok.append(dX.n)
    tok_x = dX.n

    ln_evs = [None]

    def emit_ln(prev_x_tok=None, tile_toks=None):
        if DBG & 16:
            VE.wait(dX, tok_x)
            mtk = VE.do(lambda e: e.memset(SML[:, 0:1], 0.0), inc=vV)
            ln_evs[0] = [mtk] * NT
            return mtk
        first = prev_x_tok is None and tile_toks is None
        if not first:
            VE.wait(dX, tok_x)
        if tile_toks is None and prev_x_tok is not None:
            VE.wait(vV, prev_x_tok)
        VE.raw(lambda e: e.drain())
        for i in range(NT):
            xt = X[:, 1024 * i:1024 * (i + 1)]
            if tile_toks is not None:
                VE.wait(vV, tile_toks[i])
            if first:
                VE.wait(dX, x_chunk_tok[i // TPC])
            VE.do(lambda e, xt=xt, i=i: e.bn_stats(STAT[:, 12 * i:12 * i + 6], xt[:, 0:512]))
            VE.do(lambda e, xt=xt, i=i: e.bn_stats(STAT[:, 12 * i + 6:12 * i + 12], xt[:, 512:1024]))
        VE.raw(lambda e: e.drain())
        for i in range(NT):
            VE.do(lambda e, i=i: e.bn_aggr(SML[:, 80 + 2 * i:80 + 2 * i + 2], STAT[:, 12 * i:12 * i + 12]))
        VE.raw(lambda e: e.drain())
        VE.do(lambda e: e.tensor_scalar(SML[:, 0:NT], AP(SML, 81, [[128, 128], [2, NT]]),
                                        1e-5, None, ALU.add))
        vtok = VE.mark(vV)
        SE.wait(vV, vtok)
        stok = SE.do(lambda e: e.activation(out=SML[:, 16:16 + NT], in_=SML[:, 0:NT], func=AF.Sqrt), inc=aA)
        VE.wait(aA, stok)
        VE.raw(lambda e: e.drain())
        VE.do(lambda e: e.reciprocal(SML[:, 32:32 + NT], SML[:, 16:16 + NT]))
        VE.raw(lambda e: e.drain())
        for _ in range(2):
            VE.do(lambda e: e.tensor_tensor(SML[:, 48:48 + NT], SML[:, 32:32 + NT], SML[:, 32:32 + NT], ALU.mult))
            VE.raw(lambda e: e.drain())
            VE.do(lambda e: e.tensor_tensor(SML[:, 48:48 + NT], SML[:, 48:48 + NT], SML[:, 0:NT], ALU.mult))
            VE.raw(lambda e: e.drain())
            VE.do(lambda e: e.tensor_scalar(SML[:, 48:48 + NT], SML[:, 48:48 + NT], -0.5, 1.5, ALU.mult, ALU.add))
            VE.raw(lambda e: e.drain())
            VE.do(lambda e: e.tensor_tensor(SML[:, 32:32 + NT], SML[:, 32:32 + NT], SML[:, 48:48 + NT], ALU.mult))
            VE.raw(lambda e: e.drain())
        VE.do(lambda e: e.tensor_tensor(SML[:, 64:64 + NT], AP(SML, 80, [[128, 128], [2, NT]]),
                                        SML[:, 32:32 + NT], ALU.mult))
        VE.raw(lambda e: e.drain())
        VE.do(lambda e: e.tensor_scalar(SML[:, 64:64 + NT], SML[:, 64:64 + NT], -1.0, None, ALU.mult))
        ntok = VE.mark(vV)
        tr_tok = [None, None]
        last_ev = None
        ev_sems = []
        SE.wait(vV, ntok)
        for i in range(NT):
            xt = X[:, 1024 * i:1024 * (i + 1)]
            half = 1024 * (i % 2)
            xo = PT[:, half:half + 1024]
            if tr_tok[i % 2] is not None:
                SE.wait(tT, tr_tok[i % 2])
            atok = SE.do(lambda e, xt=xt, xo=xo, i=i: e.activation(
                out=xo, in_=xt, func=AF.Identity, bias=SML[:, 64 + i:65 + i],
                scale=SML[:, 32 + i:33 + i]), inc=aA)
            pt = P[i % 2]
            TE.wait(aA, atok)
            p_write_ready(TE, pt)
            for c in range(8):
                TE.do(lambda e, xo=xo, c=c, pt=pt: e.matmul(
                    pt.t[:, 128 * c:128 * (c + 1)], xo[:, 128 * c:128 * (c + 1)],
                    IDNB[:, :], start=True, stop=True))
            ttok = TE.mark(tT)
            p_mark_w(pt, ttok)
            tr_tok[i % 2] = ttok
            dst = AP(XLN, 128 * i, [[NCH * T, 128], [T, 8], [1, 128]])
            VE.wait(tT, ttok)
            etok = VE.do(lambda e, pt=pt, dst=dst: e.tensor_copy(dst, pt.t[:, 0:1024]), inc=vV)
            p_mark_r(pt, VE, etok)
            ev_sems.append(etok)
            last_ev = etok
        ln_evs[0] = ev_sems
        return last_ev

    rows_free = [None]
    rb_free = [None]
    stg_free = [0]
    rowA_free = [None]
    rowB_free = [None]

    def emit_bias_bcast(src_row, reader_toks):
        if DBG & 32:
            return None
        s = WSL[6]
        if rows_free[0] is not None:
            GE.wait(tT, rows_free[0])
        tok = s.nxt(16)
        GE.ops.append(lambda e, sr=src_row: e.dma_start(
            out=ROWS[0:1, :], in_=sr).then_inc(s.h, 16))
        pp = P[0]
        TE.wait(WSL[6], tok)
        p_write_ready(TE, pp)
        for f in range(2):
            btok = TE.do(lambda e, f=f: e.matmul(pp.t[:, 512 * f:512 * (f + 1)],
                                                 ONESF[0:1, 0:128],
                                                 ROWS[0:1, 512 * f:512 * (f + 1)],
                                                 start=True, stop=True), inc=tT)
        p_mark_w(pp, btok)
        rows_free[0] = btok
        SE.wait(tT, btok)
        for st, tk in reader_toks:
            SE.wait(st, tk)
        atok = SE.do(lambda e: e.activation(out=BIAS[:, :], in_=pp.t[:, 0:1024], func=AF.Identity), inc=aA)
        p_mark_r(pp, SE, atok)
        return atok

    def emit_pair_proj(wsem, wcol, dst_sl, bias_col, wtok, pps, extra_wait=None,
                       span_evs=None):
        if DBG & 8:
            return None
        TE.wait(wsem, wtok)
        if extra_wait is not None:
            TE.wait(*extra_wait)
        last_atok = None
        tps = SP512 // 128
        for sp in range(NSP512):
            pp = pps[sp % len(pps)]
            half = SP512 * ((sp // len(pps)) % 2) if SP512 == 512 else 0
            if span_evs is not None:
                TE.wait(vV, span_evs[min(tps * (sp + 1) - 1, NT - 1)])
            p_write_ready(TE, pp)
            for ch in range(8):
                lhsT = WA[:, wcol + 128 * ch: wcol + 128 * (ch + 1)]
                rhs = XLN[:, ch * T + SP512 * sp: ch * T + SP512 * (sp + 1)]
                TE.do(lambda e, o=pp.t[:, half:half + SP512], l=lhsT, r=rhs,
                      s=(ch == 0), p2=(ch == 7): e.matmul(o, l, r, start=s, stop=p2))
            mtok = TE.mark(tT)
            p_mark_w(pp, mtok)
            SE.wait(tT, mtok)
            out = dst_sl[:, SP512 * sp:SP512 * (sp + 1)]
            if bias_col is not None:
                atok = SE.do(lambda e, o=out, i=pp.t[:, half:half + SP512],
                             b=QKVB[:, bias_col:bias_col + 1]:
                             e.activation(out=o, in_=i, func=AF.Identity, bias=b), inc=aA)
            else:
                atok = SE.do(lambda e, o=out, i=pp.t[:, half:half + SP512]:
                             e.activation(out=o, in_=i, func=AF.Identity), inc=aA)
            p_mark_r(pp, SE, atok)
            last_atok = atok
        return last_atok

    def emit_v_half(wcol, bias_half, wsem, wtok, bias_tok):
        if DBG & 8:
            return None
        TE.wait(wsem, wtok)
        last = None
        for kt in range(NKT):
            pp = P[kt % 2]
            p_write_ready(TE, pp)
            for ch in range(8):
                lhsT = XLN[:, ch * T + 128 * kt: ch * T + 128 * (kt + 1)]
                rhs = WA[:, wcol + 512 * ch: wcol + 512 * (ch + 1)]
                TE.do(lambda e, o=pp.t[:, 0:512], l=lhsT, r=rhs, s=(ch == 0), p2=(ch == 7):
                      e.matmul(o, l, r, start=s, stop=p2))
            mtok = TE.mark(tT)
            p_mark_w(pp, mtok)
            VE.wait(tT, mtok)
            if bias_tok is not None and kt == 0:
                VE.wait(aA, bias_tok)
            dst = AP(VB, 520 * kt, [[NKT * 520, 128], [130, 4], [65, 2], [1, 64]])
            srcp = AP(pp.t.handle, pp.t.off, [[2048, 128], [128, 4], [64, 2], [1, 64]])
            if bias_half is not None:
                bsrc = AP(BIAS, 512 * bias_half, [[1024, 128], [128, 4], [64, 2], [1, 64]])
                vtok = VE.do(lambda e, d=dst, i=srcp, b=bsrc:
                             e.tensor_tensor(d, i, b, ALU.add), inc=vV)
            else:
                vtok = VE.do(lambda e, d=dst, i=srcp: e.tensor_copy(d, i), inc=vV)
            p_mark_r(pp, VE, vtok)
            last = vtok
        return last

    def emit_attention_pair(c, causal, cshift, qk_tok, v_tok):
        if DBG & 4:
            return v_tok
        qsl = QS[:, c * T:(c + 1) * T]
        koff = ((c % 4) if causal else (c % 2)) * T
        ksl = KS[:, koff:koff + T]
        vbase = 130 * (c % 4)
        S_A, S_B, PV_A, PV_B = P[0], P[1], P[2], P[3]
        TE.wait(vV, v_tok)
        if qk_tok is not None:
            TE.wait(aA, qk_tok)
        last_ev = None
        for sp in range(NSP):
            kts = [kt for kt in range(NKT)
                   if not (causal and 128 * kt > SPAN * sp + SPAN - 1)]
            def _off(kt):
                return max(0, 128 * kt - SPAN * sp) if causal else 0
            last_kt_bank = {}
            for bi in range((SPAN + 511) // 512):
                contrib = [kt for kt in kts if _off(kt) < 512 * bi + 512]
                last_kt_bank[bi] = contrib[-1]
            def emit_pv(hh, kt, ki, segs, ptok, slot):
                pv = PV_A if hh == 0 else PV_B
                TE.wait(*ptok)
                if ki == 0 and hh == 0:
                    p_write_ready(TE, PV_A)
                    p_write_ready(TE, PV_B)
                vsl = VB[:, 520 * kt + vbase + 65 * hh:520 * kt + vbase + 65 * hh + 65]
                st = (ki == 0)
                for (a, b) in segs:
                    prt = PT[:, slot + SPAN * hh + a:slot + SPAN * hh + b]
                    fin = (kt == last_kt_bank[a // 512])
                    TE.do(lambda e, o_=pv.t[0:65, a:b], l=vsl, r=prt, s=st, f=fin:
                          e.matmul(o_, l, r, start=s, stop=f))
                pvtok = TE.mark(tT)
                p_mark_w(pv, pvtok)

            pending = None
            for ki, kt in enumerate(kts):
                o = max(0, 128 * kt - SPAN * sp) if causal else 0
                segs = []
                n0 = o
                while n0 < SPAN:
                    n1 = min((n0 // 512 + 1) * 512, SPAN)
                    segs.append((n0, n1))
                    n0 = n1
                slot = 2 * SPAN * (kt % 2)
                mtoks = [None, None]
                for hh in range(2):
                    pp = S_A if hh == 0 else S_B
                    rowlo = 64 * hh
                    p_write_ready(TE, pp)
                    lhsT = ksl[rowlo:rowlo + 64, 128 * kt:128 * (kt + 1)]
                    for (a, b) in segs:
                        rhs = qsl[rowlo:rowlo + 64, SPAN * sp + a:SPAN * sp + b]
                        TE.do(lambda e, o_=pp.t[0:128, a:b], l=lhsT, r=rhs, tp=(rowlo, 0):
                              e.matmul(o_, l, r, start=True, stop=True, tile_position=tp))
                    mtoks[hh] = TE.mark(tT)
                    p_mark_w(pp, mtoks[hh])
                ptoks = [None, None]
                if False:
                    SE.wait(tT, mtoks[1])
                    etok = SE.do(lambda e, o_=PT[:, slot:slot + 2 * SPAN],
                                 i=PAB[:, 0:2 * SPAN], cs=cshift:
                                 e.activation(out=o_, in_=i, func=AF.Exp,
                                              bias=CSH[:, cs:cs + 1], scale=0.125),
                                 inc=aA)
                    p_mark_r(S_A, SE, etok)
                    p_mark_r(S_B, SE, etok)
                    etoks = [etok, etok]
                else:
                    etoks = [None, None]
                    for hh in range(2):
                        pp = S_A if hh == 0 else S_B
                        SE.wait(tT, mtoks[hh])
                        etoks[hh] = SE.do(lambda e, o_=PT[:, slot + SPAN * hh + o:slot + SPAN * (hh + 1)],
                                          i=pp.t[:, o:SPAN], cs=cshift:
                                          e.activation(out=o_, in_=i, func=AF.Exp,
                                                       bias=CSH[:, cs:cs + 1], scale=0.125),
                                          inc=aA)
                        p_mark_r(pp, SE, etoks[hh])
                for hh in range(2):
                    if causal and 128 * kt >= SPAN * sp:
                        VE.wait(aA, etoks[hh])
                        mtok2 = VE.do(lambda e, o_=PT[:, slot + SPAN * hh + o:slot + SPAN * hh + o + 128]:
                                      e.tensor_tensor(o_, o_, TRI[:, :], ALU.mult), inc=vV)
                        ptoks[hh] = (vV, mtok2)
                    else:
                        ptoks[hh] = (aA, etoks[hh])
                if pending is not None:
                    emit_pv(0, pending[0], pending[1], pending[2], pending[3], pending[5])
                    emit_pv(1, pending[0], pending[1], pending[2], pending[4], pending[5])
                pending = (kt, ki, segs, ptoks[0], ptoks[1], slot)
            emit_pv(0, pending[0], pending[1], pending[2], pending[3], pending[5])
            emit_pv(1, pending[0], pending[1], pending[2], pending[4], pending[5])
            if DBG & 1:
                last_ev = vV.n if vV.n > 0 else None
                continue
            spc0, spc1 = SPAN * sp, SPAN * (sp + 1)
            SE.wait(*PV_A.wtok)
            SE.wait(tT, rowA_free[0])
            SE.do(lambda e: e.activation(out=ROWS[64:65, 0:SPAN], in_=PV_A.t[64:65, 0:SPAN], func=AF.Ln))
            SE.wait(*PV_B.wtok)
            SE.wait(tT, rowB_free[0])
            SE.do(lambda e: e.activation(out=RB[64:65, 0:SPAN], in_=PV_B.t[64:65, 0:SPAN], func=AF.Ln))
            SE.raw(lambda e: e.drain())
            r1A = SE.do(lambda e: e.activation(out=ROWS[64:65, 0:SPAN], in_=ROWS[64:65, 0:SPAN], func=AF.Exp, scale=-1.0), inc=aA)
            p_mark_r(PV_A, SE, r1A)
            r1B = SE.do(lambda e: e.activation(out=RB[64:65, 0:SPAN], in_=RB[64:65, 0:SPAN], func=AF.Exp, scale=-1.0), inc=aA)
            p_mark_r(PV_B, SE, r1B)
            for hh, (pv, bp, r1, srow) in enumerate(((PV_A, S_A, r1A, ROWS), (PV_B, S_B, r1B, RB))):
                TE.wait(aA, r1)
                p_write_ready(TE, bp)
                for (a, b) in ((x, min(x + 512, SPAN)) for x in range(0, SPAN, 512)):
                    btok = TE.do(lambda e, bp=bp, a=a, b=b, srow=srow: e.matmul(
                        bp.t[0:64, a:b], ONESF[64:65, 0:64], srow[64:65, a:b],
                        start=True, stop=True, tile_position=(64, 0)), inc=tT)
                p_mark_w(bp, btok)
                if hh == 0:
                    rowA_free[0] = btok
                else:
                    rowB_free[0] = btok
                SE.wait(tT, btok)
                if hh == 1:
                    SE.wait(vV, e1)
                else:
                    SE.wait(vV, rb_free[0])
                ctok = SE.do(lambda e, bp=bp: e.activation(out=RB[0:64, 0:SPAN], in_=bp.t[0:64, 0:SPAN], func=AF.Identity), inc=aA)
                p_mark_r(bp, SE, ctok)
                VE.wait(aA, ctok)
                VE.wait(*pv.wtok)
                if hh == 0:
                    e1 = VE.do(lambda e, d=qsl[0:64, spc0:spc1], pv=pv:
                               e.tensor_tensor(d, pv.t[0:64, 0:SPAN], RB[0:64, 0:SPAN], ALU.mult), inc=vV)
                    p_mark_r(pv, VE, e1)
                else:
                    VE.wait(dF, stg_free[0])
                    e2 = VE.do(lambda e, pv=pv:
                               e.tensor_tensor(STG[0:64, 0:SPAN], pv.t[0:64, 0:SPAN], RB[0:64, 0:SPAN], ALU.mult), inc=vV)
                    p_mark_r(pv, VE, e2)
                    rb_free[0] = e2
                    GE.wait(vV, e2)
                    dF.n += 16
                    stg_free[0] = dF.n
                    GE.ops.append(lambda e, d=qsl[64:128, spc0:spc1]:
                                  e.dma_start(out=d, in_=STG[0:64, 0:SPAN]).then_inc(dF.h, 16))
            last_ev = e2
        return last_ev

    def emit_out_proj(src_buf, w_dram, wslots, bias_tok, ready_toks, store=False):
        last_vtok = None
        pass_mm_tok = [None, None]
        p1_toks = [None] * NT
        if DBG & 2:
            return (vV.n if vV.n else None), [None] * NT, pass_mm_tok
        for p_ in range(2):
            wsem, wcol, afree = wslots[p_]
            src_ap = AP(w_dram, 512 * p_ * D, [[D, 128], [128 * D, 4], [1, D]])
            wtok = dma_w(wsem, WA[:, wcol:wcol + 4096], src_ap, after_tok=afree)
            TE.wait(wsem, wtok)
            if p_ == 0:
                TE.wait(dF, dF.n)
                for s, t in ready_toks:
                    TE.wait(s, t)
            for i in range(NT):
                pp = P[i % 2] if p_ == 0 else P[2 + i % 2]
                p_write_ready(TE, pp)
                for ch4 in range(4):
                    ch = 4 * p_ + ch4
                    lhsT = src_buf[:, ch * T + 128 * i: ch * T + 128 * (i + 1)]
                    for f in range(2):
                        rhs = WA[:, wcol + 1024 * ch4 + 512 * f: wcol + 1024 * ch4 + 512 * (f + 1)]
                        TE.do(lambda e, o=pp.t[:, 512 * f:512 * (f + 1)], l=lhsT, r=rhs,
                              s=(ch4 == 0), p2=(ch4 == 3): e.matmul(o, l, r, start=s, stop=p2))
                mtok = TE.mark(tT)
                p_mark_w(pp, mtok)
                pass_mm_tok[p_] = mtok
                VE.wait(tT, mtok)
                xsl = X[:, 1024 * i:1024 * (i + 1)]
                vt = VE.do(lambda e, x=xsl, i_=pp.t[:, 0:1024]:
                           e.tensor_tensor(x, x, i_, ALU.add), inc=vV)
                p_mark_r(pp, VE, vt)
                last_vtok = vt
                if p_ == 1:
                    p1_toks[i] = vt
            if p_ == 0 and bias_tok is not None:
                VE.wait(aA, bias_tok)
                VE.raw(lambda e: e.drain())
                for i in range(NT):
                    xsl = X[:, 1024 * i:1024 * (i + 1)]
                    VE.do(lambda e, x=xsl: e.tensor_tensor(x, x, BIAS[:, :], ALU.add))
        tile_toks = p1_toks
        if store:
            for i in range(NT):
                SY.wait(vV, tile_toks[i])
                dO.n += 16
                SY.ops.append(lambda e, i=i: e.dma_start(
                    out=AP(d_y, 128 * i * D, [[D, 128], [1, D]]),
                    in_=X[:, 1024 * i:1024 * (i + 1)]).then_inc(dO.h, 16))
        return last_vtok, tile_toks, pass_mm_tok

    GE.wait(dC, tok_consts)
    TE.wait(dC, tok_consts)
    VE.wait(dC, tok_consts)
    SE.wait(dC, tok_consts)
    ln_tok = emit_ln()
    vb_tok = emit_bias_bcast(d_vbrow.ap()[:, :], [])

    pair_free = [None, None]
    v_free = None
    v_tok = None
    last_attn = None
    for c in range(NPAIR):
        wsem = WSL[c % 2]
        wcol = 2048 * (c % 2)
        qtok = dma_w(wsem, WA[:, wcol:wcol + 1024],
                     AP(d_wqkvT, 128 * c, [[3 * D, 128], [3 * D * 128, 8], [1, 128]]),
                     after_tok=pair_free[c % 2])
        ktok = dma_w(wsem, WA[:, wcol + 1024:wcol + 2048],
                     AP(d_wqkvT, D + 128 * c, [[3 * D, 128], [3 * D * 128, 8], [1, 128]]))
        emit_pair_proj(wsem, wcol, QS[:, c * T:(c + 1) * T], c, ktok, [P[2], P[3]],
                       span_evs=ln_evs[0] if c == 0 else None)
        ak = emit_pair_proj(wsem, wcol + 1024, KS[:, (c % 2) * T:(c % 2) * T + T],
                            8 + c, ktok, [P[2], P[3]])
        pair_free[c % 2] = tT.n
        if c in (0, 4):
            vwtok = dma_w(WSL[2], WA[:, 4096:8192],
                          AP(d_wqkvT, 2 * D + 512 * (c // 4),
                             [[3 * D, 128], [3 * D * 128, 8], [1, 512]]),
                          after_tok=v_free)
            v_tok = emit_v_half(4096, c // 4, WSL[2], vwtok, vb_tok if c == 0 else None)
            v_free = tT.n
        last_attn = emit_attention_pair(c, False, 0, ak, v_tok)

    ob_tok = emit_bias_bcast(d_obrow.ap()[:, :], [(vV, v_tok)])
    wo_slots = [(WSL[4], 0, tT.n), (WSL[5], 4096, tT.n)]
    wo_tok, wo_tts, wo_pmt = emit_out_proj(QS, d_woT, wo_slots, ob_tok, [(vV, last_attn)])

    def emit_ffn(prev_tok, bias_reader_tok, store, tile_toks=None, prev_pmt=None):
        ln_t = emit_ln(prev_x_tok=prev_tok, tile_toks=tile_toks)
        wa_free = tT.n
        wa_p0 = prev_pmt[0] if prev_pmt and prev_pmt[0] else wa_free
        wa_p1 = prev_pmt[1] if prev_pmt and prev_pmt[1] else wa_free
        rtok = dma_w(WSL[2], WA[:, 2048:2048 + 2040],
                     AP(d_routT, 0, [[255, 128], [255 * 128, 8], [1, 255]]),
                     after_tok=wa_p0)
        lvtok = dma_w(WSL[3], WA[:, 4096:6144],
                      AP(d_lv, 0, [[D, 128], [D * 128, 2], [1, D]]),
                      after_tok=wa_p1)
        fb_tok = emit_bias_bcast(d_fbrow.ap()[:, :], [(vV, bias_reader_tok)])
        qf_tok = None
        slot_free = [wa_p0, wa_p0]
        for et in range(8):
            wsem = WSL[et % 2]
            wcol = 1024 * (et % 2)
            wtok = dma_w(wsem, WA[:, wcol:wcol + 1024],
                         AP(d_fwqT, 128 * et, [[D, 128], [D * 128, 8], [1, 128]]),
                         after_tok=slot_free[et % 2])
            qf_tok = emit_pair_proj(wsem, wcol, QS[:, et * T:(et + 1) * T], None,
                                    wtok, [P[0], P[1]],
                                    span_evs=ln_evs[0] if et == 0 else None)
            slot_free[et % 2] = tT.n
        TE.wait(WSL[2], rtok)
        TE.wait(WSL[3], lvtok)

        NSPW = T // SP512
        TPS = SP512 // 128
        lp_read = {}
        tr_toks = {}
        sig_tok = {}
        ret_done = {}

        def stage_a(i):
            pz = P[2 + i % 2]
            p_write_ready(TE, pz)
            if i == 0:
                TE.wait(aA, qf_tok)
            for ch in range(8):
                lhsT = QS[:, ch * T + 128 * i: ch * T + 128 * (i + 1)]
                rhs = WA[:, 2048 + 255 * ch: 2048 + 255 * (ch + 1)]
                TE.do(lambda e, o=pz.t[:, 0:255], l=lhsT, r=rhs, s_=(ch == 0), p2=(ch == 7):
                      e.matmul(o, l, r, start=s_, stop=p2))
            ztok = TE.mark(tT)
            p_mark_w(pz, ztok)
            g2o = 512 * (i % 2)
            SE.wait(tT, ztok)
            if i >= 2 and (i - 2) in lp_read:
                SE.wait(vV, lp_read[i - 2])
            SE.do(lambda e, o=SHRD[:, g2o:g2o + 255], i_=pz.t[:, 0:255]:
                  e.activation(out=o, in_=i_, func=AF.Sigmoid, scale=-1.0))
            gtok = SE.do(lambda e, o=SHRD[:, g2o + 255:g2o + 510], i_=pz.t[:, 0:255]:
                         e.activation(out=o, in_=i_, func=AF.Sigmoid), inc=aA)
            p_mark_r(pz, SE, gtok)
            sig_tok[i] = gtok

        def stage_b(i):
            g2o = 512 * (i % 2)
            EN = GE if i % 2 == 0 else VE
            EN.wait(aA, sig_tok[i])
            if i >= 2 and (i - 2) in tr_toks:
                EN.wait(tT, tr_toks[i - 2])
            lpA = 1024 + 512 * (i % 2)
            lpB = lpA + 256
            ptf = 256 * (i % 2)
            EN.do(lambda e, g2o=g2o, lpA=lpA: e.tensor_copy(
                AP(SHRD, lpA, [[2048, 128], [1, 2]]),
                AP(SHRD, g2o, [[2048, 128], [255, 2]])))
            if EN is VE:
                EN.raw(lambda e: e.drain())
            cur, oth = lpA, lpB
            for lv_ in range(1, 8):
                n = 1 << lv_
                last = lv_ == 7
                o_ev = AP(PT, ptf, [[PTW, 128], [2, n]]) if last else AP(SHRD, oth, [[2048, 128], [2, n]])
                o_od = AP(PT, ptf + 1, [[PTW, 128], [2, n]]) if last else AP(SHRD, oth + 1, [[2048, 128], [2, n]])
                EN.do(lambda e, cur=cur, o_=o_ev, n=n, g2o=g2o: e.tensor_tensor(
                    o_, AP(SHRD, cur, [[2048, 128], [1, n]]),
                    AP(SHRD, g2o + n - 1, [[2048, 128], [1, n]]), ALU.mult))
                EN.do(lambda e, cur=cur, o_=o_od, n=n, g2o=g2o: e.tensor_tensor(
                    o_, AP(SHRD, cur, [[2048, 128], [1, n]]),
                    AP(SHRD, g2o + 255 + n - 1, [[2048, 128], [1, n]]), ALU.mult))
                if EN is VE:
                    EN.raw(lambda e: e.drain())
                cur, oth = oth, cur
            lptok = EN.mark(gG if EN is GE else vV)
            lpsem = gG if EN is GE else vV
            pt_ = P[(i + 1) % 2]
            TE.wait(lpsem, lptok)
            p_write_ready(TE, pt_)
            for b in range(2):
                TE.do(lambda e, b=b, pt_=pt_, ptf=ptf: e.matmul(
                    pt_.t[:, 256 + 128 * b:256 + 128 * (b + 1)],
                    PT[:, ptf + 128 * b:ptf + 128 * (b + 1)], IDNB[:, :], start=True, stop=True))
            ttk = TE.mark(tT)
            p_mark_w(pt_, ttk)
            tr_toks[i] = ttk
            VE.wait(tT, ttk)
            dst = AP(KS, 128 * i, [[4 * T, 128], [T, 2], [1, 128]])
            lp_read[i] = VE.do(lambda e, dst=dst, pt_=pt_: e.tensor_copy(dst, pt_.t[:, 256:512]), inc=vV)
            p_mark_r(pt_, VE, lp_read[i])

        def emit_ret(s):
            TE.wait(vV, lp_read[TPS * (s + 1) - 1])
            for dt_ in range(8):
                pp = P[dt_ % 2]
                p_write_ready(TE, pp)
                for m in range(2):
                    lhsT = WA[:, 4096 + 1024 * m + 128 * dt_: 4096 + 1024 * m + 128 * (dt_ + 1)]
                    rhs = KS[:, m * T + SP512 * s: m * T + SP512 * (s + 1)]
                    TE.do(lambda e, o=pp.t[:, 0:SP512], l=lhsT, r=rhs, s_=(m == 0), p2=(m == 1):
                          e.matmul(o, l, r, start=s_, stop=p2))
                mtok = TE.mark(tT)
                p_mark_w(pp, mtok)
                SE.wait(tT, mtok)
                atok = SE.do(lambda e, o=XLN[:, dt_ * T + SP512 * s: dt_ * T + SP512 * (s + 1)],
                             i_=pp.t[:, 0:SP512]: e.activation(out=o, in_=i_, func=AF.Identity),
                             inc=aA)
                p_mark_r(pp, SE, atok)
                ret_done[s] = atok

        z_done = 0
        for s in range(NSPW):
            for i in range(TPS * s, TPS * (s + 1)):
                stage_a(i)
                if i == NT - 1:
                    z_done = tT.n
                if i >= 1:
                    stage_b(i - 1)
            if s >= 1:
                emit_ret(s - 1)
        stage_b(TPS * NSPW - 1)
        emit_ret(NSPW - 1)
        wslots = [(WSL[4], 0, z_done), (WSL[5], 4096, tT.n)]
        lv_, tts_, pmt_ = emit_out_proj(XLN, d_fwoutT, wslots, fb_tok, [(aA, ret_done[NSPW - 1])], store=store)
        return lv_, tts_, pmt_

    ffn1_res = emit_ffn(wo_tok, wo_tok, store=False, tile_toks=wo_tts, prev_pmt=wo_pmt) if n_sub >= 2 else None
    ffn1_tok, ffn1_tts, ffn1_pmt = ffn1_res if ffn1_res else (None, None, [None, None])

    def emit_attn_b(prev_tok, tile_toks=None, prev_pmt=None):
        ln_t = emit_ln(prev_x_tok=prev_tok, tile_toks=tile_toks)
        wa_free = tT.n
        wa_p0 = prev_pmt[0] if prev_pmt and prev_pmt[0] else wa_free
        wa_p1 = prev_pmt[1] if prev_pmt and prev_pmt[1] else wa_free
        cs_sem = WSL[3]
        pre_q = dma_w(WSL[0], WA[:, 0:4096],
                      AP(d_bwqT, 0, [[D, 128], [D * 128, 8], [1, 512]]),
                      after_tok=wa_p0)
        pre_k = dma_w(WSL[0], WA[:, 4096:8192],
                      AP(d_bwkT, 0, [[D, 128], [D * 128, 8], [1, 512]]),
                      after_tok=wa_p1)
        GE.wait(vV, ln_t)
        ctok = cs_sem.nxt(16)
        GE.ops.append(lambda e: e.dma_start(out=SHRD[:, 0:32 * NT],
                                            in_=d_cos.ap()[:, :]).then_inc(cs_sem.h, 16))
        stok = cs_sem.nxt(16)
        GE.ops.append(lambda e: e.dma_start(out=SHRD[:, 512:512 + 32 * NT],
                                            in_=d_sin.ap()[:, :]).then_inc(cs_sem.h, 16))
        last_attn = None
        quad_free = wa_free
        v_free2 = wa_p0
        rt_by = {}
        tt_by = {}
        VBP = NKT * 520
        ro_buf = [(PT, 0, PTW), (PT, 1024, PTW), (VB, 0, VBP)]
        pre_k1 = None
        for quad in range(2):
            wsem = WSL[0] if quad == 0 else WSL[1]
            kwtok = pre_k if quad == 0 else pre_k1
            TE.wait(wsem, kwtok)
            VE.wait(cs_sem, stok)
            pair_tok = None

            def flush_pend(i, g, quad=None):
                pt_ = P[i % 2]
                buf, base, _ = ro_buf[g % 3]
                TE.wait(vV, rt_by[g])
                p_write_ready(TE, pt_)
                for b in range(8):
                    TE.do(lambda e, b=b, pt_=pt_, buf=buf, base=base: e.matmul(
                        pt_.t[:, 128 * b:128 * (b + 1)], buf[:, base + 128 * b:base + 128 * (b + 1)],
                        IDNB[:, :], start=True, stop=True))
                ttok = TE.mark(tT)
                p_mark_w(pt_, ttok)
                tt_by[g] = ttok
                qdst = AP(QS, (4 * quad) * T + 128 * i, [[NPAIR * T, 128], [T, 4], [1, 128]])
                kdst = AP(KS, 128 * i, [[4 * T, 128], [T, 4], [1, 128]])
                SE.wait(tT, ttok)
                SE.do(lambda e, d=qdst, pt_=pt_: e.activation(out=d, in_=pt_.t[:, 0:512], func=AF.Identity))
                ptok = SE.do(lambda e, d=kdst, pt_=pt_: e.activation(out=d, in_=pt_.t[:, 512:1024], func=AF.Identity), inc=aA)
                p_mark_r(pt_, SE, ptok)
                return ptok

            for i in range(NT):
                g = quad * NT + i
                pq = P[2 + i % 2]
                if quad == 0:
                    TE.wait(vV, ln_evs[0][i])
                p_write_ready(TE, pq)
                for ch in range(8):
                    lhsT = XLN[:, ch * T + 128 * i: ch * T + 128 * (i + 1)]
                    TE.do(lambda e, o=pq.t[:, 0:512], l=lhsT, r=WA[:, 512 * ch:512 * (ch + 1)],
                          s=(ch == 0), p2=(ch == 7): e.matmul(o, l, r, start=s, stop=p2))
                for ch in range(8):
                    lhsT = XLN[:, ch * T + 128 * i: ch * T + 128 * (i + 1)]
                    TE.do(lambda e, o=pq.t[:, 512:1024], l=lhsT, r=WA[:, 4096 + 512 * ch:4096 + 512 * (ch + 1)],
                          s=(ch == 0), p2=(ch == 7): e.matmul(o, l, r, start=s, stop=p2))
                mtok = TE.mark(tT)
                p_mark_w(pq, mtok)
                EV = 2048 + 1024 * (i % 2)
                SE.wait(tT, mtok)
                if g - 2 in rt_by:
                    SE.wait(vV, rt_by[g - 2])
                pqh = getattr(pq.t, "handle", pq.t)
                pqo = getattr(pq.t, "off", 0)
                ppitch = 2048 if pqh is PAB else 1024
                inev = AP(pqh, pqo, [[ppitch, 128], [64, 16], [2, 32]])
                inod = AP(pqh, pqo + 1, [[ppitch, 128], [64, 16], [2, 32]])
                evev = AP(PT, EV, [[PTW, 128], [64, 16], [1, 32]])
                evod = AP(PT, EV + 32, [[PTW, 128], [64, 16], [1, 32]])
                SE.do(lambda e, o_=evev, i_=inev: e.activation(out=o_, in_=i_, func=AF.Identity))
                a2 = SE.do(lambda e, o_=evod, i_=inod: e.activation(out=o_, in_=i_, func=AF.Identity), inc=aA)
                p_mark_r(pq, SE, a2)
                co = 32 * i
                rbuf, rbase, rpitch = ro_buf[g % 3]
                sin_r = AP(SHRD, 512 + co, [[2048, 128], [0, 16], [1, 32]])
                cos_r = AP(SHRD, co, [[2048, 128], [0, 16], [1, 32]])
                t1 = ROWS[:, 0:512]
                t2 = ROWS[:, 512:1024]
                bev = AP(rbuf, rbase, [[rpitch, 128], [64, 16], [1, 32]])
                bod = AP(rbuf, rbase + 32, [[rpitch, 128], [64, 16], [1, 32]])
                VE.wait(aA, a2)
                if g - 3 in tt_by:
                    VE.wait(tT, tt_by[g - 3])
                VE.do(lambda e, a=t1, b=evev, s=sin_r: e.tensor_tensor(a, b, s, ALU.mult))
                VE.do(lambda e, a=t2, b=evod, s=sin_r: e.tensor_tensor(a, b, s, ALU.mult))
                VE.raw(lambda e: e.drain())
                VE.do(lambda e, a=evev, c_=cos_r: e.tensor_tensor(a, a, c_, ALU.mult))
                VE.do(lambda e, a=evod, c_=cos_r: e.tensor_tensor(a, a, c_, ALU.mult))
                VE.raw(lambda e: e.drain())
                VE.do(lambda e, o_=bev, a=evev, b=t2: e.tensor_tensor(o_, a, b, ALU.subtract))
                VE.do(lambda e, o_=bod, a=evod, b=t1: e.tensor_tensor(o_, a, b, ALU.add))
                rtok = VE.mark(vV)
                rt_by[g] = rtok
                if i >= 2:
                    pair_tok = flush_pend(i - 2, g - 2, quad=quad)
            pair_tok = flush_pend(NT - 2, quad * NT + NT - 2, quad=quad)
            pair_tok = flush_pend(NT - 1, quad * NT + NT - 1, quad=quad)
            quad_free = tT.n
            vwtok = dma_w(WSL[2], WA[:, 0:4096],
                          AP(d_bwvT, 512 * quad, [[D, 128], [D * 128, 8], [1, 512]]),
                          after_tok=quad_free)
            VE.wait(tT, tt_by[quad * NT + NT - 1])
            for kt_ in (0, 1):
                VE.do(lambda e, kt=kt_: e.memset(
                    AP(VB, 520 * kt + 64, [[VBP, 128], [130, 4], [65, 2]]), 1.0))
            v_tok2 = emit_v_half(0, None, WSL[2], vwtok, None)
            v_free2 = tT.n
            if quad == 0:
                dma_w(WSL[1], WA[:, 0:4096],
                      AP(d_bwqT, 512, [[D, 128], [D * 128, 8], [1, 512]]),
                      after_tok=v_free2)
                pre_k1 = dma_w(WSL[1], WA[:, 4096:8192],
                               AP(d_bwkT, 512, [[D, 128], [D * 128, 8], [1, 512]]))
            for cl in range(4):
                c = 4 * quad + cl
                last_attn = emit_attention_pair(c, True, 1, pair_tok if cl == 0 else None, v_tok2)
        wslots = [(WSL[4], 0, tT.n), (WSL[5], 4096, None)]
        lvb_, ttsb_, pmtb_ = emit_out_proj(QS, d_bwoT, wslots, None, [(vV, last_attn)])
        return lvb_, ttsb_, pmtb_

    def emit_store(last_tok):
        SY.wait(vV, last_tok)
        if DBG & 512:
            GE.wait(aA, aA.n)
            GE.wait(vV, vV.n)
            dO.n += 16
            GE.ops.append(lambda e: e.dma_start(
                out=AP(d_y, 0, [[128, 128], [1, 128]]),
                in_=SML[:, :]).then_inc(dO.h, 16))
            return
        if DBG & 256:
            GE.wait(aA, aA.n)
            GE.wait(vV, vV.n)
            GE.wait(tT, tT.n)
            dO.n += 16
            GE.ops.append(lambda e, PTW=PTW: e.dma_start(
                out=AP(d_y, 0, [[D, 128], [128 * D, PTW // 1024], [1, D]]),
                in_=AP(PT, 0, [[PTW, 128], [1024, PTW // 1024], [1, 1024]])).then_inc(dO.h, 16))
            return
        if DBG & 128:
            GE.wait(aA, aA.n)
            GE.wait(vV, vV.n)
            dO.n += 16
            GE.ops.append(lambda e: e.dma_start(
                out=AP(d_y, 0, [[D, 128], [128 * D, (NCH * T) // 1024], [1, D]]),
                in_=AP(XLN, 0, [[NCH * T, 128], [1024, (NCH * T) // 1024], [1, 1024]])).then_inc(dO.h, 16))
            return
        if DBG & 64:
            GE.wait(aA, aA.n)
            GE.wait(vV, vV.n)
            GE.wait(dF, dF.n)
            dO.n += 16
            GE.ops.append(lambda e: e.dma_start(
                out=AP(d_y, 0, [[D, 128], [128 * D, NPAIR * T // 1024, ], [1, D]]) if False else AP(d_y, 0, [[D, 128], [128 * D, (NPAIR * T) // 1024], [1, D]]),
                in_=AP(QS, 0, [[NPAIR * T, 128], [1024, (NPAIR * T) // 1024], [1, 1024]])).then_inc(dO.h, 16))
            return
        for i in range(NT):
            dO.n += 16
            SY.ops.append(lambda e, i=i: e.dma_start(
                out=AP(d_y, 128 * i * D, [[D, 128], [1, D]]),
                in_=X[:, 1024 * i:1024 * (i + 1)]).then_inc(dO.h, 16))

    if n_sub >= 3:
        attb_tok, attb_tts, attb_pmt = emit_attn_b(ffn1_tok, tile_toks=ffn1_tts, prev_pmt=ffn1_pmt)
    if n_sub >= 4:
        ffn2_tok = emit_ffn(attb_tok, attb_tok, store=True, tile_toks=attb_tts, prev_pmt=attb_pmt)[0]
    elif n_sub == 3:
        emit_store(attb_tok)
    elif n_sub == 2:
        emit_store(ffn1_tok)
    else:
        emit_store(wo_tok)

    SY.raw(lambda e, n=dO.n: e.wait_ge(dO.h, n))

    import os as _os
    if _os.environ.get("KCOUNT"):
        print("opcounts:", {nm: len(EG.ops) for nm, EG in (("t", TE), ("v", VE), ("s", SE), ("g", GE), ("y", SY))})
    _tr = _os.environ.get("KBISECT", "")
    if _tr:
        eng_sel, frac = _tr.split(":")
        frac = float(frac)
        for nm, EG in (("t", TE), ("v", VE), ("s", SE), ("g", GE), ("y", SY)):
            if eng_sel != "all" and nm not in eng_sel:
                EG.ops = []
            else:
                EG.ops = EG.ops[:int(len(EG.ops) * frac)]
    with es:
        block = es.enter_context(nc.Block())

        @block.gpsimd
        def _(g):
            for f in GE.ops:
                f(g)

        @block.vector
        def _(v):
            for f in VE.ops:
                f(v)

        @block.scalar
        def _(s):
            for f in SE.ops:
                f(s)

        @block.tensor
        def _(t):
            for f in TE.ops:
                f(t)

        @block.sync
        def _(s):
            for f in SY.ops:
                f(s)

    return nc


def _prep_consts(T):
    NT = T // 128
    ident = np.eye(128, dtype=np.float32)
    tri = (np.arange(128)[None, :] >= np.arange(128)[:, None]).astype(np.float32)
    pos = np.arange(T, dtype=np.float32)
    pair = np.arange(32, dtype=np.float32)
    rates = (np.float32(10000.0) ** (-(2.0 * pair) / np.float32(64.0))).astype(np.float32)
    ang = (pos[:, None] * rates[None, :]).astype(np.float32)
    cos = np.cos(ang).astype(np.float32)
    sin = np.sin(ang).astype(np.float32)
    cos_t = np.zeros((128, 32 * NT), dtype=np.float32)
    sin_t = np.zeros((128, 32 * NT), dtype=np.float32)
    for i in range(NT):
        cos_t[:, 32 * i:32 * (i + 1)] = cos[128 * i:128 * (i + 1)]
        sin_t[:, 32 * i:32 * (i + 1)] = sin[128 * i:128 * (i + 1)]
    ones = np.ones((128, 1), dtype=np.float32)
    onesf = np.ones((128, 128), dtype=np.float32)
    csh = np.zeros((128, 2), dtype=np.float32)
    csh[:, 0] = -C_SHIFT_A
    csh[:, 1] = -C_SHIFT_B
    return ident, tri, cos_t, sin_t, ones, csh, onesf


def make_in_map(xc, w, T):
    ident, tri, cos_t, sin_t, ones, csh, onesf = _prep_consts(T)
    asc = lambda a: np.ascontiguousarray(a, dtype=np.float32)
    return {
        "x_in": asc(xc),
        "wqkvT": asc(w["L0_attn_a_qkv_w"].T),
        "woT": asc(w["L0_attn_a_out_w"].T),
        "qkvb_t": asc(w["L0_attn_a_qkv_b"].reshape(24, 128).T),
        "vb_row": asc(w["L0_attn_a_qkv_b"][2048:3072].reshape(1, 1024)),
        "ob_row": asc(w["L0_attn_a_out_b"].reshape(1, 1024)),
        "bwqT": asc(w["L0_attn_b_Wq"].T),
        "bwkT": asc(w["L0_attn_b_Wk"].T),
        "bwvT": asc(w["L0_attn_b_Wv"].T),
        "bwoT": asc(w["L0_attn_b_Wo"].T),
        "fwqT": asc(w["L0_ffn_Wq"].T),
        "routT": asc(w["L0_ffn_routers"].T),
        "lv_w": asc(w["L0_ffn_leaf_values"]),
        "fwoutT": asc(w["L0_ffn_Wout_w"].T),
        "fb_row": asc(w["L0_ffn_Wout_b"].reshape(1, 1024)),
        "ident_c": ident, "tri_c": tri, "cos_c": cos_t, "sin_c": sin_t,
        "ones_c": ones, "csh_c": csh, "onesf_c": onesf, "onesb_c": onesf[:, 0:64].copy(),
    }


def kernel(**inputs):
    x = np.asarray(inputs["L0_attn_a_x"], dtype=np.float32)
    B, T, _ = x.shape
    w = {k: np.asarray(v, dtype=np.float32) for k, v in inputs.items()
         if k != "L0_attn_a_x"}
    nc = build_nc(T)
    in_maps = [make_in_map(x[i], w, T) for i in range(B)]
    res = run_bass_kernel_spmd(nc, in_maps, core_ids=list(range(B)))
    out = np.stack([np.asarray(res.results[i]["y_out"]) for i in range(B)], axis=0)
    return out.astype(np.float32)
